# version 05
# baseline (speedup 1.0000x reference)
import numpy as np
import ml_dtypes
from contextlib import ExitStack

import concourse.bacc as bacc
import concourse.mybir as mybir
from concourse.bass import IndirectOffsetOnAxis
from concourse.bass_utils import run_bass_kernel_spmd
from concourse import library_config as LC

F32 = mybir.dt.float32
BF16 = mybir.dt.bfloat16
I16 = mybir.dt.int16
I32 = mybir.dt.int32
U32 = mybir.dt.uint32
ALU = mybir.AluOpType
ACT = mybir.ActivationFunctionType
AXX = mybir.AxisListType

NB = 8
T = 4096
NT = NB * T
C = 1024
NCT = 8
HID = 2048
NHT = 16
CAP = 4096
EPS = 1e-5
VAL_ITERS = 45
TIE_ITERS = 16
WPAD = 304
CH = 512
NCH = T // CH


class Rec:
    def __init__(self, name, sem, dsem):
        self.name = name
        self.sem = sem
        self.dsem = dsem
        self.q = []
        self.n = 0
        self.nd = 0
        self.last_out = set()
        self.recent = []
        self.waited = {}
        self.ev = []

    def wait(self, mark):
        if mark is None:
            return
        sem, count = mark
        if count <= 0:
            return
        key = id(sem)
        if self.waited.get(key, -1) >= count:
            return
        self.waited[key] = count
        self.ev.append(("w", id(sem), count, len(self.q)))
        self.q.append(lambda e: e.wait_ge(sem, count))

    def mark(self):
        return (self.sem, self.n)

    def dmark(self):
        return (self.dsem, self.nd)

    @staticmethod
    def _names(aps):
        out = set()
        for ap in aps:
            if ap is None or isinstance(ap, (int, float)):
                continue
            t = getattr(ap, "tensor", None)
            if t is not None:
                out.add(t.name)
        return out

    def _hz(self, ins, outs):
        if self.name not in ("vector", "scalar", "gpsimd"):
            return
        rd = self._names(ins)
        wr = self._names(outs)
        rws = set()
        rds = set()
        for (r, w) in self.recent:
            rws |= w
            rds |= r
        if (rd & rws) or (wr & rws) or (wr & rds):
            if self.name == "vector":
                self.q.append(lambda e: e.drain())
            else:
                sem, n = self.sem, self.n
                self.ev.append(("w", id(sem), n, len(self.q)))
                self.q.append(lambda e: e.wait_ge(sem, n))
            self.recent = []
        self.recent = (self.recent + [(rd, wr)])[-3:]

    def drain(self):
        self.q.append(lambda e: e.drain())
        self.recent = []

    def _inc(self, call):
        sem = self.sem
        self.ev.append(("i", id(sem), 1, len(self.q)))
        self.q.append(lambda e: call(e).then_inc(sem, 1))
        self.n += 1

    def ts(self, out, in0, s1, s2=None, op0=ALU.mult, op1=None, accum=None):
        self._hz([in0, s1, s2], [out, accum])
        kw = dict(out=out, in0=in0, scalar1=s1, scalar2=s2, op0=op0)
        if op1 is not None:
            kw["op1"] = op1
        if accum is not None:
            kw["accum_out"] = accum
        self._inc(lambda e: e.tensor_scalar(**kw))

    def tt(self, out, in0, in1, op):
        self._hz([in0, in1], [out])
        self._inc(lambda e: e.tensor_tensor(out=out, in0=in0, in1=in1, op=op))

    def stt(self, out, in0, s, in1, op0, op1, accum=None):
        self._hz([in0, in1, s], [out, accum])
        kw = dict(out=out, in0=in0, scalar=s, in1=in1, op0=op0, op1=op1)
        if accum is not None:
            kw["accum_out"] = accum
        self._inc(lambda e: e.scalar_tensor_tensor(**kw))

    def copy(self, out, in_):
        self._hz([in_], [out])
        self._inc(lambda e: e.tensor_copy(out=out, in_=in_))

    def memset(self, ap, val):
        self._hz([], [ap])
        self._inc(lambda e: e.memset(ap, val))

    def recip(self, out, in_):
        self._hz([in_], [out])
        self._inc(lambda e: e.reciprocal(out=out, in_=in_))

    def reduce(self, out, in_, op, axis=AXX.X):
        self._hz([in_], [out])
        self._inc(lambda e: e.tensor_reduce(out=out, in_=in_, axis=axis, op=op))

    def scan(self, out, d0, d1, initial):
        self._hz([d0, d1, initial], [out])
        self._inc(lambda e: e.tensor_tensor_scan(
            out=out, data0=d0, data1=d1, initial=initial,
            op0=ALU.mult, op1=ALU.add))

    def act(self, out, in_, func, bias=None, scale=None, accum=None):
        self._hz([in_, bias, scale], [out, accum])
        kw = dict(out=out, in_=in_, func=func)
        if bias is not None:
            kw["bias"] = bias
        if scale is not None:
            kw["scale"] = scale
        if accum is not None:
            kw["accum_out"] = accum
        self._inc(lambda e: e.activation(**kw))

    def nop(self):
        self.q.append(lambda e: e.nop())

    def gdiv(self, out, in0, in1):
        self._inc(lambda e: e.tensor_tensor(out=out, in0=in0, in1=in1,
                                            op=ALU.divide))

    def mm(self, out, lhsT, rhs, start, stop):
        ent = {"k": "mm", "out": out, "lhsT": lhsT, "rhs": rhs,
               "start": start, "stop": stop, "inc": False, "sem": self.sem}
        self.q.append(ent)

    def trans(self, out, in_, ident):
        ent = {"k": "tr", "out": out, "in_": in_, "ident": ident,
               "inc": False, "sem": self.sem}
        self.q.append(ent)

    def minc(self):
        assert isinstance(self.q[-1], dict)
        self.q[-1]["inc"] = True
        self.ev.append(("i", id(self.sem), 1, len(self.q) - 1))
        self.n += 1

    def dma(self, out, in_, **kw):
        dsem = self.dsem
        self.wait(self.dmark())
        self.ev.append(("i", id(dsem), 16, len(self.q)))
        self.q.append(lambda e: e.dma_start(out=out, in_=in_, **kw)
                      .then_inc(dsem, 16))
        self.nd += 16
        if self.name in ("vector", "scalar", "gpsimd"):
            self.recent = (self.recent + [(set(), self._names([out]))])[-3:]
        return self.dmark()

    def sg(self, out, in_, nf):
        self._hz([in_], [out, nf])
        self._inc(lambda e: e.sparse_gather(out=out, in_=in_, num_found=nf))

    def lib(self, lib):
        self.q.append(lambda e: e.load_library(lib))

    def pbcast(self, out, in_, channels):
        self._hz([in_], [out])
        self._inc(lambda e: e.partition_broadcast(out_ap=out, in_ap=in_,
                                                  channels=channels))

    def gather(self, out, tab, idxs, n, elem, transpose=False, count_src=None):
        dsem = self.dsem
        self.wait(self.dmark())
        self.ev.append(("i", id(dsem), 16, len(self.q)))

        def emit(e):
            nreg = n if count_src is None else e.value_load(
                count_src, min_val=0, max_val=n)
            e.dma_gather(
                out_ap=out, in_ap=tab, idxs_ap=idxs, num_idxs=n,
                num_idxs_reg=nreg, elem_size=elem, transpose=transpose,
                single_packet=False).then_inc(dsem, 16)
        self.q.append(emit)
        self.nd += 16
        return self.dmark()

    def scatter_add(self, out, in_, idxs, n, elem, count_src=None):
        dsem = self.dsem
        self.wait(self.dmark())
        self.ev.append(("i", id(dsem), 16, len(self.q)))

        def emit(e):
            nreg = n if count_src is None else e.value_load(
                count_src, min_val=0, max_val=n)
            e.dma_scatter_add(
                out_ap=out, in_ap=in_, idxs_ap=idxs, num_idxs=n,
                num_idxs_reg=nreg, elem_size=elem,
                single_packet=False).then_inc(dsem, 16)
        self.q.append(emit)
        self.nd += 16
        return self.dmark()

    def imerge(self, out_ap, idx_ap, in_ap):
        dsem = self.dsem
        self.wait(self.dmark())
        self.ev.append(("i", id(dsem), 16, len(self.q)))
        self.q.append(lambda e: e.indirect_dma_start(
            out=out_ap, out_offset=IndirectOffsetOnAxis(ap=idx_ap, axis=1),
            in_=in_ap, in_offset=None,
            compute_op=ALU.add).then_inc(dsem, 16))
        self.nd += 16
        return self.dmark()

    def iota(self, out, pattern, base, cm):
        self._hz([], [out])
        self._inc(lambda e: e.iota(out=out, pattern=pattern, base=base,
                                   channel_multiplier=cm,
                                   allow_small_or_imprecise_dtypes=True))

    def cc(self, kind, op, ins, outs, ccsem, state):
        state[0] += 1
        count = state[0]
        self.ev.append(("i", id(ccsem), 1, len(self.q)))
        self.q.append(lambda e: e.collective_compute(
            kind, op, replica_groups=[list(range(NB))], ins=ins, outs=outs)
            .then_inc(ccsem, 1))
        return (ccsem, count)

    def replay(self, raw):
        for fn in self.q:
            if isinstance(fn, dict):
                if fn["k"] == "mm":
                    r = raw.matmul(out=fn["out"], lhsT=fn["lhsT"],
                                   rhs=fn["rhs"], start=fn["start"],
                                   stop=fn["stop"])
                else:
                    r = raw.transpose(out=fn["out"], in_=fn["in_"],
                                      identity=fn["ident"])
                if fn["inc"]:
                    r.then_inc(fn["sem"], 1)
            else:
                fn(raw)


def build_program():
    nc = bacc.Bacc("TRN2", target_bir_lowering=False)

    def ein(name, shape, dt=F32):
        return nc.dram_tensor(name, shape, dt, kind="ExternalInput")

    x_in = ein("x", [T, C])
    wkT_in = ein("wkT", [C, C]); wvT_in = ein("wvT", [C, C])
    wrT_in = ein("wrT", [C, C]); woT_in = ein("woT", [C, C])
    w1_in = ein("w1", [C, HID]); w3_in = ein("w3", [C, HID])
    w2_in = ein("w2", [HID, C])
    wg_in = ein("wg", [C, NB])
    chvec_in = ein("chvec", [128, NCT, 5])
    rkv_in = ein("rkv", [1, 16])
    ohcol_in = ein("ohcol", [NB, 1])
    ident_in = ein("identbf", [128, 128], BF16)
    g16_in = ein("g16", [128, NB])
    h16_in = ein("h16", [128, 16], BF16)
    tri_in = ein("triexcl", [NB, NB])
    onesr_in = ein("onesrow", [1, 128])
    ones128_in = ein("ones128", [128, 1])
    tidb_in = ein("tidbase", [128, 1])
    selu_in = ein("selu16", [NB, 128])
    identf8_in = ein("identf8", [NB, NB])
    out_ext = nc.dram_tensor("out", [T, C], F32, kind="ExternalOutput")

    def dram(name, shape, dt):
        return nc.dram_tensor(name, shape, dt)

    g_mine = dram("g_mine", [T, C], BF16)
    g_all = dram("g_all", [NT, C], BF16)
    sc_mine = dram("sc_mine", [16, 256 * NB], F32)
    sc_all = dram("sc_all", [128, 256 * NB], F32)
    eo_mine = dram("eo_mine", [CAP, C], BF16)
    eo_all = dram("eo_all", [NB * CAP, C], BF16)
    x2_plane = dram("x2_plane", [T, C], F32)
    w_ids = dram("w_ids", [16, WPAD], I32)
    w_sc = dram("w_sc", [16, WPAD], F32)
    lin_sc = dram("lin_sc", [1, CAP], F32)

    top = ExitStack()

    def sb(name, shape, dt=F32, ctx=None):
        return (ctx or top).enter_context(nc.sbuf_tensor(name, shape, dt))

    def mkpsum(name, shape, dt=F32):
        return top.enter_context(nc.psum_tensor(name, shape, dt))

    def mksem(name):
        return top.enter_context(nc.semaphore(name))

    ident = sb("ident_s", [128, 128], BF16)
    chv = sb("chv", [128, NCT, 8])
    dvec = sb("dvec", [128, NCT, 2])
    wgt = sb("wgt", [128, NCT, NB], BF16)
    rkv = sb("rkv_s", [1, 16])
    ohcol = sb("ohcol_s", [NB, 1])
    g16 = sb("g16_s", [128, NB])
    h16 = sb("h16_s", [128, 16], BF16)
    tri = sb("tri_s", [NB, NB])
    onesr = sb("onesr_s", [1, 128])
    ones128 = sb("ones128_s", [128, 1])
    oh128 = sb("oh128", [128, NB])
    w128 = sb("w128", [128, 1])
    tidb = sb("tidb_s", [128, 1])
    selu16 = sb("selu16_s", [NB, 128])
    identf8 = sb("identf8_s", [NB, NB])
    iota256 = sb("iota256", [128, 256])
    iota48w = sb("iota48w", [16, 48])
    iotap16 = sb("iotap16", [16, 1])
    iotaloc = sb("iotaloc", [16, 256])
    idxcol = sb("idxcol", [128, 1], U32)
    zero16 = sb("zero16", [16, WPAD])
    cid = sb("cid", [128, 256])
    csc = sb("csc", [128, 256])
    tid = sb("tid", [128, 256])
    bnd = sb("bnd", [1, 64])
    cnt = sb("cnt", [128, NB])
    cstat = sb("cstat", [NB, NB])
    starts = sb("starts", [NB, NB])
    sm8 = sb("sm8", [NB, 1])
    scr8 = sb("scr8", [NB, NB])
    myrow = sb("myrow", [1, 2 * NB])
    owcnt_f = sb("owcnt_f", [1, 2 * NB])
    owcnt = sb("owcnt", [1, 2 * NB], I32)
    lm16 = sb("lm16", [16, 256, NB])
    sg_in = sb("sg_in", [16, 256])
    sg_in2 = sb("sg_in2", [16, 256])
    sg_out = sb("sg_out", [16, 48], I32)
    sg_outf = sb("sg_outf", [16, 48])
    sg_nf = sb("sg_nf", [1, 1], U32)
    nfb = sb("nfb", [1, 1])
    m48 = sb("m48", [16, 48])
    sg_ids = sb("sg_ids", [16, 48])
    sg_scv = sb("sg_scv", [16, 48])
    mcol = sb("mcol", [16, 4])
    gl32 = sb("gl32", [16, 256], I32)
    glf = sb("glf", [16, 256])
    gl_idx = sb("gl_idx", [128, 256], I16)
    ow_g = sb("ow_g", [128, NB, 48], I16)
    ow_s = sb("ow_s", [128, NB, 48], I16)
    lsl = sb("lsl", [16, 256])
    stats = sb("stats", [128, 16])
    sq_scr = sb("sq_scr", [128, 1024])
    sq_scr2 = sb("sq_scr2", [128, 1024])
    gb = sb("gb", [128, 2, 1024], BF16)
    gcm = sb("gcm", [128, 2, NCT, 128], BF16)
    scem = sb("scem", [NB, 128])
    sct = sb("sct", [128, NB])
    p_ex = mkpsum("p_ex", [128, 32])
    p_tr = mkpsum("p_tr", [128, 512], BF16)
    p_tr2 = mkpsum("p_tr2", [128, 512], BF16)
    p_mm0 = mkpsum("p_mm0", [128, 512])
    p_mm1 = mkpsum("p_mm1", [128, 512])
    p_mm2 = mkpsum("p_mm2", [128, 512])
    p_mm3 = mkpsum("p_mm3", [128, 512])
    p_sc = mkpsum("p_sc", [128, 128])

    s_q = mksem("s_q"); s_qd = mksem("s_qd")
    s_v = mksem("s_v"); s_vd = mksem("s_vd")
    s_s = mksem("s_s"); s_sd = mksem("s_sd")
    s_t = mksem("s_t"); s_td = mksem("s_td")
    s_g = mksem("s_g"); s_gd = mksem("s_gd")
    s_cc = mksem("s_cc")

    Q = Rec("sync", s_q, s_qd)
    V = Rec("vector", s_v, s_vd)
    S = Rec("scalar", s_s, s_sd)
    TE = Rec("tensor", s_t, s_td)
    G = Rec("gpsimd", s_g, s_gd)
    ccs = [0]

    Q.dma(x2_plane.ap(), x_in.ap())
    for dst, src in [(ident[:, :], ident_in.ap()),
                     (chv[:, :, 0:5], chvec_in.ap()),
                     (rkv[:, :], rkv_in.ap()),
                     (ohcol[:, :], ohcol_in.ap()),
                     (g16[:, :], g16_in.ap()),
                     (h16[:, :], h16_in.ap()),
                     (tri[:, :], tri_in.ap()),
                     (onesr[:, :], onesr_in.ap()),
                     (ones128[:, :], ones128_in.ap()),
                     (tidb[:, :], tidb_in.ap()),
                     (selu16[:, :], selu_in.ap()),
                     (identf8[:, :], identf8_in.ap())]:
        G.dma(dst, src)
    G.dma(wgt[:, :, :], wg_in.ap().rearrange("(ct p) e -> p ct e", p=128))
    m_consts = G.dmark()
    G.iota(iota256[:, :], [[16, 256]], 0, 0)
    G.iota(iota48w[:, :], [[16, 48]], 0, 1)
    G.iota(iotap16[:, :], [[0, 1]], 0, 1)
    G.iota(iotaloc[:, :], [[16, 256]], 0, 1)

    V.wait(m_consts)
    V.memset(zero16[:, :], 0.0)
    V.ts(chv[:, :, 5:8], chv[:, :, 2:5], -1.0, 1.0, ALU.mult, ALU.add)

    S.wait(m_consts)
    S.act(stats[:, 0:8], chv[:, :, 0:1].rearrange("p a b -> p (a b)"), ACT.Exp)
    S.nop()
    S.act(dvec[:, :, 0:1].rearrange("p a b -> p (a b)"), stats[:, 0:8],
          ACT.Exp, scale=-1.0)
    S.act(dvec[:, :, 1:2].rearrange("p a b -> p (a b)"),
          chv[:, :, 1:2].rearrange("p a b -> p (a b)"), ACT.Exp)
    m_dvec = S.mark()

    TE.wait(m_consts)
    TE.mm(p_ex[:, 0:8], onesr[:, :], rkv[0:1, 0:8], True, True)
    TE.minc()
    S.wait(TE.mark())
    S.act(oh128[:, :], p_ex[:, 0:8], ACT.Copy)
    m_oh128 = S.mark()
    V.wait(m_oh128)
    V.tt(cnt[:, :], g16[:, :], oh128[:, :], ALU.mult)
    V.reduce(w128[:, :], cnt[:, :], ALU.add)
    m_w128 = V.mark()

    lg_prev = {"tr": None, "gstore": None, "mm": None, "sct_q": None}

    def lngate(i, xin_ap, xwait=None):
        if xwait is not None:
            V.wait(xwait)
            S.wait(xwait)
        V.ts(sq_scr[:, :], xin_ap, 1.0, None, ALU.mult, ALU.add,
             accum=stats[:, 0:1])
        S.act(sq_scr2[:, :], xin_ap, ACT.Square, accum=stats[:, 1:2])
        m_sq = S.mark()
        V.wait(m_sq)
        V.drain()
        V.ts(stats[:, 2:3], stats[:, 0:1], -1.0 / C)
        V.tt(stats[:, 4:5], stats[:, 2:3], stats[:, 2:3], ALU.mult)
        V.ts(stats[:, 6:7], stats[:, 1:2], 1.0 / C, EPS, ALU.mult, ALU.add)
        V.tt(stats[:, 3:4], stats[:, 6:7], stats[:, 4:5], ALU.subtract)
        V.recip(stats[:, 7:8], stats[:, 3:4])
        m_rc = V.mark()
        S.wait(m_rc)
        S.act(stats[:, 8:9], stats[:, 7:8], ACT.Sqrt)
        S.nop()
        S.act(stats[:, 9:10], stats[:, 2:3], ACT.Identity, scale=stats[:, 8:9])
        m_bias = S.mark()
        V.wait(m_bias)
        V.wait(lg_prev["tr"])
        V.wait(lg_prev["gstore"])
        par = i % 2
        V.ts(gb[:, par, :], xin_ap, stats[:, 8:9], stats[:, 9:10],
             ALU.mult, ALU.add)
        m_g = V.mark()
        Q.wait(m_g)
        gst = Q.dma(g_mine.ap()[i * 128:(i + 1) * 128, :], gb[:, par, :])
        TE.wait(m_g)
        TE.wait(lg_prev["mm"])
        for h in range(2):
            pt = p_tr if h == 0 else p_tr2
            for j in range(4):
                ct = h * 4 + j
                TE.trans(pt[:, j * 128:(j + 1) * 128],
                         gb[:, par, ct * 128:(ct + 1) * 128], ident[:, :])
        TE.minc()
        m_tr = TE.mark()
        S.wait(m_tr)
        S.act(gcm[:, par, 0:4, :].rearrange("p a b -> p (a b)"), p_tr[:, :],
              ACT.Copy)
        S.act(gcm[:, par, 4:8, :].rearrange("p a b -> p (a b)"), p_tr2[:, :],
              ACT.Copy)
        m_gcm = S.mark()
        TE.wait(m_gcm)
        TE.wait(lg_prev["sct_q"])
        for ct in range(NCT):
            TE.mm(p_sc[0:NB, :], wgt[:, ct, :], gcm[:, par, ct, :],
                  ct == 0, ct == NCT - 1)
        TE.minc()
        m_mm = TE.mark()
        S.wait(m_mm)
        S.act(scem[:, :], p_sc[0:NB, :], ACT.Copy)
        m_scem = S.mark()
        TE.wait(m_scem)
        TE.trans(p_ex[:, 8:16], scem[:, :], identf8[:, :])
        TE.minc()
        m_sct = TE.mark()
        S.wait(m_sct)
        S.act(sct[:, :], p_ex[:, 8:16], ACT.Exp)
        m_exp = S.mark()
        V.wait(m_exp)
        V.reduce(stats[:, 10:11], sct[:, :], ALU.add)
        V.recip(stats[:, 11:12], stats[:, 10:11])
        V.ts(sct[:, :], sct[:, :], stats[:, 11:12])
        m_sm = V.mark()
        Q.wait(m_sm)
        scq = Q.dma(sc_mine.ap().rearrange("q (f e) -> q f e", e=NB)
                    [:, 8 * i:8 * i + 8, :].rearrange("q w e -> w q e"),
                    sct[:, :])
        lg_prev["tr"] = m_tr
        lg_prev["gstore"] = gst
        lg_prev["mm"] = m_mm
        lg_prev["sct_q"] = scq

    p1ctx = ExitStack()
    xt2 = sb("xt2", [128, 2, 1024], ctx=p1ctx)
    ldm = [None] * 34
    ldm[0] = Q.dma(xt2[:, 0, :], x_in.ap()[0:128, :])
    ldm[1] = Q.dma(xt2[:, 1, :], x_in.ap()[128:256, :])
    for i in range(32):
        lngate(i, xt2[:, i % 2, :], xwait=ldm[i])
        if i + 2 < 32:
            Q.wait(V.mark())
            ldm[i + 2] = Q.dma(xt2[:, i % 2, :],
                               x_in.ap()[(i + 2) * 128:(i + 3) * 128, :])
    p1ctx.close()
    m_lg1_q = Q.dmark()

    G.wait(m_lg1_q)
    m_ag_sc1 = G.cc("AllGather", ALU.bypass, [sc_mine.ap().opt()],
                    [sc_all.ap().opt()], s_cc, ccs)
    m_ag_g1 = G.cc("AllGather", ALU.bypass, [g_mine.ap().opt()],
                   [g_all.ap().opt()], s_cc, ccs)

    def emit_selection(m_ag_sc, selset, m_wfree=()):
        selsc, selmsk, seltie, selmb, selscr, selscr8 = selset
        Q.wait(m_ag_sc)
        Q.dma(selsc[:, :, :],
              sc_all.ap().rearrange("p (f e) -> p f e", e=NB))
        m_sel_ld = Q.dmark()
        for mk in m_wfree:
            G.wait(mk)
        G.dma(w_ids.ap(), zero16[:, :])
        G.dma(w_sc.ap(), zero16[:, :])

        V.wait(m_sel_ld)
        V.ts(tid[:, :], iota256[:, :], tidb[:, 0:1], None, ALU.add)
        V.memset(bnd[:, 0:8], 0.0)
        V.memset(bnd[:, 8:16], 1.0)

        for it in range(VAL_ITERS):
            V.tt(bnd[:, 16:24], bnd[:, 0:8], bnd[:, 8:16], ALU.add)
            V.ts(bnd[:, 16:24], bnd[:, 16:24], 0.5)
            TE.wait(V.mark())
            TE.mm(p_ex[:, 16:24], onesr[:, :], bnd[0:1, 16:24], True, True)
            TE.minc()
            V.wait(TE.mark())
            for e in range(NB):
                V.ts(selscr[:, :], selsc[:, :, e], p_ex[:, 16 + e:17 + e],
                     None, ALU.is_gt, ALU.add, accum=cnt[:, e:e + 1])
            TE.wait(V.mark())
            TE.mm(p_ex[0:1, 24:32], ones128[:, :], cnt[:, :], True, True)
            TE.minc()
            V.wait(TE.mark())
            V.ts(bnd[:, 24:32], p_ex[0:1, 24:32], float(CAP), None, ALU.is_ge)
            V.tt(bnd[:, 32:40], bnd[:, 16:24], bnd[:, 0:8], ALU.subtract)
            V.tt(bnd[:, 40:48], bnd[:, 24:32], bnd[:, 32:40], ALU.mult)
            V.tt(bnd[:, 0:8], bnd[:, 0:8], bnd[:, 40:48], ALU.add)
            V.tt(bnd[:, 32:40], bnd[:, 8:16], bnd[:, 16:24], ALU.subtract)
            V.tt(bnd[:, 40:48], bnd[:, 24:32], bnd[:, 32:40], ALU.mult)
            V.tt(bnd[:, 8:16], bnd[:, 16:24], bnd[:, 40:48], ALU.add)

        TE.wait(V.mark())
        TE.mm(p_ex[:, 16:24], onesr[:, :], bnd[0:1, 8:16], True, True)
        TE.mm(p_ex[:, 24:32], onesr[:, :], bnd[0:1, 0:8], True, True)
        TE.minc()
        V.wait(TE.mark())
        for e in range(NB):
            V.ts(selmsk[:, :, e], selsc[:, :, e], p_ex[:, 16 + e:17 + e],
                 None, ALU.is_gt, ALU.add, accum=cnt[:, e:e + 1])
        for e in range(NB):
            V.ts(seltie[:, :, e], selsc[:, :, e], p_ex[:, 24 + e:25 + e],
                 None, ALU.is_gt)
        for e in range(NB):
            V.tt(seltie[:, :, e], seltie[:, :, e], selmsk[:, :, e],
                 ALU.subtract)
        TE.wait(V.mark())
        TE.mm(p_ex[0:1, 24:32], ones128[:, :], cnt[:, :], True, True)
        TE.minc()
        V.wait(TE.mark())
        V.ts(bnd[:, 48:56], p_ex[0:1, 24:32], -1.0, float(CAP),
             ALU.mult, ALU.add)
        V.memset(bnd[:, 0:8], 0.0)
        V.memset(bnd[:, 8:16], float(NT))

        for it in range(TIE_ITERS):
            V.tt(bnd[:, 16:24], bnd[:, 0:8], bnd[:, 8:16], ALU.add)
            V.ts(bnd[:, 16:24], bnd[:, 16:24], 0.5)
            TE.wait(V.mark())
            TE.mm(p_ex[:, 16:24], onesr[:, :], bnd[0:1, 16:24], True, True)
            TE.minc()
            V.wait(TE.mark())
            for e in range(NB):
                V.stt(selscr[:, :], tid[:, :], p_ex[:, 16 + e:17 + e],
                      seltie[:, :, e], ALU.is_lt, ALU.mult,
                      accum=cnt[:, e:e + 1])
            TE.wait(V.mark())
            TE.mm(p_ex[0:1, 24:32], ones128[:, :], cnt[:, :], True, True)
            TE.minc()
            V.wait(TE.mark())
            V.tt(bnd[:, 24:32], p_ex[0:1, 24:32], bnd[:, 48:56], ALU.is_ge)
            V.tt(bnd[:, 32:40], bnd[:, 16:24], bnd[:, 8:16], ALU.subtract)
            V.tt(bnd[:, 40:48], bnd[:, 24:32], bnd[:, 32:40], ALU.mult)
            V.tt(bnd[:, 8:16], bnd[:, 8:16], bnd[:, 40:48], ALU.add)
            V.tt(bnd[:, 32:40], bnd[:, 0:8], bnd[:, 16:24], ALU.subtract)
            V.tt(bnd[:, 40:48], bnd[:, 24:32], bnd[:, 32:40], ALU.mult)
            V.tt(bnd[:, 0:8], bnd[:, 16:24], bnd[:, 40:48], ALU.add)

        TE.wait(V.mark())
        TE.mm(p_ex[:, 16:24], onesr[:, :], bnd[0:1, 8:16], True, True)
        TE.minc()
        V.wait(TE.mark())
        for e in range(NB):
            V.stt(selscr[:, :], tid[:, :], p_ex[:, 16 + e:17 + e],
                  seltie[:, :, e], ALU.is_lt, ALU.mult)
            V.tt(selmsk[:, :, e], selmsk[:, :, e], selscr[:, :], ALU.add)
        for e in range(NB):
            V.ts(selscr[:, :], selmsk[:, :, e], 1.0, None, ALU.mult, ALU.add,
                 accum=cnt[:, e:e + 1])
        V.wait(m_w128)
        V.tt(selmb[:, :, :], selmsk[:, :, :],
             w128[:, 0:1].rearrange("p (a b) -> p a b", a=1)
             .to_broadcast([128, 256, NB]), ALU.mult)
        m_fmask = V.mark()
        TE.wait(m_fmask)
        TE.mm(p_ex[0:8, 24:32], g16[:, :], cnt[:, :], True, True)
        TE.minc()
        S.wait(TE.mark())
        S.act(cstat[:, :], p_ex[0:8, 24:32], ACT.Copy)
        TE.wait(S.mark())
        TE.mm(p_ex[0:8, 16:24], tri[:, :], cstat[:, :], True, True)
        TE.minc()
        S.wait(TE.mark())
        S.act(starts[:, :], p_ex[0:8, 16:24], ACT.Copy)
        m_starts = S.mark()
        TE.wait(m_starts)
        TE.mm(p_ex[0:1, 16:24], ohcol[:, :], starts[:, :], True, True)
        TE.mm(p_ex[0:1, 24:32], ohcol[:, :], cstat[:, :], True, True)
        TE.minc()
        S.wait(TE.mark())
        S.act(myrow[:, 0:8], p_ex[0:1, 16:24], ACT.Copy)
        S.act(myrow[:, 8:16], p_ex[0:1, 24:32], ACT.Copy)
        m_myrow = S.mark()

        V.wait(m_starts)
        V.tt(scr8[:, :], starts[:, :], oh128[0:8, :], ALU.mult)
        V.reduce(sm8[:, :], scr8[:, :], ALU.add)
        ohb = oh128[:, 0:NB].rearrange("p (a e) -> p a e", a=1) \
            .to_broadcast([128, 256, NB])
        V.tt(selscr8[:, :, :], selmsk[:, :, :], ohb, ALU.mult)
        V.reduce(selscr[:, :], selscr8[:, :, :], ALU.add)
        V.tt(selscr8[:, :, :], selsc[:, :, :], ohb, ALU.mult)
        V.reduce(csc[:, :], selscr8[:, :, :], ALU.add)
        V.ts(cid[:, :], tid[:, :], 1.0, 2.0, ALU.mult, ALU.add)
        V.tt(cid[:, :], cid[:, :], selscr[:, :], ALU.mult)
        V.ts(cid[:, :], cid[:, :], 1.0, -1.0, ALU.mult, ALU.add)
        V.ts(csc[:, :], csc[:, :], 1.0, 1.0, ALU.mult, ALU.add)
        V.tt(csc[:, :], csc[:, :], selscr[:, :], ALU.mult)
        V.ts(csc[:, :], csc[:, :], 1.0, -1.0, ALU.mult, ALU.add)
        m_cands = V.mark()

        TE.wait(m_cands)
        for ncol in range(4):
            if ncol > 0:
                TE.wait(S.mark())
            TE.mm(p_mm0[0:16, :], h16[:, :],
                  selmb[:, :, :].rearrange("p f e -> p (f e)")
                  [:, ncol * 512:(ncol + 1) * 512], True, True)
            TE.minc()
            S.wait(TE.mark())
            S.act(lm16[:, :, :].rearrange("p f e -> p (f e)")
                  [:, ncol * 512:(ncol + 1) * 512], p_mm0[0:16, :], ACT.Copy)
        m_lm16 = S.mark()
        return dict(cands=m_cands, myrow=m_myrow, lm16=m_lm16)

    def emit_merge(msel):
        G.lib(LC.sparse_gather)
        G.wait(msel["cands"])
        m_prev_v = None
        m_prev_mg = None
        for u in range(NB):
            G.wait(m_prev_v)
            G.dma(sg_in[:, :], cid[16 * u:16 * u + 16, :])
            G.dma(sg_in2[:, :], csc[16 * u:16 * u + 16, :])
            m_mv = G.dmark()
            G.wait(m_mv)
            G.sg(sg_out[:, :], sg_in[:, :], sg_nf[:, :])
            m_sg1 = G.mark()
            V.wait(m_sg1)
            V.copy(nfb[:, :], sg_nf[:, :])
            TE.wait(V.mark())
            TE.mm(p_ex[0:16, 16:17], onesr[0:1, 0:16], nfb[0:1, 0:1],
                  True, True)
            TE.mm(p_ex[0:16, 17:18], selu16[:, 16 * u:16 * u + 16],
                  sm8[:, :], True, True)
            TE.minc()
            V.wait(TE.mark())
            V.wait(m_prev_mg)
            V.ts(m48[:, :], iota48w[:, :], p_ex[0:16, 16:17], None, ALU.is_lt)
            V.tt(sg_ids[:, :], sg_out[:, :], m48[:, :], ALU.mult)
            V.ts(mcol[:, 0:1], p_ex[0:16, 17:18], 1.0)
            for kpow in (2048.0, 1024.0, 512.0, 256.0, 128.0, 64.0, 32.0,
                         16.0):
                V.ts(mcol[:, 1:2], mcol[:, 0:1], kpow, None, ALU.is_ge)
                V.stt(mcol[:, 0:1], mcol[:, 1:2], -kpow, mcol[:, 0:1],
                      ALU.mult, ALU.add)
            V.tt(mcol[:, 1:2], p_ex[0:16, 17:18], mcol[:, 0:1], ALU.subtract)
            V.ts(mcol[:, 1:2], mcol[:, 1:2], 1.0 / 16)
            V.tt(mcol[:, 2:3], mcol[:, 0:1], iotap16[:, :], ALU.add)
            V.ts(mcol[:, 3:4], mcol[:, 2:3], 16.0, None, ALU.is_ge)
            V.stt(mcol[:, 2:3], mcol[:, 3:4], -16.0, mcol[:, 2:3],
                  ALU.mult, ALU.add)
            V.tt(mcol[:, 3:4], mcol[:, 1:2], mcol[:, 3:4], ALU.add)
            V.stt(mcol[:, 1:2], mcol[:, 2:3], float(WPAD), mcol[:, 3:4],
                  ALU.mult, ALU.add)
            V.copy(idxcol[0:16, 0:1], mcol[:, 1:2])
            m_idx = V.mark()
            G.wait(m_idx)
            G.imerge(w_ids.ap().rearrange("(a r) c -> a (r c)", a=1),
                     idxcol[0:16, 0:1], sg_ids[:, :])
            G.sg(sg_outf[:, :], sg_in2[:, :], sg_nf[:, :])
            m_sg2 = G.mark()
            V.wait(m_sg2)
            V.tt(sg_scv[:, :], sg_outf[:, :], m48[:, :], ALU.mult)
            m_scv = V.mark()
            G.wait(m_scv)
            G.imerge(w_sc.ap().rearrange("(a r) c -> a (r c)", a=1),
                     idxcol[0:16, 0:1], sg_scv[:, :])
            m_prev_v = V.mark()
            m_prev_mg = G.dmark()
        return G.dmark()

    def emit_owner_lists(msel, m_merge):
        V.wait(msel["lm16"])
        V.wait(msel["myrow"])
        V.wait(m_merge)
        for e in range(NB):
            V.ts(lsl[:, :], iotaloc[:, :], 1.0, 2.0, ALU.mult, ALU.add)
            V.tt(lsl[:, :], lsl[:, :], lm16[:, :, e], ALU.mult)
            V.ts(lsl[:, :], lsl[:, :], 1.0, -1.0, ALU.mult, ALU.add)
            m_c = V.mark()
            G.wait(m_c)
            G.sg(sg_out[:, :], lsl[:, :], sg_nf[:, :])
            m_sg = G.mark()
            V.wait(m_sg)
            V.copy(nfb[:, :], sg_nf[:, :])
            TE.wait(V.mark())
            TE.mm(p_ex[0:16, 16:17], onesr[0:1, 0:16], nfb[0:1, 0:1],
                  True, True)
            TE.mm(p_ex[0:16, 17:18], onesr[0:1, 0:16], myrow[0:1, e:e + 1],
                  True, True)
            TE.minc()
            V.wait(TE.mark())
            V.ts(m48[:, :], iota48w[:, :], p_ex[0:16, 16:17], None, ALU.is_lt)
            V.tt(sg_scv[:, :], sg_out[:, :], m48[:, :], ALU.mult)
            V.ts(sg_scv[:, :], sg_scv[:, :], 1.0, -1.0, ALU.mult, ALU.add)
            V.copy(ow_s[0:16, e, :], sg_scv[:, :])
            V.ts(sg_scv[:, :], iota48w[:, :], p_ex[0:16, 17:18],
                 float(e * CAP) + 1.0, ALU.add, ALU.add)
            V.tt(sg_scv[:, :], sg_scv[:, :], m48[:, :], ALU.mult)
            V.ts(sg_scv[:, :], sg_scv[:, :], 1.0, -1.0, ALU.mult, ALU.add)
            V.copy(ow_g[0:16, e, :], sg_scv[:, :])
        ohf = owcnt_f.rearrange("a (e h) -> a e h", h=2)
        V.ts(ohf[:, :, 0], myrow[0:1, 8:16], 384.0, None, ALU.min)
        V.tt(ohf[:, :, 1], myrow[0:1, 8:16], ohf[:, :, 0], ALU.subtract)
        V.copy(owcnt[:, :], owcnt_f[:, :])
        m_ow0 = V.mark()
        Q.wait(m_ow0)
        Q.dma(ow_s[16:32, :, :], ow_s[0:16, :, :])
        Q.dma(ow_g[16:32, :, :], ow_g[0:16, :, :])
        Q.wait(Q.dmark())
        Q.dma(ow_s[32:64, :, :], ow_s[0:32, :, :])
        Q.dma(ow_g[32:64, :, :], ow_g[0:32, :, :])
        Q.wait(Q.dmark())
        Q.dma(ow_s[64:128, :, :], ow_s[0:64, :, :])
        Q.dma(ow_g[64:128, :, :], ow_g[0:64, :, :])
        return Q.dmark()

    def emit_gl(m_merge, scrow, scband):
        Q.wait(m_merge)
        Q.dma(gl32[:, :], w_ids.ap()[:, 0:256])
        V.wait(Q.dmark())
        V.ts(glf[:, :], gl32[:, :], 1.0, -1.0, ALU.mult, ALU.add)
        V.copy(gl_idx[0:16, :], glf[:, :])
        m_gl0 = V.mark()
        Q.wait(m_gl0)
        Q.dma(gl_idx[16:32, :], gl_idx[0:16, :])
        Q.wait(Q.dmark())
        Q.dma(gl_idx[32:64, :], gl_idx[0:32, :])
        Q.wait(Q.dmark())
        Q.dma(gl_idx[64:128, :], gl_idx[0:64, :])
        m_gl = Q.dmark()
        for rr in range(16):
            Q.dma(lin_sc.ap().rearrange("a (c r) -> (a c) r", r=16)[:, rr:rr + 1],
                  w_sc.ap()[rr:rr + 1, 0:256].rearrange("r c -> c r"),
                  allow_slow_non_contiguous=True)
        m_lin = Q.dmark()
        ret_lin = m_lin
        G.lib(LC.mlp)
        G.wait(m_lin)
        m_scrow = G.dma(scrow[:, :], lin_sc.ap())
        G.wait(m_scrow)
        G.pbcast(scband[:, :], scrow[:, :], 128)
        return dict(gl=m_gl, scband=G.mark(), lin=ret_lin)

    def emit_ffn(mrg, m_ag_g, fset, m_eo_free):
        fw1, fw3, fw2db, rt, hbuf, sil, outt, scrow, scband = fset
        mw = G.dmark()
        G.dma(fw1[:, :, :, :],
              w1_in.ap().rearrange("(ct p) (ht j) -> p ct ht j", p=128, j=128))
        G.dma(fw3[:, :, :, :],
              w3_in.ap().rearrange("(ct p) (ht j) -> p ct ht j", p=128, j=128))
        m_w13 = G.dmark()
        G.wait(mrg["gl"])
        G.wait(m_ag_g)
        G.dma(fw2db[:, :, :], w2_in.ap().rearrange("(ht p) d -> p ht d", p=128))
        m_w2 = G.dmark()
        gmk = [None] * 4
        rt_free = None
        m_eo_prev = None
        for k in range(4):
            G.wait(rt_free)
            gmk[k] = G.gather(rt[:, :, :],
                              g_all.ap(), gl_idx[:, k * 64:(k + 1) * 64],
                              1024, C, transpose=True)
            V.wait(gmk[k])
            V.wait(mrg["scband"])
            V.tt(rt[:, :, :], rt[:, :, :],
                 scband[:, k * 1024:(k + 1) * 1024]
                 .rearrange("p (a b) -> p a b", a=1)
                 .to_broadcast([128, NCT, 1024]), ALU.mult)
            m_rt = V.mark()

            for n in range(2):
                kk = 2 * k + n
                TE.wait(m_rt)
                TE.wait(S.mark())
                TE.wait(m_w13)
                for ht in range(NHT):
                    pmu = p_mm0 if ht % 2 == 0 else p_mm2
                    pmv = p_mm1 if ht % 2 == 0 else p_mm3
                    for ct in range(NCT):
                        TE.mm(pmu[:, :], fw1[:, ct, ht, :],
                              rt[:, ct, n * 512:(n + 1) * 512],
                              ct == 0, ct == NCT - 1)
                    for ct in range(NCT):
                        TE.mm(pmv[:, :], fw3[:, ct, ht, :],
                              rt[:, ct, n * 512:(n + 1) * 512],
                              ct == 0, ct == NCT - 1)
                    TE.minc()
                    m_u = TE.mark()
                    S.wait(m_u)
                    S.act(sil[:, ht % 2, :], pmu[:, :], ACT.Silu)
                    m_sil = S.mark()
                    V.wait(m_sil)
                    V.stt(hbuf[:, ht, :], pmv[:, :], 0.5, sil[:, ht % 2, :],
                          ALU.mult, ALU.mult)
                    if ht >= 1:
                        TE.wait(V.mark())
                m_h = V.mark()
                TE.wait(m_h)
                TE.wait(m_w2)
                for st4 in range(4):
                    if st4 > 0:
                        TE.wait(S.mark())
                    else:
                        S.wait(m_eo_prev)
                    for dn in range(2):
                        pm = p_mm2 if dn == 0 else p_mm3
                        for ht in range(NHT):
                            TE.mm(pm[:, :],
                                  hbuf[:, ht, st4 * 128:(st4 + 1) * 128],
                                  fw2db[:, ht, dn * 512:(dn + 1) * 512],
                                  ht == 0, ht == NHT - 1)
                    TE.minc()
                    m_o = TE.mark()
                    S.wait(m_o)
                    S.act(outt[:, st4, 0:512], p_mm2[:, :], ACT.Copy)
                    S.act(outt[:, st4, 512:1024], p_mm3[:, :], ACT.Copy)
                m_ev = S.mark()
                Q.wait(m_ev)
                if kk == 0:
                    Q.wait(m_eo_free)
                Q.dma(eo_mine.ap().rearrange("(a p) d -> p a d", p=128)
                      [:, kk * 4:(kk + 1) * 4, :], outt[:, :, :])
                m_eo_prev = Q.dmark()
            rt_free = TE.mark()
        m_eo = Q.dmark()
        G.wait(m_eo)
        m_ag_eo = G.cc("AllGather", ALU.bypass, [eo_mine.ap().opt()],
                       [eo_all.ap().opt()], s_cc, ccs)
        return m_ag_eo

    def emit_combine(m_ag_eo, m_owner, target_ap, m_target_ready, cset):
        cmb_b, cmb_f = cset
        G.wait(m_ag_eo)
        G.wait(m_owner)
        G.wait(m_target_ready)
        prev_cast = None
        for e in range(NB):
            for hhalf in range(2):
                G.wait(prev_cast)
                mg = G.gather(cmb_b[:, :, :], eo_all.ap(),
                              ow_g[:, e, hhalf * 24:(hhalf + 1) * 24],
                              384, C, transpose=False,
                              count_src=owcnt[0:1, 2 * e + hhalf:
                                              2 * e + hhalf + 1])
                V.wait(mg)
                V.copy(cmb_f[:, :, :].rearrange("p a b -> p (a b)"),
                       cmb_b[:, :, :].rearrange("p a b -> p (a b)"))
                m_cast = V.mark()
                prev_cast = m_cast
                G.wait(m_cast)
                G.scatter_add(target_ap, cmb_f[:, :, :],
                              ow_s[:, e, hhalf * 24:(hhalf + 1) * 24],
                              384, C,
                              count_src=owcnt[0:1, 2 * e + hhalf:
                                              2 * e + hhalf + 1])
        return G.dmark()

    sel_ctx = ExitStack()
    selset = (sb("selsc", [128, 256, NB], ctx=sel_ctx),
              sb("selmsk", [128, 256, NB], ctx=sel_ctx),
              sb("seltie", [128, 256, NB], ctx=sel_ctx),
              sb("selmb", [128, 256, NB], BF16, ctx=sel_ctx),
              sb("selscr", [128, 256], ctx=sel_ctx),
              sb("selscr8", [128, 256, NB], ctx=sel_ctx))
    msel1 = emit_selection(m_ag_sc1, selset)
    m_merge1 = emit_merge(msel1)
    m_owner1 = emit_owner_lists(msel1, m_merge1)
    sel_ctx.close()

    ffn_ctx = ExitStack()
    fset1 = (sb("fw1", [128, NCT, NHT, 128], BF16, ctx=ffn_ctx),
             sb("fw3", [128, NCT, NHT, 128], BF16, ctx=ffn_ctx),
             sb("fw2db", [128, NHT, 1024], BF16, ctx=ffn_ctx),
             sb("rt", [128, NCT, 1024], BF16, ctx=ffn_ctx),
             sb("hbuf", [128, NHT, 512], BF16, ctx=ffn_ctx),
             sb("sil", [128, 2, 512], BF16, ctx=ffn_ctx),
             sb("outt", [128, 4, 1024], BF16, ctx=ffn_ctx),
             sb("scrow", [1, CAP], BF16, ctx=ffn_ctx),
             sb("scband", [128, CAP], BF16, ctx=ffn_ctx))
    mgl1 = emit_gl(m_merge1, fset1[7], fset1[8])
    mrg1 = dict(gl=mgl1["gl"], scband=mgl1["scband"])
    m_ag_eo1 = emit_ffn(mrg1, m_ag_g1, fset1, None)
    ffn_ctx.close()
    cmb_ctx = ExitStack()
    cset1 = (sb("cmb_b", [128, 3, 1024], BF16, ctx=cmb_ctx),
             sb("cmb_f", [128, 3, 1024], ctx=cmb_ctx))
    m_cmb1 = emit_combine(m_ag_eo1, m_owner1, x2_plane.ap(), Q.dmark(), cset1)
    cmb_ctx.close()

    attn_ctx = ExitStack()
    wk4 = sb("wk4", [128, NCT, NCT, 128], BF16, ctx=attn_ctx)
    wv4 = sb("wv4", [128, NCT, NCT, 128], BF16, ctx=attn_ctx)
    wr4 = sb("wr4", [128, NCT, NCT, 128], BF16, ctx=attn_ctx)
    wo4 = sb("wo4", [128, NCT, 1024], BF16, ctx=attn_ctx)
    xt34 = sb("xt34", [128, 4, 1024], ctx=attn_ctx)
    g2cm = sb("g2cm", [128, NCT, CH + 1], BF16, ctx=attn_ctx)
    mixb = sb("mixb", [128, 3, CH], BF16, ctx=attn_ctx)
    xk = sb("xk", [128, NCT, CH], BF16, ctx=attn_ctx)
    xv = sb("xv", [128, NCT, CH], BF16, ctx=attn_ctx)
    xr = sb("xr", [128, NCT, CH], BF16, ctx=attn_ctx)
    kd = sb("kd", [128, CH], ctx=attn_ctx)
    vdd = sb("vdd", [128, CH], ctx=attn_ctx)
    ad = sb("ad", [128, CH], ctx=attn_ctx)
    avd = sb("avd", [128, CH], ctx=attn_ctx)
    srd = sb("srd", [128, CH], BF16, ctx=attn_ctx)
    numb = sb("numb", [128, CH], ctx=attn_ctx)
    denb = sb("denb", [128, CH], ctx=attn_ctx)
    yb = sb("yb", [128, CH], ctx=attn_ctx)
    zb = sb("zb", [128, NCT, CH], BF16, ctx=attn_ctx)
    pbuf = sb("pbuf", [128, CH + 1], ctx=attn_ctx)
    qbuf = sb("qbuf", [128, CH + 1], ctx=attn_ctx)
    pcar = sb("pcar", [128, NCT], ctx=attn_ctx)
    qcar = sb("qcar", [128, NCT], ctx=attn_ctx)
    ddfull = sb("ddfull", [128, NCT, CH], ctx=attn_ctx)

    G.lib(LC.standard)
    G.wait(m_cmb1)
    G.wait(TE.mark())
    G.wait(S.mark())
    G.wait(V.mark())
    G.dma(wk4[:, :, :, :],
          wkT_in.ap().rearrange("(ct p) (dt j) -> p ct dt j", p=128, j=128))
    G.dma(wv4[:, :, :, :],
          wvT_in.ap().rearrange("(ct p) (dt j) -> p ct dt j", p=128, j=128))
    G.dma(wr4[:, :, :, :],
          wrT_in.ap().rearrange("(ct p) (dt j) -> p ct dt j", p=128, j=128))
    G.dma(wo4[:, :, :],
          woT_in.ap().rearrange("(ct p) d -> p ct d", p=128))
    m_aw = G.dmark()

    V.wait(m_dvec)
    V.wait(m_aw)
    for dt in range(NCT):
        V.ts(ddfull[:, dt, :], dvec[:, dt, 0:1].to_broadcast([128, CH]), 1.0)
    V.memset(pcar[:, :], 0.0)
    V.memset(qcar[:, :], 0.0)
    V.memset(g2cm[:, :, 0:1].rearrange("p a b -> p (a b)"), 0.0)
    m_attn_init = V.mark()

    m_x2_ready = m_cmb1
    prev_xk_cons = None
    prev_xt_cons = None
    for c in range(NCH):
        Q.wait(m_x2_ready)
        Q.wait(prev_xt_cons)
        mx = Q.dma(xt34[:, :, :],
                   x2_plane.ap()[c * CH:(c + 1) * CH, :]
                   .rearrange("(a p) d -> p a d", p=128))
        if c > 0:
            V.copy(g2cm[:, :, 0:1].rearrange("p a b -> p (a b)"),
                   g2cm[:, :, CH:CH + 1].rearrange("p a b -> p (a b)"))
        for tt in range(4):
            V.wait(mx)
            S.wait(mx)
            V.wait(m_attn_init)
            V.ts(sq_scr[:, :], xt34[:, tt, :], 1.0, None, ALU.mult, ALU.add,
                 accum=stats[:, 0:1])
            S.act(sq_scr2[:, :], xt34[:, tt, :], ACT.Square,
                  accum=stats[:, 1:2])
            m_sq = S.mark()
            V.wait(m_sq)
            V.drain()
            V.ts(stats[:, 2:3], stats[:, 0:1], -1.0 / C)
            V.tt(stats[:, 4:5], stats[:, 2:3], stats[:, 2:3], ALU.mult)
            V.ts(stats[:, 6:7], stats[:, 1:2], 1.0 / C, EPS, ALU.mult,
                 ALU.add)
            V.tt(stats[:, 3:4], stats[:, 6:7], stats[:, 4:5], ALU.subtract)
            V.recip(stats[:, 7:8], stats[:, 3:4])
            m_rc = V.mark()
            S.wait(m_rc)
            S.act(stats[:, 8:9], stats[:, 7:8], ACT.Sqrt)
            S.nop()
            S.act(stats[:, 9:10], stats[:, 2:3], ACT.Identity,
                  scale=stats[:, 8:9])
            m_bias = S.mark()
            V.wait(m_bias)
            V.wait(lg_prev["tr"])
            V.ts(gb[:, tt % 2, :], xt34[:, tt, :], stats[:, 8:9],
                 stats[:, 9:10], ALU.mult, ALU.add)
            m_g2 = V.mark()
            TE.wait(m_g2)
            TE.wait(prev_xk_cons)
            for h in range(2):
                pt = p_tr if h == 0 else p_tr2
                for j in range(4):
                    ct = h * 4 + j
                    TE.trans(pt[:, j * 128:(j + 1) * 128],
                             gb[:, tt % 2, ct * 128:(ct + 1) * 128],
                             ident[:, :])
            TE.minc()
            m_tr = TE.mark()
            S.wait(m_tr)
            for h in range(2):
                pt = p_tr if h == 0 else p_tr2
                for j in range(4):
                    ct = h * 4 + j
                    S.act(g2cm[:, ct, 1 + tt * 128:1 + (tt + 1) * 128],
                          pt[:, j * 128:(j + 1) * 128], ACT.Copy)
            lg_prev["tr"] = m_tr
        m_g2cm = S.mark()

        V.wait(m_g2cm)
        S.wait(m_g2cm)
        for ct in range(NCT):
            for mi, t1col in enumerate([5, 6, 7]):
                S.act(mixb[:, mi, :], g2cm[:, ct, 0:CH], ACT.Identity,
                      scale=chv[:, ct, t1col:t1col + 1])
            m_mx = S.mark()
            V.wait(m_mx)
            V.stt(xk[:, ct, :], g2cm[:, ct, 1:CH + 1], chv[:, ct, 2:3],
                  mixb[:, 0, :], ALU.mult, ALU.add)
            V.stt(xv[:, ct, :], g2cm[:, ct, 1:CH + 1], chv[:, ct, 3:4],
                  mixb[:, 1, :], ALU.mult, ALU.add)
            V.stt(xr[:, ct, :], g2cm[:, ct, 1:CH + 1], chv[:, ct, 4:5],
                  mixb[:, 2, :], ALU.mult, ALU.add)
            S.wait(V.mark())
        m_mix = V.mark()

        m_z = None
        m_prev_evac = None
        for dt in range(NCT):
            TE.wait(m_mix)
            TE.wait(m_aw)
            TE.wait(m_prev_evac)
            for ct in range(NCT):
                TE.mm(p_mm0[:, :], wk4[:, ct, dt, :], xk[:, ct, :],
                      ct == 0, ct == NCT - 1)
            for ct in range(NCT):
                TE.mm(p_mm1[:, :], wv4[:, ct, dt, :], xv[:, ct, :],
                      ct == 0, ct == NCT - 1)
            for ct in range(NCT):
                TE.mm(p_mm2[:, :], wr4[:, ct, dt, :], xr[:, ct, :],
                      ct == 0, ct == NCT - 1)
            TE.minc()
            m_kvr = TE.mark()
            S.wait(m_kvr)
            S.wait(V.mark())
            S.act(kd[:, :], p_mm0[:, :], ACT.Copy)
            S.act(vdd[:, :], p_mm1[:, :], ACT.Copy)
            S.act(srd[:, :], p_mm2[:, :], ACT.Sigmoid)
            S.nop()
            S.act(ad[:, :], kd[:, :], ACT.Exp)
            m_sc_dt = S.mark()
            m_prev_evac = m_sc_dt
            V.wait(m_sc_dt)
            V.tt(avd[:, :], ad[:, :], vdd[:, :], ALU.mult)
            V.copy(pbuf[:, 0:1], pcar[:, dt:dt + 1])
            V.copy(qbuf[:, 0:1], qcar[:, dt:dt + 1])
            V.scan(pbuf[:, 1:CH + 1], ddfull[:, dt, :], avd[:, :],
                   pbuf[:, 0:1])
            V.scan(qbuf[:, 1:CH + 1], ddfull[:, dt, :], ad[:, :],
                   qbuf[:, 0:1])
            V.copy(pcar[:, dt:dt + 1], pbuf[:, CH:CH + 1])
            V.copy(qcar[:, dt:dt + 1], qbuf[:, CH:CH + 1])
            V.stt(numb[:, :], avd[:, :], dvec[:, dt, 1:2], pbuf[:, 0:CH],
                  ALU.mult, ALU.add)
            V.stt(denb[:, :], ad[:, :], dvec[:, dt, 1:2], qbuf[:, 0:CH],
                  ALU.mult, ALU.add)
            V.recip(yb[:, :], denb[:, :])
            V.tt(yb[:, :], yb[:, :], numb[:, :], ALU.mult)
            V.tt(zb[:, dt, :], yb[:, :], srd[:, :], ALU.mult)
            m_z = V.mark()
        prev_xk_cons = m_z

        for tt in range(4):
            TE.wait(m_z)
            for dn in range(2):
                pm = p_mm2 if dn == 0 else p_mm3
                for ct in range(NCT):
                    TE.mm(pm[:, :], zb[:, ct, tt * 128:(tt + 1) * 128],
                          wo4[:, ct, dn * 512:(dn + 1) * 512],
                          ct == 0, ct == NCT - 1)
            TE.minc()
            m_wo = TE.mark()
            V.wait(m_wo)
            V.tt(xt34[:, tt, 0:512], xt34[:, tt, 0:512], p_mm2[:, :], ALU.add)
            V.tt(xt34[:, tt, 512:1024], xt34[:, tt, 512:1024], p_mm3[:, :],
                 ALU.add)
            m_x3 = V.mark()
            TE.wait(m_x3)
            Q.wait(m_x3)
            Q.dma(out_ext.ap()[(c * 4 + tt) * 128:(c * 4 + tt + 1) * 128, :],
                  xt34[:, tt, :])
            lngate(c * 4 + tt, xt34[:, tt, :], xwait=m_x3)
        prev_xt_cons = V.mark()
    m_attn_q = Q.dmark()
    m_attn_done = (V.mark(), TE.mark(), S.mark(), G.mark())
    attn_ctx.close()

    G.wait(m_attn_q)
    m_ag_sc2 = G.cc("AllGather", ALU.bypass, [sc_mine.ap().opt()],
                    [sc_all.ap().opt()], s_cc, ccs)
    m_ag_g2 = G.cc("AllGather", ALU.bypass, [g_mine.ap().opt()],
                   [g_all.ap().opt()], s_cc, ccs)

    sel_ctx2 = ExitStack()
    selset2 = (sb("selsc2", [128, 256, NB], ctx=sel_ctx2),
               sb("selmsk2", [128, 256, NB], ctx=sel_ctx2),
               sb("seltie2", [128, 256, NB], ctx=sel_ctx2),
               sb("selmb2", [128, 256, NB], BF16, ctx=sel_ctx2),
               sb("selscr2", [128, 256], ctx=sel_ctx2),
               sb("selscr82", [128, 256, NB], ctx=sel_ctx2))
    msel2 = emit_selection(m_ag_sc2, selset2,
                           m_wfree=(mgl1['gl'], mgl1['lin']))
    m_merge2 = emit_merge(msel2)
    m_owner2 = emit_owner_lists(msel2, m_merge2)
    sel_ctx2.close()

    ffn_ctx2 = ExitStack()
    fset2 = (sb("fw1b", [128, NCT, NHT, 128], BF16, ctx=ffn_ctx2),
             sb("fw3b", [128, NCT, NHT, 128], BF16, ctx=ffn_ctx2),
             sb("fw2db2", [128, NHT, 1024], BF16, ctx=ffn_ctx2),
             sb("rt2", [128, NCT, 1024], BF16, ctx=ffn_ctx2),
             sb("hbuf2", [128, NHT, 512], BF16, ctx=ffn_ctx2),
             sb("sil2", [128, 2, 512], BF16, ctx=ffn_ctx2),
             sb("outt2", [128, 4, 1024], BF16, ctx=ffn_ctx2),
             sb("scrow2", [1, CAP], BF16, ctx=ffn_ctx2),
             sb("scband2", [128, CAP], BF16, ctx=ffn_ctx2))
    mgl2 = emit_gl(m_merge2, fset2[7], fset2[8])
    mrg2 = dict(gl=mgl2["gl"], scband=mgl2["scband"])
    m_ag_eo2 = emit_ffn(mrg2, m_ag_g2, fset2, m_ag_eo1)
    ffn_ctx2.close()
    cmb_ctx2 = ExitStack()
    cset2 = (sb("cmb_b2", [128, 3, 1024], BF16, ctx=cmb_ctx2),
             sb("cmb_f2", [128, 3, 1024], ctx=cmb_ctx2))
    m_cmb2 = emit_combine(m_ag_eo2, m_owner2, out_ext.ap(), m_attn_q, cset2)
    cmb_ctx2.close()

    G.wait(m_cmb2)
    G.wait(Q.dmark())
    G.wait(V.mark())
    G.wait(S.mark())
    G.wait(TE.mark())

    globals()["_RECS"] = dict(Q=Q, V=V, S=S, TE=TE, G=G)
    with top, nc.Block() as block:
        @block.sync
        def _(e):
            Q.replay(e)

        @block.vector
        def _(e):
            V.replay(e)

        @block.scalar
        def _(e):
            S.replay(e)

        @block.tensor
        def _(e):
            TE.replay(e)

        @block.gpsimd
        def _(e):
            G.replay(e)

    return nc


_NC_CACHE = {}


def kernel(L0_attn_x, L0_attn_time_decay, L0_attn_time_first,
           L0_attn_time_mix_k, L0_attn_time_mix_v, L0_attn_time_mix_r,
           L0_attn_Wk, L0_attn_Wv, L0_attn_Wr, L0_attn_Wo,
           L0_ffn_wg, L0_ffn_w1, L0_ffn_w3, L0_ffn_w2):
    x = np.ascontiguousarray(np.asarray(L0_attn_x, dtype=np.float32))
    td = np.asarray(L0_attn_time_decay, np.float32).reshape(C)
    tf = np.asarray(L0_attn_time_first, np.float32).reshape(C)
    tmk = np.asarray(L0_attn_time_mix_k, np.float32).reshape(C)
    tmv = np.asarray(L0_attn_time_mix_v, np.float32).reshape(C)
    tmr = np.asarray(L0_attn_time_mix_r, np.float32).reshape(C)

    def chmaj(v):
        return v.reshape(NCT, 128).T

    chvec = np.ascontiguousarray(
        np.stack([chmaj(td), chmaj(tf), chmaj(tmk), chmaj(tmv), chmaj(tmr)],
                 axis=2).astype(np.float32))

    wkT = np.ascontiguousarray(np.asarray(L0_attn_Wk, np.float32).T)
    wvT = np.ascontiguousarray(np.asarray(L0_attn_Wv, np.float32).T)
    wrT = np.ascontiguousarray(np.asarray(L0_attn_Wr, np.float32).T)
    woT = np.ascontiguousarray(np.asarray(L0_attn_Wo, np.float32).T)
    wg = np.ascontiguousarray(np.asarray(L0_ffn_wg, np.float32))
    w1 = np.asarray(L0_ffn_w1, np.float32)
    w3 = np.asarray(L0_ffn_w3, np.float32)
    w2 = np.asarray(L0_ffn_w2, np.float32)

    identbf = np.eye(128, dtype=ml_dtypes.bfloat16)
    p = np.arange(128)
    g16c = (p[:, None] // 16 == np.arange(NB)[None, :]).astype(np.float32)
    h16c = (p[:, None] % 16 == np.arange(16)[None, :]).astype(ml_dtypes.bfloat16)
    tric = (np.arange(NB)[:, None] < np.arange(NB)[None, :]).astype(np.float32)
    onesrow = np.ones((1, 128), np.float32)
    ones128c = np.ones((128, 1), np.float32)
    tidbase = (4096.0 * (p // 16) + (p % 16)).astype(np.float32).reshape(128, 1)
    selu = np.zeros((NB, 128), np.float32)
    for u in range(NB):
        selu[u, 16 * u:16 * u + 16] = 1.0

    in_maps = []
    for r in range(NB):
        rkvc = np.zeros((1, 16), np.float32)
        rkvc[0, r] = 1.0
        rkvc[0, 8] = r
        rkvc[0, 9] = 4096.0 * r
        ohcolc = np.zeros((NB, 1), np.float32)
        ohcolc[r, 0] = 1.0
        in_maps.append({
            "x": np.ascontiguousarray(x[r]),
            "wkT": wkT, "wvT": wvT, "wrT": wrT, "woT": woT,
            "w1": np.ascontiguousarray(w1[r]),
            "w3": np.ascontiguousarray(w3[r]),
            "w2": np.ascontiguousarray(w2[r]),
            "wg": wg,
            "chvec": chvec,
            "rkv": rkvc,
            "ohcol": ohcolc,
            "identbf": identbf,
            "g16": g16c,
            "h16": h16c,
            "triexcl": tric,
            "onesrow": onesrow,
            "ones128": ones128c,
            "tidbase": tidbase,
            "selu16": selu,
            "identf8": np.eye(NB, dtype=np.float32),
        })

    if "prog" not in _NC_CACHE:
        nc = build_program()
        nc.finalize()
        _NC_CACHE["prog"] = nc
    nc = _NC_CACHE["prog"]
    res = run_bass_kernel_spmd(nc, in_maps, core_ids=list(range(NB)))
    out = np.stack([np.asarray(res.results[r]["out"], np.float32)
                    for r in range(NB)], axis=0)
    return out


# version 06
# speedup vs baseline: 1.0538x; 1.0538x over previous
import numpy as np
import ml_dtypes
from contextlib import ExitStack

import concourse.bacc as bacc
import concourse.mybir as mybir
from concourse.bass import IndirectOffsetOnAxis
from concourse.bass_utils import run_bass_kernel_spmd
from concourse import library_config as LC

F32 = mybir.dt.float32
BF16 = mybir.dt.bfloat16
I16 = mybir.dt.int16
I32 = mybir.dt.int32
U32 = mybir.dt.uint32
ALU = mybir.AluOpType
ACT = mybir.ActivationFunctionType
AXX = mybir.AxisListType

NB = 8
T = 4096
NT = NB * T
C = 1024
NCT = 8
HID = 2048
NHT = 16
CAP = 4096
EPS = 1e-5
VAL_ITERS = 45
TIE_ITERS = 16
WPAD = 304
CH = 512
NCH = T // CH


class Rec:
    def __init__(self, name, sem, dsem):
        self.name = name
        self.sem = sem
        self.dsem = dsem
        self.q = []
        self.n = 0
        self.nd = 0
        self.last_out = set()
        self.recent = []
        self.waited = {}
        self.ev = []

    def wait(self, mark):
        if mark is None:
            return
        sem, count = mark
        if count <= 0:
            return
        key = id(sem)
        if self.waited.get(key, -1) >= count:
            return
        self.waited[key] = count
        self.ev.append(("w", id(sem), count, len(self.q)))
        self.q.append(lambda e: e.wait_ge(sem, count))

    def mark(self):
        return (self.sem, self.n)

    def dmark(self):
        return (self.dsem, self.nd)

    @staticmethod
    def _names(aps):
        out = set()
        for ap in aps:
            if ap is None or isinstance(ap, (int, float)):
                continue
            t = getattr(ap, "tensor", None)
            if t is not None:
                out.add(t.name)
        return out

    def _hz(self, ins, outs):
        if self.name not in ("vector", "scalar", "gpsimd"):
            return
        rd = self._names(ins)
        wr = self._names(outs)
        rws = set()
        rds = set()
        for (r, w) in self.recent:
            rws |= w
            rds |= r
        if (rd & rws) or (wr & rws) or (wr & rds):
            if self.name == "vector":
                self.q.append(lambda e: e.drain())
            else:
                sem, n = self.sem, self.n
                self.ev.append(("w", id(sem), n, len(self.q)))
                self.q.append(lambda e: e.wait_ge(sem, n))
            self.recent = []
        self.recent = (self.recent + [(rd, wr)])[-3:]

    def drain(self):
        self.q.append(lambda e: e.drain())
        self.recent = []

    def _inc(self, call):
        sem = self.sem
        self.ev.append(("i", id(sem), 1, len(self.q)))
        self.q.append(lambda e: call(e).then_inc(sem, 1))
        self.n += 1

    def ts(self, out, in0, s1, s2=None, op0=ALU.mult, op1=None, accum=None):
        self._hz([in0, s1, s2], [out, accum])
        kw = dict(out=out, in0=in0, scalar1=s1, scalar2=s2, op0=op0)
        if op1 is not None:
            kw["op1"] = op1
        if accum is not None:
            kw["accum_out"] = accum
        self._inc(lambda e: e.tensor_scalar(**kw))

    def tt(self, out, in0, in1, op):
        self._hz([in0, in1], [out])
        self._inc(lambda e: e.tensor_tensor(out=out, in0=in0, in1=in1, op=op))

    def stt(self, out, in0, s, in1, op0, op1, accum=None):
        self._hz([in0, in1, s], [out, accum])
        kw = dict(out=out, in0=in0, scalar=s, in1=in1, op0=op0, op1=op1)
        if accum is not None:
            kw["accum_out"] = accum
        self._inc(lambda e: e.scalar_tensor_tensor(**kw))

    def copy(self, out, in_):
        self._hz([in_], [out])
        self._inc(lambda e: e.tensor_copy(out=out, in_=in_))

    def memset(self, ap, val):
        self._hz([], [ap])
        self._inc(lambda e: e.memset(ap, val))

    def recip(self, out, in_):
        self._hz([in_], [out])
        self._inc(lambda e: e.reciprocal(out=out, in_=in_))

    def reduce(self, out, in_, op, axis=AXX.X):
        self._hz([in_], [out])
        self._inc(lambda e: e.tensor_reduce(out=out, in_=in_, axis=axis, op=op))

    def scan(self, out, d0, d1, initial):
        self._hz([d0, d1, initial], [out])
        self._inc(lambda e: e.tensor_tensor_scan(
            out=out, data0=d0, data1=d1, initial=initial,
            op0=ALU.mult, op1=ALU.add))

    def act(self, out, in_, func, bias=None, scale=None, accum=None):
        self._hz([in_, bias, scale], [out, accum])
        kw = dict(out=out, in_=in_, func=func)
        if bias is not None:
            kw["bias"] = bias
        if scale is not None:
            kw["scale"] = scale
        if accum is not None:
            kw["accum_out"] = accum
        self._inc(lambda e: e.activation(**kw))

    def nop(self):
        self.q.append(lambda e: e.nop())

    def gdiv(self, out, in0, in1):
        self._inc(lambda e: e.tensor_tensor(out=out, in0=in0, in1=in1,
                                            op=ALU.divide))

    def mm(self, out, lhsT, rhs, start, stop):
        ent = {"k": "mm", "out": out, "lhsT": lhsT, "rhs": rhs,
               "start": start, "stop": stop, "inc": False, "sem": self.sem}
        self.q.append(ent)

    def trans(self, out, in_, ident):
        ent = {"k": "tr", "out": out, "in_": in_, "ident": ident,
               "inc": False, "sem": self.sem}
        self.q.append(ent)

    def minc(self):
        assert isinstance(self.q[-1], dict)
        self.q[-1]["inc"] = True
        self.ev.append(("i", id(self.sem), 1, len(self.q) - 1))
        self.n += 1

    def dma(self, out, in_, **kw):
        dsem = self.dsem
        self.wait(self.dmark())
        self.ev.append(("i", id(dsem), 16, len(self.q)))
        self.q.append(lambda e: e.dma_start(out=out, in_=in_, **kw)
                      .then_inc(dsem, 16))
        self.nd += 16
        if self.name in ("vector", "scalar", "gpsimd"):
            self.recent = (self.recent + [(set(), self._names([out]))])[-3:]
        return self.dmark()

    def sg(self, out, in_, nf):
        self._hz([in_], [out, nf])
        self._inc(lambda e: e.sparse_gather(out=out, in_=in_, num_found=nf))

    def lib(self, lib):
        self.q.append(lambda e: e.load_library(lib))

    def pbcast(self, out, in_, channels):
        self._hz([in_], [out])
        self._inc(lambda e: e.partition_broadcast(out_ap=out, in_ap=in_,
                                                  channels=channels))

    def gather(self, out, tab, idxs, n, elem, transpose=False, count_src=None):
        dsem = self.dsem
        self.wait(self.dmark())
        self.ev.append(("i", id(dsem), 16, len(self.q)))

        def emit(e):
            nreg = n if count_src is None else e.value_load(
                count_src, min_val=0, max_val=n)
            e.dma_gather(
                out_ap=out, in_ap=tab, idxs_ap=idxs, num_idxs=n,
                num_idxs_reg=nreg, elem_size=elem, transpose=transpose,
                single_packet=False).then_inc(dsem, 16)
        self.q.append(emit)
        self.nd += 16
        return self.dmark()

    def scatter_add(self, out, in_, idxs, n, elem, count_src=None):
        dsem = self.dsem
        self.wait(self.dmark())
        self.ev.append(("i", id(dsem), 16, len(self.q)))

        def emit(e):
            nreg = n if count_src is None else e.value_load(
                count_src, min_val=0, max_val=n)
            e.dma_scatter_add(
                out_ap=out, in_ap=in_, idxs_ap=idxs, num_idxs=n,
                num_idxs_reg=nreg, elem_size=elem,
                single_packet=False).then_inc(dsem, 16)
        self.q.append(emit)
        self.nd += 16
        return self.dmark()

    def imerge(self, out_ap, idx_ap, in_ap):
        dsem = self.dsem
        self.wait(self.dmark())
        self.ev.append(("i", id(dsem), 16, len(self.q)))
        self.q.append(lambda e: e.indirect_dma_start(
            out=out_ap, out_offset=IndirectOffsetOnAxis(ap=idx_ap, axis=1),
            in_=in_ap, in_offset=None,
            compute_op=ALU.add).then_inc(dsem, 16))
        self.nd += 16
        return self.dmark()

    def iota(self, out, pattern, base, cm):
        self._hz([], [out])
        self._inc(lambda e: e.iota(out=out, pattern=pattern, base=base,
                                   channel_multiplier=cm,
                                   allow_small_or_imprecise_dtypes=True))

    def cc(self, kind, op, ins, outs, ccsem, state):
        state[0] += 1
        count = state[0]
        self.ev.append(("i", id(ccsem), 1, len(self.q)))
        self.q.append(lambda e: e.collective_compute(
            kind, op, replica_groups=[list(range(NB))], ins=ins, outs=outs)
            .then_inc(ccsem, 1))
        return (ccsem, count)

    def replay(self, raw):
        for fn in self.q:
            if isinstance(fn, dict):
                if fn["k"] == "mm":
                    r = raw.matmul(out=fn["out"], lhsT=fn["lhsT"],
                                   rhs=fn["rhs"], start=fn["start"],
                                   stop=fn["stop"])
                else:
                    r = raw.transpose(out=fn["out"], in_=fn["in_"],
                                      identity=fn["ident"])
                if fn["inc"]:
                    r.then_inc(fn["sem"], 1)
            else:
                fn(raw)


def build_program():
    nc = bacc.Bacc("TRN2", target_bir_lowering=False)

    def ein(name, shape, dt=F32):
        return nc.dram_tensor(name, shape, dt, kind="ExternalInput")

    x_in = ein("x", [T, C])
    wkT_in = ein("wkT", [C, C]); wvT_in = ein("wvT", [C, C])
    wrT_in = ein("wrT", [C, C]); woT_in = ein("woT", [C, C])
    w1_in = ein("w1", [C, HID]); w3_in = ein("w3", [C, HID])
    w2_in = ein("w2", [HID, C])
    wg_in = ein("wg", [C, NB])
    chvec_in = ein("chvec", [128, NCT, 5])
    rkv_in = ein("rkv", [1, 16])
    ohcol_in = ein("ohcol", [NB, 1])
    ident_in = ein("identbf", [128, 128], BF16)
    g16_in = ein("g16", [128, NB])
    h16_in = ein("h16", [128, 16], BF16)
    tri_in = ein("triexcl", [NB, NB])
    onesr_in = ein("onesrow", [1, 128])
    ones128_in = ein("ones128", [128, 1])
    tidb_in = ein("tidbase", [128, 1])
    selu_in = ein("selu16", [NB, 128])
    identf8_in = ein("identf8", [NB, NB])
    out_ext = nc.dram_tensor("out", [T, C], F32, kind="ExternalOutput")

    def dram(name, shape, dt):
        return nc.dram_tensor(name, shape, dt)

    g_mine = dram("g_mine", [T, C], BF16)
    g_all = nc.dram_tensor("g_all", [NT, C], BF16, addr_space="Shared")
    sc_mine = dram("sc_mine", [16, 256 * NB], F32)
    sc_all = nc.dram_tensor("sc_all", [128, 256 * NB], F32, addr_space="Shared")
    eo_mine = dram("eo_mine", [CAP, C], BF16)
    eo_all = nc.dram_tensor("eo_all", [NB * CAP, C], BF16, addr_space="Shared")
    x2_plane = dram("x2_plane", [T, C], F32)
    w_ids = dram("w_ids", [16, WPAD], I32)
    w_sc = dram("w_sc", [16, WPAD], F32)
    lin_sc = dram("lin_sc", [1, CAP], F32)

    top = ExitStack()

    def sb(name, shape, dt=F32, ctx=None):
        return (ctx or top).enter_context(nc.sbuf_tensor(name, shape, dt))

    def mkpsum(name, shape, dt=F32):
        return top.enter_context(nc.psum_tensor(name, shape, dt))

    def mksem(name):
        return top.enter_context(nc.semaphore(name))

    ident = sb("ident_s", [128, 128], BF16)
    chv = sb("chv", [128, NCT, 8])
    dvec = sb("dvec", [128, NCT, 2])
    wgt = sb("wgt", [128, NCT, NB], BF16)
    rkv = sb("rkv_s", [1, 16])
    ohcol = sb("ohcol_s", [NB, 1])
    g16 = sb("g16_s", [128, NB])
    h16 = sb("h16_s", [128, 16], BF16)
    tri = sb("tri_s", [NB, NB])
    onesr = sb("onesr_s", [1, 128])
    ones128 = sb("ones128_s", [128, 1])
    oh128 = sb("oh128", [128, NB])
    w128 = sb("w128", [128, 1])
    tidb = sb("tidb_s", [128, 1])
    selu16 = sb("selu16_s", [NB, 128])
    identf8 = sb("identf8_s", [NB, NB])
    iota256 = sb("iota256", [128, 256])
    iota48w = sb("iota48w", [16, 48])
    iotap16 = sb("iotap16", [16, 1])
    iotaloc = sb("iotaloc", [16, 256])
    idxcol = sb("idxcol", [128, 1], U32)
    zero16 = sb("zero16", [16, WPAD])
    cid = sb("cid", [128, 256])
    csc = sb("csc", [128, 256])
    tid = sb("tid", [128, 256])
    bnd = sb("bnd", [1, 64])
    cnt = sb("cnt", [128, NB])
    cstat = sb("cstat", [NB, NB])
    starts = sb("starts", [NB, NB])
    sm8 = sb("sm8", [NB, 1])
    scr8 = sb("scr8", [NB, NB])
    myrow = sb("myrow", [1, 2 * NB])
    owcnt_f = sb("owcnt_f", [1, 2 * NB])
    owcnt = sb("owcnt", [1, 2 * NB], I32)
    lm16 = sb("lm16", [16, 256, NB])
    sg_in = sb("sg_in", [16, 256])
    sg_in2 = sb("sg_in2", [16, 256])
    sg_out = sb("sg_out", [16, 48], I32)
    sg_outf = sb("sg_outf", [16, 48])
    sg_nf = sb("sg_nf", [1, 1], U32)
    nfb = sb("nfb", [1, 1])
    m48 = sb("m48", [16, 48])
    sg_ids = sb("sg_ids", [16, 48])
    sg_scv = sb("sg_scv", [16, 48])
    mcol = sb("mcol", [16, 4])
    gl32 = sb("gl32", [16, 256], I32)
    glf = sb("glf", [16, 256])
    gl_idx = sb("gl_idx", [128, 256], I16)
    ow_g = sb("ow_g", [128, NB, 48], I16)
    ow_s = sb("ow_s", [128, NB, 48], I16)
    lsl = sb("lsl", [16, 256])
    stats = sb("stats", [128, 16])
    sq_scr = sb("sq_scr", [128, 1024])
    sq_scr2 = sb("sq_scr2", [128, 1024])
    gb = sb("gb", [128, 2, 1024], BF16)
    gcm = sb("gcm", [128, 2, NCT, 128], BF16)
    scem = sb("scem", [NB, 128])
    sct = sb("sct", [128, NB])
    p_ex = mkpsum("p_ex", [128, 32])
    p_tr = mkpsum("p_tr", [128, 512], BF16)
    p_tr2 = mkpsum("p_tr2", [128, 512], BF16)
    p_mm0 = mkpsum("p_mm0", [128, 512])
    p_mm1 = mkpsum("p_mm1", [128, 512])
    p_mm2 = mkpsum("p_mm2", [128, 512])
    p_mm3 = mkpsum("p_mm3", [128, 512])
    p_sc = mkpsum("p_sc", [128, 128])

    s_q = mksem("s_q"); s_qd = mksem("s_qd")
    s_v = mksem("s_v"); s_vd = mksem("s_vd")
    s_s = mksem("s_s"); s_sd = mksem("s_sd")
    s_t = mksem("s_t"); s_td = mksem("s_td")
    s_g = mksem("s_g"); s_gd = mksem("s_gd")
    s_cc = mksem("s_cc")

    Q = Rec("sync", s_q, s_qd)
    V = Rec("vector", s_v, s_vd)
    S = Rec("scalar", s_s, s_sd)
    TE = Rec("tensor", s_t, s_td)
    G = Rec("gpsimd", s_g, s_gd)
    ccs = [0]

    Q.dma(x2_plane.ap(), x_in.ap())
    for dst, src in [(ident[:, :], ident_in.ap()),
                     (chv[:, :, 0:5], chvec_in.ap()),
                     (rkv[:, :], rkv_in.ap()),
                     (ohcol[:, :], ohcol_in.ap()),
                     (g16[:, :], g16_in.ap()),
                     (h16[:, :], h16_in.ap()),
                     (tri[:, :], tri_in.ap()),
                     (onesr[:, :], onesr_in.ap()),
                     (ones128[:, :], ones128_in.ap()),
                     (tidb[:, :], tidb_in.ap()),
                     (selu16[:, :], selu_in.ap()),
                     (identf8[:, :], identf8_in.ap())]:
        G.dma(dst, src)
    G.dma(wgt[:, :, :], wg_in.ap().rearrange("(ct p) e -> p ct e", p=128))
    m_consts = G.dmark()
    G.iota(iota256[:, :], [[16, 256]], 0, 0)
    G.iota(iota48w[:, :], [[16, 48]], 0, 1)
    G.iota(iotap16[:, :], [[0, 1]], 0, 1)
    G.iota(iotaloc[:, :], [[16, 256]], 0, 1)

    V.wait(m_consts)
    V.memset(zero16[:, :], 0.0)
    V.ts(chv[:, :, 5:8], chv[:, :, 2:5], -1.0, 1.0, ALU.mult, ALU.add)

    S.wait(m_consts)
    S.act(stats[:, 0:8], chv[:, :, 0:1].rearrange("p a b -> p (a b)"), ACT.Exp)
    S.nop()
    S.act(dvec[:, :, 0:1].rearrange("p a b -> p (a b)"), stats[:, 0:8],
          ACT.Exp, scale=-1.0)
    S.act(dvec[:, :, 1:2].rearrange("p a b -> p (a b)"),
          chv[:, :, 1:2].rearrange("p a b -> p (a b)"), ACT.Exp)
    m_dvec = S.mark()

    TE.wait(m_consts)
    TE.mm(p_ex[:, 0:8], onesr[:, :], rkv[0:1, 0:8], True, True)
    TE.minc()
    S.wait(TE.mark())
    S.act(oh128[:, :], p_ex[:, 0:8], ACT.Copy)
    m_oh128 = S.mark()
    V.wait(m_oh128)
    V.tt(cnt[:, :], g16[:, :], oh128[:, :], ALU.mult)
    V.reduce(w128[:, :], cnt[:, :], ALU.add)
    m_w128 = V.mark()

    lg_prev = {"tr": None, "gstore": None, "mm": None, "sct_q": None}

    def lngate(i, xin_ap, xwait=None):
        if xwait is not None:
            V.wait(xwait)
            S.wait(xwait)
        V.ts(sq_scr[:, :], xin_ap, 1.0, None, ALU.mult, ALU.add,
             accum=stats[:, 0:1])
        S.act(sq_scr2[:, :], xin_ap, ACT.Square, accum=stats[:, 1:2])
        m_sq = S.mark()
        V.wait(m_sq)
        V.drain()
        V.ts(stats[:, 2:3], stats[:, 0:1], -1.0 / C)
        V.tt(stats[:, 4:5], stats[:, 2:3], stats[:, 2:3], ALU.mult)
        V.ts(stats[:, 6:7], stats[:, 1:2], 1.0 / C, EPS, ALU.mult, ALU.add)
        V.tt(stats[:, 3:4], stats[:, 6:7], stats[:, 4:5], ALU.subtract)
        V.recip(stats[:, 7:8], stats[:, 3:4])
        m_rc = V.mark()
        S.wait(m_rc)
        S.act(stats[:, 8:9], stats[:, 7:8], ACT.Sqrt)
        S.nop()
        S.act(stats[:, 9:10], stats[:, 2:3], ACT.Identity, scale=stats[:, 8:9])
        m_bias = S.mark()
        V.wait(m_bias)
        V.wait(lg_prev["tr"])
        V.wait(lg_prev["gstore"])
        par = i % 2
        V.ts(gb[:, par, :], xin_ap, stats[:, 8:9], stats[:, 9:10],
             ALU.mult, ALU.add)
        m_g = V.mark()
        Q.wait(m_g)
        gst = Q.dma(g_mine.ap()[i * 128:(i + 1) * 128, :], gb[:, par, :])
        TE.wait(m_g)
        TE.wait(lg_prev["mm"])
        for h in range(2):
            pt = p_tr if h == 0 else p_tr2
            for j in range(4):
                ct = h * 4 + j
                TE.trans(pt[:, j * 128:(j + 1) * 128],
                         gb[:, par, ct * 128:(ct + 1) * 128], ident[:, :])
        TE.minc()
        m_tr = TE.mark()
        S.wait(m_tr)
        S.act(gcm[:, par, 0:4, :].rearrange("p a b -> p (a b)"), p_tr[:, :],
              ACT.Copy)
        S.act(gcm[:, par, 4:8, :].rearrange("p a b -> p (a b)"), p_tr2[:, :],
              ACT.Copy)
        m_gcm = S.mark()
        TE.wait(m_gcm)
        TE.wait(lg_prev["sct_q"])
        for ct in range(NCT):
            TE.mm(p_sc[0:NB, :], wgt[:, ct, :], gcm[:, par, ct, :],
                  ct == 0, ct == NCT - 1)
        TE.minc()
        m_mm = TE.mark()
        S.wait(m_mm)
        S.act(scem[:, :], p_sc[0:NB, :], ACT.Copy)
        m_scem = S.mark()
        TE.wait(m_scem)
        TE.trans(p_ex[:, 8:16], scem[:, :], identf8[:, :])
        TE.minc()
        m_sct = TE.mark()
        S.wait(m_sct)
        S.act(sct[:, :], p_ex[:, 8:16], ACT.Exp)
        m_exp = S.mark()
        V.wait(m_exp)
        V.reduce(stats[:, 10:11], sct[:, :], ALU.add)
        V.recip(stats[:, 11:12], stats[:, 10:11])
        V.ts(sct[:, :], sct[:, :], stats[:, 11:12])
        m_sm = V.mark()
        Q.wait(m_sm)
        scq = Q.dma(sc_mine.ap().rearrange("q (f e) -> q f e", e=NB)
                    [:, 8 * i:8 * i + 8, :].rearrange("q w e -> w q e"),
                    sct[:, :])
        lg_prev["tr"] = m_tr
        lg_prev["gstore"] = gst
        lg_prev["mm"] = m_mm
        lg_prev["sct_q"] = scq

    p1ctx = ExitStack()
    xt2 = sb("xt2", [128, 2, 1024], ctx=p1ctx)
    ldm = [None] * 34
    ldm[0] = Q.dma(xt2[:, 0, :], x_in.ap()[0:128, :])
    ldm[1] = Q.dma(xt2[:, 1, :], x_in.ap()[128:256, :])
    for i in range(32):
        lngate(i, xt2[:, i % 2, :], xwait=ldm[i])
        if i + 2 < 32:
            Q.wait(V.mark())
            ldm[i + 2] = Q.dma(xt2[:, i % 2, :],
                               x_in.ap()[(i + 2) * 128:(i + 3) * 128, :])
    p1ctx.close()
    m_lg1_q = Q.dmark()

    G.wait(m_lg1_q)
    m_ag_sc1 = G.cc("AllGather", ALU.bypass, [sc_mine.ap().opt()],
                    [sc_all.ap().opt()], s_cc, ccs)
    m_ag_g1 = G.cc("AllGather", ALU.bypass, [g_mine.ap().opt()],
                   [g_all.ap().opt()], s_cc, ccs)

    def emit_selection(m_ag_sc, selset, m_wfree=()):
        selsc, selmsk, seltie, selmb, selscr, selscr8 = selset
        Q.wait(m_ag_sc)
        Q.dma(selsc[:, :, :],
              sc_all.ap().rearrange("p (f e) -> p f e", e=NB))
        m_sel_ld = Q.dmark()
        for mk in m_wfree:
            G.wait(mk)
        G.dma(w_ids.ap(), zero16[:, :])
        G.dma(w_sc.ap(), zero16[:, :])

        V.wait(m_sel_ld)
        V.ts(tid[:, :], iota256[:, :], tidb[:, 0:1], None, ALU.add)
        V.memset(bnd[:, 0:8], 0.0)
        V.memset(bnd[:, 8:16], 1.0)

        for it in range(VAL_ITERS):
            V.tt(bnd[:, 16:24], bnd[:, 0:8], bnd[:, 8:16], ALU.add)
            V.ts(bnd[:, 16:24], bnd[:, 16:24], 0.5)
            TE.wait(V.mark())
            TE.mm(p_ex[:, 16:24], onesr[:, :], bnd[0:1, 16:24], True, True)
            TE.minc()
            V.wait(TE.mark())
            for e in range(NB):
                V.ts(selscr[:, :], selsc[:, :, e], p_ex[:, 16 + e:17 + e],
                     None, ALU.is_gt, ALU.add, accum=cnt[:, e:e + 1])
            TE.wait(V.mark())
            TE.mm(p_ex[0:1, 24:32], ones128[:, :], cnt[:, :], True, True)
            TE.minc()
            V.wait(TE.mark())
            V.ts(bnd[:, 24:32], p_ex[0:1, 24:32], float(CAP), None, ALU.is_ge)
            V.tt(bnd[:, 32:40], bnd[:, 16:24], bnd[:, 0:8], ALU.subtract)
            V.tt(bnd[:, 40:48], bnd[:, 24:32], bnd[:, 32:40], ALU.mult)
            V.tt(bnd[:, 0:8], bnd[:, 0:8], bnd[:, 40:48], ALU.add)
            V.tt(bnd[:, 32:40], bnd[:, 8:16], bnd[:, 16:24], ALU.subtract)
            V.tt(bnd[:, 40:48], bnd[:, 24:32], bnd[:, 32:40], ALU.mult)
            V.tt(bnd[:, 8:16], bnd[:, 16:24], bnd[:, 40:48], ALU.add)

        TE.wait(V.mark())
        TE.mm(p_ex[:, 16:24], onesr[:, :], bnd[0:1, 8:16], True, True)
        TE.mm(p_ex[:, 24:32], onesr[:, :], bnd[0:1, 0:8], True, True)
        TE.minc()
        V.wait(TE.mark())
        for e in range(NB):
            V.ts(selmsk[:, :, e], selsc[:, :, e], p_ex[:, 16 + e:17 + e],
                 None, ALU.is_gt, ALU.add, accum=cnt[:, e:e + 1])
        for e in range(NB):
            V.ts(seltie[:, :, e], selsc[:, :, e], p_ex[:, 24 + e:25 + e],
                 None, ALU.is_gt)
        for e in range(NB):
            V.tt(seltie[:, :, e], seltie[:, :, e], selmsk[:, :, e],
                 ALU.subtract)
        TE.wait(V.mark())
        TE.mm(p_ex[0:1, 24:32], ones128[:, :], cnt[:, :], True, True)
        TE.minc()
        V.wait(TE.mark())
        V.ts(bnd[:, 48:56], p_ex[0:1, 24:32], -1.0, float(CAP),
             ALU.mult, ALU.add)
        V.memset(bnd[:, 0:8], 0.0)
        V.memset(bnd[:, 8:16], float(NT))

        for it in range(TIE_ITERS):
            V.tt(bnd[:, 16:24], bnd[:, 0:8], bnd[:, 8:16], ALU.add)
            V.ts(bnd[:, 16:24], bnd[:, 16:24], 0.5)
            TE.wait(V.mark())
            TE.mm(p_ex[:, 16:24], onesr[:, :], bnd[0:1, 16:24], True, True)
            TE.minc()
            V.wait(TE.mark())
            for e in range(NB):
                V.stt(selscr[:, :], tid[:, :], p_ex[:, 16 + e:17 + e],
                      seltie[:, :, e], ALU.is_lt, ALU.mult,
                      accum=cnt[:, e:e + 1])
            TE.wait(V.mark())
            TE.mm(p_ex[0:1, 24:32], ones128[:, :], cnt[:, :], True, True)
            TE.minc()
            V.wait(TE.mark())
            V.tt(bnd[:, 24:32], p_ex[0:1, 24:32], bnd[:, 48:56], ALU.is_ge)
            V.tt(bnd[:, 32:40], bnd[:, 16:24], bnd[:, 8:16], ALU.subtract)
            V.tt(bnd[:, 40:48], bnd[:, 24:32], bnd[:, 32:40], ALU.mult)
            V.tt(bnd[:, 8:16], bnd[:, 8:16], bnd[:, 40:48], ALU.add)
            V.tt(bnd[:, 32:40], bnd[:, 0:8], bnd[:, 16:24], ALU.subtract)
            V.tt(bnd[:, 40:48], bnd[:, 24:32], bnd[:, 32:40], ALU.mult)
            V.tt(bnd[:, 0:8], bnd[:, 16:24], bnd[:, 40:48], ALU.add)

        TE.wait(V.mark())
        TE.mm(p_ex[:, 16:24], onesr[:, :], bnd[0:1, 8:16], True, True)
        TE.minc()
        V.wait(TE.mark())
        for e in range(NB):
            V.stt(selscr[:, :], tid[:, :], p_ex[:, 16 + e:17 + e],
                  seltie[:, :, e], ALU.is_lt, ALU.mult)
            V.tt(selmsk[:, :, e], selmsk[:, :, e], selscr[:, :], ALU.add)
        for e in range(NB):
            V.ts(selscr[:, :], selmsk[:, :, e], 1.0, None, ALU.mult, ALU.add,
                 accum=cnt[:, e:e + 1])
        V.wait(m_w128)
        V.tt(selmb[:, :, :], selmsk[:, :, :],
             w128[:, 0:1].rearrange("p (a b) -> p a b", a=1)
             .to_broadcast([128, 256, NB]), ALU.mult)
        m_fmask = V.mark()
        TE.wait(m_fmask)
        TE.mm(p_ex[0:8, 24:32], g16[:, :], cnt[:, :], True, True)
        TE.minc()
        S.wait(TE.mark())
        S.act(cstat[:, :], p_ex[0:8, 24:32], ACT.Copy)
        TE.wait(S.mark())
        TE.mm(p_ex[0:8, 16:24], tri[:, :], cstat[:, :], True, True)
        TE.minc()
        S.wait(TE.mark())
        S.act(starts[:, :], p_ex[0:8, 16:24], ACT.Copy)
        m_starts = S.mark()
        TE.wait(m_starts)
        TE.mm(p_ex[0:1, 16:24], ohcol[:, :], starts[:, :], True, True)
        TE.mm(p_ex[0:1, 24:32], ohcol[:, :], cstat[:, :], True, True)
        TE.minc()
        S.wait(TE.mark())
        S.act(myrow[:, 0:8], p_ex[0:1, 16:24], ACT.Copy)
        S.act(myrow[:, 8:16], p_ex[0:1, 24:32], ACT.Copy)
        m_myrow = S.mark()

        V.wait(m_starts)
        V.tt(scr8[:, :], starts[:, :], oh128[0:8, :], ALU.mult)
        V.reduce(sm8[:, :], scr8[:, :], ALU.add)
        ohb = oh128[:, 0:NB].rearrange("p (a e) -> p a e", a=1) \
            .to_broadcast([128, 256, NB])
        V.tt(selscr8[:, :, :], selmsk[:, :, :], ohb, ALU.mult)
        V.reduce(selscr[:, :], selscr8[:, :, :], ALU.add)
        V.tt(selscr8[:, :, :], selsc[:, :, :], ohb, ALU.mult)
        V.reduce(csc[:, :], selscr8[:, :, :], ALU.add)
        V.ts(cid[:, :], tid[:, :], 1.0, 2.0, ALU.mult, ALU.add)
        V.tt(cid[:, :], cid[:, :], selscr[:, :], ALU.mult)
        V.ts(cid[:, :], cid[:, :], 1.0, -1.0, ALU.mult, ALU.add)
        V.ts(csc[:, :], csc[:, :], 1.0, 1.0, ALU.mult, ALU.add)
        V.tt(csc[:, :], csc[:, :], selscr[:, :], ALU.mult)
        V.ts(csc[:, :], csc[:, :], 1.0, -1.0, ALU.mult, ALU.add)
        m_cands = V.mark()

        TE.wait(m_cands)
        for ncol in range(4):
            if ncol > 0:
                TE.wait(S.mark())
            TE.mm(p_mm0[0:16, :], h16[:, :],
                  selmb[:, :, :].rearrange("p f e -> p (f e)")
                  [:, ncol * 512:(ncol + 1) * 512], True, True)
            TE.minc()
            S.wait(TE.mark())
            S.act(lm16[:, :, :].rearrange("p f e -> p (f e)")
                  [:, ncol * 512:(ncol + 1) * 512], p_mm0[0:16, :], ACT.Copy)
        m_lm16 = S.mark()
        return dict(cands=m_cands, myrow=m_myrow, lm16=m_lm16)

    def emit_merge(msel):
        G.lib(LC.sparse_gather)
        G.wait(msel["cands"])
        m_prev_v = None
        m_prev_mg = None
        for u in range(NB):
            G.wait(m_prev_v)
            G.dma(sg_in[:, :], cid[16 * u:16 * u + 16, :])
            G.dma(sg_in2[:, :], csc[16 * u:16 * u + 16, :])
            m_mv = G.dmark()
            G.wait(m_mv)
            G.sg(sg_out[:, :], sg_in[:, :], sg_nf[:, :])
            m_sg1 = G.mark()
            V.wait(m_sg1)
            V.copy(nfb[:, :], sg_nf[:, :])
            TE.wait(V.mark())
            TE.mm(p_ex[0:16, 16:17], onesr[0:1, 0:16], nfb[0:1, 0:1],
                  True, True)
            TE.mm(p_ex[0:16, 17:18], selu16[:, 16 * u:16 * u + 16],
                  sm8[:, :], True, True)
            TE.minc()
            V.wait(TE.mark())
            V.wait(m_prev_mg)
            V.ts(m48[:, :], iota48w[:, :], p_ex[0:16, 16:17], None, ALU.is_lt)
            V.tt(sg_ids[:, :], sg_out[:, :], m48[:, :], ALU.mult)
            V.ts(mcol[:, 0:1], p_ex[0:16, 17:18], 1.0)
            for kpow in (2048.0, 1024.0, 512.0, 256.0, 128.0, 64.0, 32.0,
                         16.0):
                V.ts(mcol[:, 1:2], mcol[:, 0:1], kpow, None, ALU.is_ge)
                V.stt(mcol[:, 0:1], mcol[:, 1:2], -kpow, mcol[:, 0:1],
                      ALU.mult, ALU.add)
            V.tt(mcol[:, 1:2], p_ex[0:16, 17:18], mcol[:, 0:1], ALU.subtract)
            V.ts(mcol[:, 1:2], mcol[:, 1:2], 1.0 / 16)
            V.tt(mcol[:, 2:3], mcol[:, 0:1], iotap16[:, :], ALU.add)
            V.ts(mcol[:, 3:4], mcol[:, 2:3], 16.0, None, ALU.is_ge)
            V.stt(mcol[:, 2:3], mcol[:, 3:4], -16.0, mcol[:, 2:3],
                  ALU.mult, ALU.add)
            V.tt(mcol[:, 3:4], mcol[:, 1:2], mcol[:, 3:4], ALU.add)
            V.stt(mcol[:, 1:2], mcol[:, 2:3], float(WPAD), mcol[:, 3:4],
                  ALU.mult, ALU.add)
            V.copy(idxcol[0:16, 0:1], mcol[:, 1:2])
            m_idx = V.mark()
            G.wait(m_idx)
            G.imerge(w_ids.ap().rearrange("(a r) c -> a (r c)", a=1),
                     idxcol[0:16, 0:1], sg_ids[:, :])
            G.sg(sg_outf[:, :], sg_in2[:, :], sg_nf[:, :])
            m_sg2 = G.mark()
            V.wait(m_sg2)
            V.tt(sg_scv[:, :], sg_outf[:, :], m48[:, :], ALU.mult)
            m_scv = V.mark()
            G.wait(m_scv)
            G.imerge(w_sc.ap().rearrange("(a r) c -> a (r c)", a=1),
                     idxcol[0:16, 0:1], sg_scv[:, :])
            m_prev_v = V.mark()
            m_prev_mg = G.dmark()
        return G.dmark()

    def emit_owner_lists(msel, m_merge):
        V.wait(msel["lm16"])
        V.wait(msel["myrow"])
        V.wait(m_merge)
        for e in range(NB):
            V.ts(lsl[:, :], iotaloc[:, :], 1.0, 2.0, ALU.mult, ALU.add)
            V.tt(lsl[:, :], lsl[:, :], lm16[:, :, e], ALU.mult)
            V.ts(lsl[:, :], lsl[:, :], 1.0, -1.0, ALU.mult, ALU.add)
            m_c = V.mark()
            G.wait(m_c)
            G.sg(sg_out[:, :], lsl[:, :], sg_nf[:, :])
            m_sg = G.mark()
            V.wait(m_sg)
            V.copy(nfb[:, :], sg_nf[:, :])
            TE.wait(V.mark())
            TE.mm(p_ex[0:16, 16:17], onesr[0:1, 0:16], nfb[0:1, 0:1],
                  True, True)
            TE.mm(p_ex[0:16, 17:18], onesr[0:1, 0:16], myrow[0:1, e:e + 1],
                  True, True)
            TE.minc()
            V.wait(TE.mark())
            V.ts(m48[:, :], iota48w[:, :], p_ex[0:16, 16:17], None, ALU.is_lt)
            V.tt(sg_scv[:, :], sg_out[:, :], m48[:, :], ALU.mult)
            V.ts(sg_scv[:, :], sg_scv[:, :], 1.0, -1.0, ALU.mult, ALU.add)
            V.copy(ow_s[0:16, e, :], sg_scv[:, :])
            V.ts(sg_scv[:, :], iota48w[:, :], p_ex[0:16, 17:18],
                 float(e * CAP) + 1.0, ALU.add, ALU.add)
            V.tt(sg_scv[:, :], sg_scv[:, :], m48[:, :], ALU.mult)
            V.ts(sg_scv[:, :], sg_scv[:, :], 1.0, -1.0, ALU.mult, ALU.add)
            V.copy(ow_g[0:16, e, :], sg_scv[:, :])
        ohf = owcnt_f.rearrange("a (e h) -> a e h", h=2)
        V.ts(ohf[:, :, 0], myrow[0:1, 8:16], 384.0, None, ALU.min)
        V.tt(ohf[:, :, 1], myrow[0:1, 8:16], ohf[:, :, 0], ALU.subtract)
        V.copy(owcnt[:, :], owcnt_f[:, :])
        m_ow0 = V.mark()
        Q.wait(m_ow0)
        Q.dma(ow_s[16:32, :, :], ow_s[0:16, :, :])
        Q.dma(ow_g[16:32, :, :], ow_g[0:16, :, :])
        Q.wait(Q.dmark())
        Q.dma(ow_s[32:64, :, :], ow_s[0:32, :, :])
        Q.dma(ow_g[32:64, :, :], ow_g[0:32, :, :])
        Q.wait(Q.dmark())
        Q.dma(ow_s[64:128, :, :], ow_s[0:64, :, :])
        Q.dma(ow_g[64:128, :, :], ow_g[0:64, :, :])
        return Q.dmark()

    def emit_gl(m_merge, scrow, scband):
        Q.wait(m_merge)
        Q.dma(gl32[:, :], w_ids.ap()[:, 0:256])
        V.wait(Q.dmark())
        V.ts(glf[:, :], gl32[:, :], 1.0, -1.0, ALU.mult, ALU.add)
        V.copy(gl_idx[0:16, :], glf[:, :])
        m_gl0 = V.mark()
        Q.wait(m_gl0)
        Q.dma(gl_idx[16:32, :], gl_idx[0:16, :])
        Q.wait(Q.dmark())
        Q.dma(gl_idx[32:64, :], gl_idx[0:32, :])
        Q.wait(Q.dmark())
        Q.dma(gl_idx[64:128, :], gl_idx[0:64, :])
        m_gl = Q.dmark()
        for rr in range(16):
            Q.dma(lin_sc.ap().rearrange("a (c r) -> (a c) r", r=16)[:, rr:rr + 1],
                  w_sc.ap()[rr:rr + 1, 0:256].rearrange("r c -> c r"),
                  allow_slow_non_contiguous=True)
        m_lin = Q.dmark()
        ret_lin = m_lin
        G.lib(LC.mlp)
        G.wait(m_lin)
        m_scrow = G.dma(scrow[:, :], lin_sc.ap())
        G.wait(m_scrow)
        G.pbcast(scband[:, :], scrow[:, :], 128)
        return dict(gl=m_gl, scband=G.mark(), lin=ret_lin)

    def emit_ffn(mrg, m_ag_g, fset, m_eo_free):
        fw1, fw3, fw2db, rt, hbuf, sil, outt, scrow, scband = fset
        mw = G.dmark()
        G.dma(fw1[:, :, :, :],
              w1_in.ap().rearrange("(ct p) (ht j) -> p ct ht j", p=128, j=128))
        G.dma(fw3[:, :, :, :],
              w3_in.ap().rearrange("(ct p) (ht j) -> p ct ht j", p=128, j=128))
        m_w13 = G.dmark()
        G.wait(mrg["gl"])
        G.wait(m_ag_g)
        G.dma(fw2db[:, :, :], w2_in.ap().rearrange("(ht p) d -> p ht d", p=128))
        m_w2 = G.dmark()
        gmk = [None] * 4
        rt_free = None
        m_eo_prev = None
        for k in range(4):
            G.wait(rt_free)
            gmk[k] = G.gather(rt[:, :, :],
                              g_all.ap(), gl_idx[:, k * 64:(k + 1) * 64],
                              1024, C, transpose=True)
            V.wait(gmk[k])
            V.wait(mrg["scband"])
            V.tt(rt[:, :, :], rt[:, :, :],
                 scband[:, k * 1024:(k + 1) * 1024]
                 .rearrange("p (a b) -> p a b", a=1)
                 .to_broadcast([128, NCT, 1024]), ALU.mult)
            m_rt = V.mark()

            for n in range(2):
                kk = 2 * k + n
                TE.wait(m_rt)
                TE.wait(S.mark())
                TE.wait(m_w13)
                for ht in range(NHT):
                    pmu = p_mm0 if ht % 2 == 0 else p_mm2
                    pmv = p_mm1 if ht % 2 == 0 else p_mm3
                    for ct in range(NCT):
                        TE.mm(pmu[:, :], fw1[:, ct, ht, :],
                              rt[:, ct, n * 512:(n + 1) * 512],
                              ct == 0, ct == NCT - 1)
                    for ct in range(NCT):
                        TE.mm(pmv[:, :], fw3[:, ct, ht, :],
                              rt[:, ct, n * 512:(n + 1) * 512],
                              ct == 0, ct == NCT - 1)
                    TE.minc()
                    m_u = TE.mark()
                    S.wait(m_u)
                    S.act(sil[:, ht % 2, :], pmu[:, :], ACT.Silu)
                    m_sil = S.mark()
                    V.wait(m_sil)
                    V.stt(hbuf[:, ht, :], pmv[:, :], 0.5, sil[:, ht % 2, :],
                          ALU.mult, ALU.mult)
                    if ht >= 1:
                        TE.wait(V.mark())
                m_h = V.mark()
                TE.wait(m_h)
                TE.wait(m_w2)
                for st4 in range(4):
                    if st4 > 0:
                        TE.wait(S.mark())
                    else:
                        S.wait(m_eo_prev)
                    for dn in range(2):
                        pm = p_mm2 if dn == 0 else p_mm3
                        for ht in range(NHT):
                            TE.mm(pm[:, :],
                                  hbuf[:, ht, st4 * 128:(st4 + 1) * 128],
                                  fw2db[:, ht, dn * 512:(dn + 1) * 512],
                                  ht == 0, ht == NHT - 1)
                    TE.minc()
                    m_o = TE.mark()
                    S.wait(m_o)
                    S.act(outt[:, st4, 0:512], p_mm2[:, :], ACT.Copy)
                    S.act(outt[:, st4, 512:1024], p_mm3[:, :], ACT.Copy)
                m_ev = S.mark()
                Q.wait(m_ev)
                if kk == 0:
                    Q.wait(m_eo_free)
                Q.dma(eo_mine.ap().rearrange("(a p) d -> p a d", p=128)
                      [:, kk * 4:(kk + 1) * 4, :], outt[:, :, :])
                m_eo_prev = Q.dmark()
            rt_free = TE.mark()
        m_eo = Q.dmark()
        G.wait(m_eo)
        m_ag_eo = G.cc("AllGather", ALU.bypass, [eo_mine.ap().opt()],
                       [eo_all.ap().opt()], s_cc, ccs)
        return m_ag_eo

    def emit_combine(m_ag_eo, m_owner, target_ap, m_target_ready, cset):
        cmb_b, cmb_f = cset
        G.wait(m_ag_eo)
        G.wait(m_owner)
        G.wait(m_target_ready)
        prev_cast = None
        for e in range(NB):
            for hhalf in range(2):
                G.wait(prev_cast)
                mg = G.gather(cmb_b[:, :, :], eo_all.ap(),
                              ow_g[:, e, hhalf * 24:(hhalf + 1) * 24],
                              384, C, transpose=False,
                              count_src=owcnt[0:1, 2 * e + hhalf:
                                              2 * e + hhalf + 1])
                V.wait(mg)
                V.copy(cmb_f[:, :, :].rearrange("p a b -> p (a b)"),
                       cmb_b[:, :, :].rearrange("p a b -> p (a b)"))
                m_cast = V.mark()
                prev_cast = m_cast
                G.wait(m_cast)
                G.scatter_add(target_ap, cmb_f[:, :, :],
                              ow_s[:, e, hhalf * 24:(hhalf + 1) * 24],
                              384, C,
                              count_src=owcnt[0:1, 2 * e + hhalf:
                                              2 * e + hhalf + 1])
        return G.dmark()

    sel_ctx = ExitStack()
    selset = (sb("selsc", [128, 256, NB], ctx=sel_ctx),
              sb("selmsk", [128, 256, NB], ctx=sel_ctx),
              sb("seltie", [128, 256, NB], ctx=sel_ctx),
              sb("selmb", [128, 256, NB], BF16, ctx=sel_ctx),
              sb("selscr", [128, 256], ctx=sel_ctx),
              sb("selscr8", [128, 256, NB], ctx=sel_ctx))
    msel1 = emit_selection(m_ag_sc1, selset)
    m_merge1 = emit_merge(msel1)
    m_owner1 = emit_owner_lists(msel1, m_merge1)
    sel_ctx.close()

    ffn_ctx = ExitStack()
    fset1 = (sb("fw1", [128, NCT, NHT, 128], BF16, ctx=ffn_ctx),
             sb("fw3", [128, NCT, NHT, 128], BF16, ctx=ffn_ctx),
             sb("fw2db", [128, NHT, 1024], BF16, ctx=ffn_ctx),
             sb("rt", [128, NCT, 1024], BF16, ctx=ffn_ctx),
             sb("hbuf", [128, NHT, 512], BF16, ctx=ffn_ctx),
             sb("sil", [128, 2, 512], BF16, ctx=ffn_ctx),
             sb("outt", [128, 4, 1024], BF16, ctx=ffn_ctx),
             sb("scrow", [1, CAP], BF16, ctx=ffn_ctx),
             sb("scband", [128, CAP], BF16, ctx=ffn_ctx))
    mgl1 = emit_gl(m_merge1, fset1[7], fset1[8])
    mrg1 = dict(gl=mgl1["gl"], scband=mgl1["scband"])
    m_ag_eo1 = emit_ffn(mrg1, m_ag_g1, fset1, None)
    ffn_ctx.close()
    cmb_ctx = ExitStack()
    cset1 = (sb("cmb_b", [128, 3, 1024], BF16, ctx=cmb_ctx),
             sb("cmb_f", [128, 3, 1024], ctx=cmb_ctx))
    m_cmb1 = emit_combine(m_ag_eo1, m_owner1, x2_plane.ap(), Q.dmark(), cset1)
    cmb_ctx.close()

    attn_ctx = ExitStack()
    wk4 = sb("wk4", [128, NCT, NCT, 128], BF16, ctx=attn_ctx)
    wv4 = sb("wv4", [128, NCT, NCT, 128], BF16, ctx=attn_ctx)
    wr4 = sb("wr4", [128, NCT, NCT, 128], BF16, ctx=attn_ctx)
    wo4 = sb("wo4", [128, NCT, 1024], BF16, ctx=attn_ctx)
    xt34 = sb("xt34", [128, 4, 1024], ctx=attn_ctx)
    g2cm = sb("g2cm", [128, NCT, CH + 1], BF16, ctx=attn_ctx)
    mixb = sb("mixb", [128, 3, CH], BF16, ctx=attn_ctx)
    xk = sb("xk", [128, NCT, CH], BF16, ctx=attn_ctx)
    xv = sb("xv", [128, NCT, CH], BF16, ctx=attn_ctx)
    xr = sb("xr", [128, NCT, CH], BF16, ctx=attn_ctx)
    kd = sb("kd", [128, CH], ctx=attn_ctx)
    vdd = sb("vdd", [128, CH], ctx=attn_ctx)
    ad = sb("ad", [128, CH], ctx=attn_ctx)
    avd = sb("avd", [128, CH], ctx=attn_ctx)
    srd = sb("srd", [128, CH], BF16, ctx=attn_ctx)
    numb = sb("numb", [128, CH], ctx=attn_ctx)
    denb = sb("denb", [128, CH], ctx=attn_ctx)
    yb = sb("yb", [128, CH], ctx=attn_ctx)
    zb = sb("zb", [128, NCT, CH], BF16, ctx=attn_ctx)
    pbuf = sb("pbuf", [128, CH + 1], ctx=attn_ctx)
    qbuf = sb("qbuf", [128, CH + 1], ctx=attn_ctx)
    pcar = sb("pcar", [128, NCT], ctx=attn_ctx)
    qcar = sb("qcar", [128, NCT], ctx=attn_ctx)
    ddfull = sb("ddfull", [128, NCT, CH], ctx=attn_ctx)

    G.lib(LC.standard)
    G.wait(m_cmb1)
    G.wait(TE.mark())
    G.wait(S.mark())
    G.wait(V.mark())
    G.dma(wk4[:, :, :, :],
          wkT_in.ap().rearrange("(ct p) (dt j) -> p ct dt j", p=128, j=128))
    G.dma(wv4[:, :, :, :],
          wvT_in.ap().rearrange("(ct p) (dt j) -> p ct dt j", p=128, j=128))
    G.dma(wr4[:, :, :, :],
          wrT_in.ap().rearrange("(ct p) (dt j) -> p ct dt j", p=128, j=128))
    G.dma(wo4[:, :, :],
          woT_in.ap().rearrange("(ct p) d -> p ct d", p=128))
    m_aw = G.dmark()

    V.wait(m_dvec)
    V.wait(m_aw)
    for dt in range(NCT):
        V.ts(ddfull[:, dt, :], dvec[:, dt, 0:1].to_broadcast([128, CH]), 1.0)
    V.memset(pcar[:, :], 0.0)
    V.memset(qcar[:, :], 0.0)
    V.memset(g2cm[:, :, 0:1].rearrange("p a b -> p (a b)"), 0.0)
    m_attn_init = V.mark()

    m_x2_ready = m_cmb1
    prev_xk_cons = None
    prev_xt_cons = None
    for c in range(NCH):
        Q.wait(m_x2_ready)
        Q.wait(prev_xt_cons)
        mx = Q.dma(xt34[:, :, :],
                   x2_plane.ap()[c * CH:(c + 1) * CH, :]
                   .rearrange("(a p) d -> p a d", p=128))
        if c > 0:
            V.copy(g2cm[:, :, 0:1].rearrange("p a b -> p (a b)"),
                   g2cm[:, :, CH:CH + 1].rearrange("p a b -> p (a b)"))
        for tt in range(4):
            V.wait(mx)
            S.wait(mx)
            V.wait(m_attn_init)
            V.ts(sq_scr[:, :], xt34[:, tt, :], 1.0, None, ALU.mult, ALU.add,
                 accum=stats[:, 0:1])
            S.act(sq_scr2[:, :], xt34[:, tt, :], ACT.Square,
                  accum=stats[:, 1:2])
            m_sq = S.mark()
            V.wait(m_sq)
            V.drain()
            V.ts(stats[:, 2:3], stats[:, 0:1], -1.0 / C)
            V.tt(stats[:, 4:5], stats[:, 2:3], stats[:, 2:3], ALU.mult)
            V.ts(stats[:, 6:7], stats[:, 1:2], 1.0 / C, EPS, ALU.mult,
                 ALU.add)
            V.tt(stats[:, 3:4], stats[:, 6:7], stats[:, 4:5], ALU.subtract)
            V.recip(stats[:, 7:8], stats[:, 3:4])
            m_rc = V.mark()
            S.wait(m_rc)
            S.act(stats[:, 8:9], stats[:, 7:8], ACT.Sqrt)
            S.nop()
            S.act(stats[:, 9:10], stats[:, 2:3], ACT.Identity,
                  scale=stats[:, 8:9])
            m_bias = S.mark()
            V.wait(m_bias)
            V.wait(lg_prev["tr"])
            V.ts(gb[:, tt % 2, :], xt34[:, tt, :], stats[:, 8:9],
                 stats[:, 9:10], ALU.mult, ALU.add)
            m_g2 = V.mark()
            TE.wait(m_g2)
            TE.wait(prev_xk_cons)
            for h in range(2):
                pt = p_tr if h == 0 else p_tr2
                for j in range(4):
                    ct = h * 4 + j
                    TE.trans(pt[:, j * 128:(j + 1) * 128],
                             gb[:, tt % 2, ct * 128:(ct + 1) * 128],
                             ident[:, :])
            TE.minc()
            m_tr = TE.mark()
            S.wait(m_tr)
            for h in range(2):
                pt = p_tr if h == 0 else p_tr2
                for j in range(4):
                    ct = h * 4 + j
                    S.act(g2cm[:, ct, 1 + tt * 128:1 + (tt + 1) * 128],
                          pt[:, j * 128:(j + 1) * 128], ACT.Copy)
            lg_prev["tr"] = m_tr
        m_g2cm = S.mark()

        V.wait(m_g2cm)
        S.wait(m_g2cm)
        for ct in range(NCT):
            for mi, t1col in enumerate([5, 6, 7]):
                S.act(mixb[:, mi, :], g2cm[:, ct, 0:CH], ACT.Identity,
                      scale=chv[:, ct, t1col:t1col + 1])
            m_mx = S.mark()
            V.wait(m_mx)
            V.stt(xk[:, ct, :], g2cm[:, ct, 1:CH + 1], chv[:, ct, 2:3],
                  mixb[:, 0, :], ALU.mult, ALU.add)
            V.stt(xv[:, ct, :], g2cm[:, ct, 1:CH + 1], chv[:, ct, 3:4],
                  mixb[:, 1, :], ALU.mult, ALU.add)
            V.stt(xr[:, ct, :], g2cm[:, ct, 1:CH + 1], chv[:, ct, 4:5],
                  mixb[:, 2, :], ALU.mult, ALU.add)
            S.wait(V.mark())
        m_mix = V.mark()

        m_z = None
        m_prev_evac = None
        for dt in range(NCT):
            TE.wait(m_mix)
            TE.wait(m_aw)
            TE.wait(m_prev_evac)
            for ct in range(NCT):
                TE.mm(p_mm0[:, :], wk4[:, ct, dt, :], xk[:, ct, :],
                      ct == 0, ct == NCT - 1)
            for ct in range(NCT):
                TE.mm(p_mm1[:, :], wv4[:, ct, dt, :], xv[:, ct, :],
                      ct == 0, ct == NCT - 1)
            for ct in range(NCT):
                TE.mm(p_mm2[:, :], wr4[:, ct, dt, :], xr[:, ct, :],
                      ct == 0, ct == NCT - 1)
            TE.minc()
            m_kvr = TE.mark()
            S.wait(m_kvr)
            S.wait(V.mark())
            S.act(kd[:, :], p_mm0[:, :], ACT.Copy)
            S.act(vdd[:, :], p_mm1[:, :], ACT.Copy)
            S.act(srd[:, :], p_mm2[:, :], ACT.Sigmoid)
            S.nop()
            S.act(ad[:, :], kd[:, :], ACT.Exp)
            m_sc_dt = S.mark()
            m_prev_evac = m_sc_dt
            V.wait(m_sc_dt)
            V.tt(avd[:, :], ad[:, :], vdd[:, :], ALU.mult)
            V.copy(pbuf[:, 0:1], pcar[:, dt:dt + 1])
            V.copy(qbuf[:, 0:1], qcar[:, dt:dt + 1])
            V.scan(pbuf[:, 1:CH + 1], ddfull[:, dt, :], avd[:, :],
                   pbuf[:, 0:1])
            V.scan(qbuf[:, 1:CH + 1], ddfull[:, dt, :], ad[:, :],
                   qbuf[:, 0:1])
            V.copy(pcar[:, dt:dt + 1], pbuf[:, CH:CH + 1])
            V.copy(qcar[:, dt:dt + 1], qbuf[:, CH:CH + 1])
            V.stt(numb[:, :], avd[:, :], dvec[:, dt, 1:2], pbuf[:, 0:CH],
                  ALU.mult, ALU.add)
            V.stt(denb[:, :], ad[:, :], dvec[:, dt, 1:2], qbuf[:, 0:CH],
                  ALU.mult, ALU.add)
            V.recip(yb[:, :], denb[:, :])
            V.tt(yb[:, :], yb[:, :], numb[:, :], ALU.mult)
            V.tt(zb[:, dt, :], yb[:, :], srd[:, :], ALU.mult)
            m_z = V.mark()
        prev_xk_cons = m_z

        for tt in range(4):
            TE.wait(m_z)
            for dn in range(2):
                pm = p_mm2 if dn == 0 else p_mm3
                for ct in range(NCT):
                    TE.mm(pm[:, :], zb[:, ct, tt * 128:(tt + 1) * 128],
                          wo4[:, ct, dn * 512:(dn + 1) * 512],
                          ct == 0, ct == NCT - 1)
            TE.minc()
            m_wo = TE.mark()
            V.wait(m_wo)
            V.tt(xt34[:, tt, 0:512], xt34[:, tt, 0:512], p_mm2[:, :], ALU.add)
            V.tt(xt34[:, tt, 512:1024], xt34[:, tt, 512:1024], p_mm3[:, :],
                 ALU.add)
            m_x3 = V.mark()
            TE.wait(m_x3)
            Q.wait(m_x3)
            Q.dma(out_ext.ap()[(c * 4 + tt) * 128:(c * 4 + tt + 1) * 128, :],
                  xt34[:, tt, :])
            lngate(c * 4 + tt, xt34[:, tt, :], xwait=m_x3)
        prev_xt_cons = V.mark()
    m_attn_q = Q.dmark()
    m_attn_done = (V.mark(), TE.mark(), S.mark(), G.mark())
    attn_ctx.close()

    G.wait(m_attn_q)
    m_ag_sc2 = G.cc("AllGather", ALU.bypass, [sc_mine.ap().opt()],
                    [sc_all.ap().opt()], s_cc, ccs)
    m_ag_g2 = G.cc("AllGather", ALU.bypass, [g_mine.ap().opt()],
                   [g_all.ap().opt()], s_cc, ccs)

    sel_ctx2 = ExitStack()
    selset2 = (sb("selsc2", [128, 256, NB], ctx=sel_ctx2),
               sb("selmsk2", [128, 256, NB], ctx=sel_ctx2),
               sb("seltie2", [128, 256, NB], ctx=sel_ctx2),
               sb("selmb2", [128, 256, NB], BF16, ctx=sel_ctx2),
               sb("selscr2", [128, 256], ctx=sel_ctx2),
               sb("selscr82", [128, 256, NB], ctx=sel_ctx2))
    msel2 = emit_selection(m_ag_sc2, selset2,
                           m_wfree=(mgl1['gl'], mgl1['lin']))
    m_merge2 = emit_merge(msel2)
    m_owner2 = emit_owner_lists(msel2, m_merge2)
    sel_ctx2.close()

    ffn_ctx2 = ExitStack()
    fset2 = (sb("fw1b", [128, NCT, NHT, 128], BF16, ctx=ffn_ctx2),
             sb("fw3b", [128, NCT, NHT, 128], BF16, ctx=ffn_ctx2),
             sb("fw2db2", [128, NHT, 1024], BF16, ctx=ffn_ctx2),
             sb("rt2", [128, NCT, 1024], BF16, ctx=ffn_ctx2),
             sb("hbuf2", [128, NHT, 512], BF16, ctx=ffn_ctx2),
             sb("sil2", [128, 2, 512], BF16, ctx=ffn_ctx2),
             sb("outt2", [128, 4, 1024], BF16, ctx=ffn_ctx2),
             sb("scrow2", [1, CAP], BF16, ctx=ffn_ctx2),
             sb("scband2", [128, CAP], BF16, ctx=ffn_ctx2))
    mgl2 = emit_gl(m_merge2, fset2[7], fset2[8])
    mrg2 = dict(gl=mgl2["gl"], scband=mgl2["scband"])
    m_ag_eo2 = emit_ffn(mrg2, m_ag_g2, fset2, m_ag_eo1)
    ffn_ctx2.close()
    cmb_ctx2 = ExitStack()
    cset2 = (sb("cmb_b2", [128, 3, 1024], BF16, ctx=cmb_ctx2),
             sb("cmb_f2", [128, 3, 1024], ctx=cmb_ctx2))
    m_cmb2 = emit_combine(m_ag_eo2, m_owner2, out_ext.ap(), m_attn_q, cset2)
    cmb_ctx2.close()

    G.wait(m_cmb2)
    G.wait(Q.dmark())
    G.wait(V.mark())
    G.wait(S.mark())
    G.wait(TE.mark())

    globals()["_RECS"] = dict(Q=Q, V=V, S=S, TE=TE, G=G)
    with top, nc.Block() as block:
        @block.sync
        def _(e):
            Q.replay(e)

        @block.vector
        def _(e):
            V.replay(e)

        @block.scalar
        def _(e):
            S.replay(e)

        @block.tensor
        def _(e):
            TE.replay(e)

        @block.gpsimd
        def _(e):
            G.replay(e)

    return nc


_NC_CACHE = {}


def kernel(L0_attn_x, L0_attn_time_decay, L0_attn_time_first,
           L0_attn_time_mix_k, L0_attn_time_mix_v, L0_attn_time_mix_r,
           L0_attn_Wk, L0_attn_Wv, L0_attn_Wr, L0_attn_Wo,
           L0_ffn_wg, L0_ffn_w1, L0_ffn_w3, L0_ffn_w2):
    x = np.ascontiguousarray(np.asarray(L0_attn_x, dtype=np.float32))
    td = np.asarray(L0_attn_time_decay, np.float32).reshape(C)
    tf = np.asarray(L0_attn_time_first, np.float32).reshape(C)
    tmk = np.asarray(L0_attn_time_mix_k, np.float32).reshape(C)
    tmv = np.asarray(L0_attn_time_mix_v, np.float32).reshape(C)
    tmr = np.asarray(L0_attn_time_mix_r, np.float32).reshape(C)

    def chmaj(v):
        return v.reshape(NCT, 128).T

    chvec = np.ascontiguousarray(
        np.stack([chmaj(td), chmaj(tf), chmaj(tmk), chmaj(tmv), chmaj(tmr)],
                 axis=2).astype(np.float32))

    wkT = np.ascontiguousarray(np.asarray(L0_attn_Wk, np.float32).T)
    wvT = np.ascontiguousarray(np.asarray(L0_attn_Wv, np.float32).T)
    wrT = np.ascontiguousarray(np.asarray(L0_attn_Wr, np.float32).T)
    woT = np.ascontiguousarray(np.asarray(L0_attn_Wo, np.float32).T)
    wg = np.ascontiguousarray(np.asarray(L0_ffn_wg, np.float32))
    w1 = np.asarray(L0_ffn_w1, np.float32)
    w3 = np.asarray(L0_ffn_w3, np.float32)
    w2 = np.asarray(L0_ffn_w2, np.float32)

    identbf = np.eye(128, dtype=ml_dtypes.bfloat16)
    p = np.arange(128)
    g16c = (p[:, None] // 16 == np.arange(NB)[None, :]).astype(np.float32)
    h16c = (p[:, None] % 16 == np.arange(16)[None, :]).astype(ml_dtypes.bfloat16)
    tric = (np.arange(NB)[:, None] < np.arange(NB)[None, :]).astype(np.float32)
    onesrow = np.ones((1, 128), np.float32)
    ones128c = np.ones((128, 1), np.float32)
    tidbase = (4096.0 * (p // 16) + (p % 16)).astype(np.float32).reshape(128, 1)
    selu = np.zeros((NB, 128), np.float32)
    for u in range(NB):
        selu[u, 16 * u:16 * u + 16] = 1.0

    in_maps = []
    for r in range(NB):
        rkvc = np.zeros((1, 16), np.float32)
        rkvc[0, r] = 1.0
        rkvc[0, 8] = r
        rkvc[0, 9] = 4096.0 * r
        ohcolc = np.zeros((NB, 1), np.float32)
        ohcolc[r, 0] = 1.0
        in_maps.append({
            "x": np.ascontiguousarray(x[r]),
            "wkT": wkT, "wvT": wvT, "wrT": wrT, "woT": woT,
            "w1": np.ascontiguousarray(w1[r]),
            "w3": np.ascontiguousarray(w3[r]),
            "w2": np.ascontiguousarray(w2[r]),
            "wg": wg,
            "chvec": chvec,
            "rkv": rkvc,
            "ohcol": ohcolc,
            "identbf": identbf,
            "g16": g16c,
            "h16": h16c,
            "triexcl": tric,
            "onesrow": onesrow,
            "ones128": ones128c,
            "tidbase": tidbase,
            "selu16": selu,
            "identf8": np.eye(NB, dtype=np.float32),
        })

    if "prog" not in _NC_CACHE:
        nc = build_program()
        nc.finalize()
        _NC_CACHE["prog"] = nc
    nc = _NC_CACHE["prog"]
    res = run_bass_kernel_spmd(nc, in_maps, core_ids=list(range(NB)))
    out = np.stack([np.asarray(res.results[r]["out"], np.float32)
                    for r in range(NB)], axis=0)
    return out


# version 07
# speedup vs baseline: 1.0619x; 1.0077x over previous
import numpy as np
import ml_dtypes
from contextlib import ExitStack

import concourse.bacc as bacc
import concourse.mybir as mybir
from concourse.bass import IndirectOffsetOnAxis
from concourse.bass_utils import run_bass_kernel_spmd
from concourse import library_config as LC

F32 = mybir.dt.float32
BF16 = mybir.dt.bfloat16
I16 = mybir.dt.int16
I32 = mybir.dt.int32
U32 = mybir.dt.uint32
ALU = mybir.AluOpType
ACT = mybir.ActivationFunctionType
AXX = mybir.AxisListType

NB = 8
T = 4096
NT = NB * T
C = 1024
NCT = 8
HID = 2048
NHT = 16
CAP = 4096
EPS = 1e-5
VAL_ITERS = 45
TIE_ITERS = 16
WPAD = 304
CH = 512
NCH = T // CH


class Rec:
    def __init__(self, name, sem, dsem):
        self.name = name
        self.sem = sem
        self.dsem = dsem
        self.q = []
        self.n = 0
        self.nd = 0
        self.last_out = set()
        self.recent = []
        self.waited = {}
        self.ev = []

    def wait(self, mark):
        if mark is None:
            return
        sem, count = mark
        if count <= 0:
            return
        key = id(sem)
        if self.waited.get(key, -1) >= count:
            return
        self.waited[key] = count
        self.ev.append(("w", id(sem), count, len(self.q)))
        self.q.append(lambda e: e.wait_ge(sem, count))

    def mark(self):
        return (self.sem, self.n)

    def dmark(self):
        return (self.dsem, self.nd)

    @staticmethod
    def _names(aps):
        out = set()
        for ap in aps:
            if ap is None or isinstance(ap, (int, float)):
                continue
            t = getattr(ap, "tensor", None)
            if t is not None:
                out.add(t.name)
        return out

    def _hz(self, ins, outs):
        if self.name not in ("vector", "scalar", "gpsimd"):
            return
        rd = self._names(ins)
        wr = self._names(outs)
        rws = set()
        rds = set()
        for (r, w) in self.recent:
            rws |= w
            rds |= r
        if (rd & rws) or (wr & rws) or (wr & rds):
            if self.name == "vector":
                self.q.append(lambda e: e.drain())
            else:
                sem, n = self.sem, self.n
                self.ev.append(("w", id(sem), n, len(self.q)))
                self.q.append(lambda e: e.wait_ge(sem, n))
            self.recent = []
        self.recent = (self.recent + [(rd, wr)])[-3:]

    def drain(self):
        self.q.append(lambda e: e.drain())
        self.recent = []

    def _inc(self, call):
        sem = self.sem
        self.ev.append(("i", id(sem), 1, len(self.q)))
        self.q.append(lambda e: call(e).then_inc(sem, 1))
        self.n += 1

    def ts(self, out, in0, s1, s2=None, op0=ALU.mult, op1=None, accum=None):
        self._hz([in0, s1, s2], [out, accum])
        kw = dict(out=out, in0=in0, scalar1=s1, scalar2=s2, op0=op0)
        if op1 is not None:
            kw["op1"] = op1
        if accum is not None:
            kw["accum_out"] = accum
        self._inc(lambda e: e.tensor_scalar(**kw))

    def tt(self, out, in0, in1, op):
        self._hz([in0, in1], [out])
        self._inc(lambda e: e.tensor_tensor(out=out, in0=in0, in1=in1, op=op))

    def stt(self, out, in0, s, in1, op0, op1, accum=None):
        self._hz([in0, in1, s], [out, accum])
        kw = dict(out=out, in0=in0, scalar=s, in1=in1, op0=op0, op1=op1)
        if accum is not None:
            kw["accum_out"] = accum
        self._inc(lambda e: e.scalar_tensor_tensor(**kw))

    def copy(self, out, in_):
        self._hz([in_], [out])
        self._inc(lambda e: e.tensor_copy(out=out, in_=in_))

    def memset(self, ap, val):
        self._hz([], [ap])
        self._inc(lambda e: e.memset(ap, val))

    def recip(self, out, in_):
        self._hz([in_], [out])
        self._inc(lambda e: e.reciprocal(out=out, in_=in_))

    def reduce(self, out, in_, op, axis=AXX.X):
        self._hz([in_], [out])
        self._inc(lambda e: e.tensor_reduce(out=out, in_=in_, axis=axis, op=op))

    def scan(self, out, d0, d1, initial):
        self._hz([d0, d1, initial], [out])
        self._inc(lambda e: e.tensor_tensor_scan(
            out=out, data0=d0, data1=d1, initial=initial,
            op0=ALU.mult, op1=ALU.add))

    def act(self, out, in_, func, bias=None, scale=None, accum=None):
        self._hz([in_, bias, scale], [out, accum])
        kw = dict(out=out, in_=in_, func=func)
        if bias is not None:
            kw["bias"] = bias
        if scale is not None:
            kw["scale"] = scale
        if accum is not None:
            kw["accum_out"] = accum
        self._inc(lambda e: e.activation(**kw))

    def nop(self):
        self.q.append(lambda e: e.nop())

    def gdiv(self, out, in0, in1):
        self._inc(lambda e: e.tensor_tensor(out=out, in0=in0, in1=in1,
                                            op=ALU.divide))

    def mm(self, out, lhsT, rhs, start, stop):
        ent = {"k": "mm", "out": out, "lhsT": lhsT, "rhs": rhs,
               "start": start, "stop": stop, "inc": False, "sem": self.sem}
        self.q.append(ent)

    def trans(self, out, in_, ident):
        ent = {"k": "tr", "out": out, "in_": in_, "ident": ident,
               "inc": False, "sem": self.sem}
        self.q.append(ent)

    def minc(self):
        assert isinstance(self.q[-1], dict)
        self.q[-1]["inc"] = True
        self.ev.append(("i", id(self.sem), 1, len(self.q) - 1))
        self.n += 1

    def dma(self, out, in_, **kw):
        dsem = self.dsem
        self.wait(self.dmark())
        self.ev.append(("i", id(dsem), 16, len(self.q)))
        self.q.append(lambda e: e.dma_start(out=out, in_=in_, **kw)
                      .then_inc(dsem, 16))
        self.nd += 16
        if self.name in ("vector", "scalar", "gpsimd"):
            self.recent = (self.recent + [(set(), self._names([out]))])[-3:]
        return self.dmark()

    def sg(self, out, in_, nf):
        self._hz([in_], [out, nf])
        self._inc(lambda e: e.sparse_gather(out=out, in_=in_, num_found=nf))

    def lib(self, lib):
        self.q.append(lambda e: e.load_library(lib))

    def pbcast(self, out, in_, channels):
        self._hz([in_], [out])
        self._inc(lambda e: e.partition_broadcast(out_ap=out, in_ap=in_,
                                                  channels=channels))

    def gather(self, out, tab, idxs, n, elem, transpose=False, count_src=None):
        dsem = self.dsem
        self.wait(self.dmark())
        self.ev.append(("i", id(dsem), 16, len(self.q)))

        def emit(e):
            nreg = n if count_src is None else e.value_load(
                count_src, min_val=0, max_val=n)
            e.dma_gather(
                out_ap=out, in_ap=tab, idxs_ap=idxs, num_idxs=n,
                num_idxs_reg=nreg, elem_size=elem, transpose=transpose,
                single_packet=False).then_inc(dsem, 16)
        self.q.append(emit)
        self.nd += 16
        return self.dmark()

    def scatter_add(self, out, in_, idxs, n, elem, count_src=None):
        dsem = self.dsem
        self.wait(self.dmark())
        self.ev.append(("i", id(dsem), 16, len(self.q)))

        def emit(e):
            nreg = n if count_src is None else e.value_load(
                count_src, min_val=0, max_val=n)
            e.dma_scatter_add(
                out_ap=out, in_ap=in_, idxs_ap=idxs, num_idxs=n,
                num_idxs_reg=nreg, elem_size=elem,
                single_packet=False).then_inc(dsem, 16)
        self.q.append(emit)
        self.nd += 16
        return self.dmark()

    def imerge(self, out_ap, idx_ap, in_ap):
        dsem = self.dsem
        self.wait(self.dmark())
        self.ev.append(("i", id(dsem), 16, len(self.q)))
        self.q.append(lambda e: e.indirect_dma_start(
            out=out_ap, out_offset=IndirectOffsetOnAxis(ap=idx_ap, axis=1),
            in_=in_ap, in_offset=None,
            compute_op=ALU.add).then_inc(dsem, 16))
        self.nd += 16
        return self.dmark()

    def iota(self, out, pattern, base, cm):
        self._hz([], [out])
        self._inc(lambda e: e.iota(out=out, pattern=pattern, base=base,
                                   channel_multiplier=cm,
                                   allow_small_or_imprecise_dtypes=True))

    def cc(self, kind, op, ins, outs, ccsem, state):
        state[0] += 1
        count = state[0]
        self.ev.append(("i", id(ccsem), 1, len(self.q)))
        self.q.append(lambda e: e.collective_compute(
            kind, op, replica_groups=[list(range(NB))], ins=ins, outs=outs)
            .then_inc(ccsem, 1))
        return (ccsem, count)

    def replay(self, raw):
        for fn in self.q:
            if isinstance(fn, dict):
                if fn["k"] == "mm":
                    r = raw.matmul(out=fn["out"], lhsT=fn["lhsT"],
                                   rhs=fn["rhs"], start=fn["start"],
                                   stop=fn["stop"])
                else:
                    r = raw.transpose(out=fn["out"], in_=fn["in_"],
                                      identity=fn["ident"])
                if fn["inc"]:
                    r.then_inc(fn["sem"], 1)
            else:
                fn(raw)


def build_program():
    nc = bacc.Bacc("TRN2", target_bir_lowering=False)

    def ein(name, shape, dt=F32):
        return nc.dram_tensor(name, shape, dt, kind="ExternalInput")

    x_in = ein("x", [T, C])
    wkT_in = ein("wkT", [C, C]); wvT_in = ein("wvT", [C, C])
    wrT_in = ein("wrT", [C, C]); woT_in = ein("woT", [C, C])
    w1_in = ein("w1", [C, HID]); w3_in = ein("w3", [C, HID])
    w2_in = ein("w2", [HID, C])
    wg_in = ein("wg", [C, NB])
    chvec_in = ein("chvec", [128, NCT, 5])
    rkv_in = ein("rkv", [1, 16])
    ohcol_in = ein("ohcol", [NB, 1])
    ident_in = ein("identbf", [128, 128], BF16)
    g16_in = ein("g16", [128, NB])
    h16_in = ein("h16", [128, 16], BF16)
    tri_in = ein("triexcl", [NB, NB])
    onesr_in = ein("onesrow", [1, 128])
    ones128_in = ein("ones128", [128, 1])
    tidb_in = ein("tidbase", [128, 1])
    selu_in = ein("selu16", [NB, 128])
    identf8_in = ein("identf8", [NB, NB])
    out_ext = nc.dram_tensor("out", [T, C], F32, kind="ExternalOutput")

    def dram(name, shape, dt):
        return nc.dram_tensor(name, shape, dt)

    g_mine = dram("g_mine", [T, C], BF16)
    g_all = dram("g_all", [NT, C], BF16)
    sc_mine = dram("sc_mine", [16, 256 * NB], F32)
    sc_all = dram("sc_all", [128, 256 * NB], F32)
    eo_mine = dram("eo_mine", [CAP, C], BF16)
    eo_all = dram("eo_all", [NB * CAP, C], BF16)
    x2_plane = dram("x2_plane", [T, C], F32)
    w_ids = dram("w_ids", [16, WPAD], I32)
    w_sc = dram("w_sc", [16, WPAD], F32)
    lin_sc = dram("lin_sc", [1, CAP], F32)

    top = ExitStack()

    def sb(name, shape, dt=F32, ctx=None):
        return (ctx or top).enter_context(nc.sbuf_tensor(name, shape, dt))

    def mkpsum(name, shape, dt=F32):
        return top.enter_context(nc.psum_tensor(name, shape, dt))

    def mksem(name):
        return top.enter_context(nc.semaphore(name))

    ident = sb("ident_s", [128, 128], BF16)
    chv = sb("chv", [128, NCT, 8])
    dvec = sb("dvec", [128, NCT, 2])
    wgt = sb("wgt", [128, NCT, NB], BF16)
    rkv = sb("rkv_s", [1, 16])
    ohcol = sb("ohcol_s", [NB, 1])
    g16 = sb("g16_s", [128, NB])
    h16 = sb("h16_s", [128, 16], BF16)
    tri = sb("tri_s", [NB, NB])
    onesr = sb("onesr_s", [1, 128])
    ones128 = sb("ones128_s", [128, 1])
    oh128 = sb("oh128", [128, NB])
    w128 = sb("w128", [128, 1])
    tidb = sb("tidb_s", [128, 1])
    selu16 = sb("selu16_s", [NB, 128])
    identf8 = sb("identf8_s", [NB, NB])
    iota256 = sb("iota256", [128, 256])
    iota48w = sb("iota48w", [16, 48])
    iotap16 = sb("iotap16", [16, 1])
    iotaloc = sb("iotaloc", [16, 256])
    idxcol = sb("idxcol", [128, 1], U32)
    zero16 = sb("zero16", [16, WPAD])
    cid = sb("cid", [128, 256])
    csc = sb("csc", [128, 256])
    tid = sb("tid", [128, 256])
    bnd = sb("bnd", [1, 64])
    cnt = sb("cnt", [128, NB])
    cstat = sb("cstat", [NB, NB])
    starts = sb("starts", [NB, NB])
    sm8 = sb("sm8", [NB, 1])
    scr8 = sb("scr8", [NB, NB])
    myrow = sb("myrow", [1, 2 * NB])
    owcnt_f = sb("owcnt_f", [1, 2 * NB])
    owcnt = sb("owcnt", [1, 2 * NB], I32)
    lm16 = sb("lm16", [16, 256, NB])
    sg_in = sb("sg_in", [16, 256])
    sg_in2 = sb("sg_in2", [16, 256])
    sg_out = sb("sg_out", [16, 48], I32)
    sg_outf = sb("sg_outf", [16, 48])
    sg_nf = sb("sg_nf", [1, 1], U32)
    nfb = sb("nfb", [1, 1])
    m48 = sb("m48", [16, 48])
    sg_ids = sb("sg_ids", [16, 48])
    sg_scv = sb("sg_scv", [16, 48])
    mcol = sb("mcol", [16, 4])
    gl32 = sb("gl32", [16, 256], I32)
    glf = sb("glf", [16, 256])
    gl_idx = sb("gl_idx", [128, 256], I16)
    ow_g = sb("ow_g", [128, NB, 48], I16)
    ow_s = sb("ow_s", [128, NB, 48], I16)
    lsl = sb("lsl", [16, 256])
    stats = sb("stats", [128, 16])
    sq_scr = sb("sq_scr", [128, 1024])
    sq_scr2 = sb("sq_scr2", [128, 1024])
    gb = sb("gb", [128, 2, 1024], BF16)
    gcm = sb("gcm", [128, 2, NCT, 128], BF16)
    scem = sb("scem", [NB, 128])
    sct = sb("sct", [128, NB])
    p_ex = mkpsum("p_ex", [128, 32])
    p_tr = mkpsum("p_tr", [128, 512], BF16)
    p_tr2 = mkpsum("p_tr2", [128, 512], BF16)
    p_mm0 = mkpsum("p_mm0", [128, 512])
    p_mm1 = mkpsum("p_mm1", [128, 512])
    p_mm2 = mkpsum("p_mm2", [128, 512])
    p_mm3 = mkpsum("p_mm3", [128, 512])
    p_sc = mkpsum("p_sc", [128, 128])

    s_q = mksem("s_q"); s_qd = mksem("s_qd")
    s_v = mksem("s_v"); s_vd = mksem("s_vd")
    s_s = mksem("s_s"); s_sd = mksem("s_sd")
    s_t = mksem("s_t"); s_td = mksem("s_td")
    s_g = mksem("s_g"); s_gd = mksem("s_gd")
    s_cc = mksem("s_cc")

    Q = Rec("sync", s_q, s_qd)
    V = Rec("vector", s_v, s_vd)
    S = Rec("scalar", s_s, s_sd)
    TE = Rec("tensor", s_t, s_td)
    G = Rec("gpsimd", s_g, s_gd)
    ccs = [0]

    Q.dma(x2_plane.ap(), x_in.ap())
    for dst, src in [(ident[:, :], ident_in.ap()),
                     (chv[:, :, 0:5], chvec_in.ap()),
                     (rkv[:, :], rkv_in.ap()),
                     (ohcol[:, :], ohcol_in.ap()),
                     (g16[:, :], g16_in.ap()),
                     (h16[:, :], h16_in.ap()),
                     (tri[:, :], tri_in.ap()),
                     (onesr[:, :], onesr_in.ap()),
                     (ones128[:, :], ones128_in.ap()),
                     (tidb[:, :], tidb_in.ap()),
                     (selu16[:, :], selu_in.ap()),
                     (identf8[:, :], identf8_in.ap())]:
        G.dma(dst, src)
    G.dma(wgt[:, :, :], wg_in.ap().rearrange("(ct p) e -> p ct e", p=128))
    m_consts = G.dmark()
    G.iota(iota256[:, :], [[16, 256]], 0, 0)
    G.iota(iota48w[:, :], [[16, 48]], 0, 1)
    G.iota(iotap16[:, :], [[0, 1]], 0, 1)
    G.iota(iotaloc[:, :], [[16, 256]], 0, 1)

    V.wait(m_consts)
    V.memset(zero16[:, :], 0.0)
    V.ts(chv[:, :, 5:8], chv[:, :, 2:5], -1.0, 1.0, ALU.mult, ALU.add)

    S.wait(m_consts)
    S.act(stats[:, 0:8], chv[:, :, 0:1].rearrange("p a b -> p (a b)"), ACT.Exp)
    S.nop()
    S.act(dvec[:, :, 0:1].rearrange("p a b -> p (a b)"), stats[:, 0:8],
          ACT.Exp, scale=-1.0)
    S.act(dvec[:, :, 1:2].rearrange("p a b -> p (a b)"),
          chv[:, :, 1:2].rearrange("p a b -> p (a b)"), ACT.Exp)
    m_dvec = S.mark()

    TE.wait(m_consts)
    TE.mm(p_ex[:, 0:8], onesr[:, :], rkv[0:1, 0:8], True, True)
    TE.minc()
    S.wait(TE.mark())
    S.act(oh128[:, :], p_ex[:, 0:8], ACT.Copy)
    m_oh128 = S.mark()
    V.wait(m_oh128)
    V.tt(cnt[:, :], g16[:, :], oh128[:, :], ALU.mult)
    V.reduce(w128[:, :], cnt[:, :], ALU.add)
    m_w128 = V.mark()

    lg_prev = {"tr": None, "gstore": None, "mm": None, "sct_q": None}

    def lngate(i, xin_ap, xwait=None):
        if xwait is not None:
            V.wait(xwait)
            S.wait(xwait)
        V.ts(sq_scr[:, :], xin_ap, 1.0, None, ALU.mult, ALU.add,
             accum=stats[:, 0:1])
        S.act(sq_scr2[:, :], xin_ap, ACT.Square, accum=stats[:, 1:2])
        m_sq = S.mark()
        V.wait(m_sq)
        V.drain()
        V.ts(stats[:, 2:3], stats[:, 0:1], -1.0 / C)
        V.tt(stats[:, 4:5], stats[:, 2:3], stats[:, 2:3], ALU.mult)
        V.ts(stats[:, 6:7], stats[:, 1:2], 1.0 / C, EPS, ALU.mult, ALU.add)
        V.tt(stats[:, 3:4], stats[:, 6:7], stats[:, 4:5], ALU.subtract)
        V.recip(stats[:, 7:8], stats[:, 3:4])
        m_rc = V.mark()
        S.wait(m_rc)
        S.act(stats[:, 8:9], stats[:, 7:8], ACT.Sqrt)
        S.nop()
        S.act(stats[:, 9:10], stats[:, 2:3], ACT.Identity, scale=stats[:, 8:9])
        m_bias = S.mark()
        V.wait(m_bias)
        V.wait(lg_prev["tr"])
        V.wait(lg_prev["gstore"])
        par = i % 2
        V.ts(gb[:, par, :], xin_ap, stats[:, 8:9], stats[:, 9:10],
             ALU.mult, ALU.add)
        m_g = V.mark()
        Q.wait(m_g)
        gst = Q.dma(g_mine.ap()[i * 128:(i + 1) * 128, :], gb[:, par, :])
        TE.wait(m_g)
        TE.wait(lg_prev["mm"])
        for h in range(2):
            pt = p_tr if h == 0 else p_tr2
            for j in range(4):
                ct = h * 4 + j
                TE.trans(pt[:, j * 128:(j + 1) * 128],
                         gb[:, par, ct * 128:(ct + 1) * 128], ident[:, :])
        TE.minc()
        m_tr = TE.mark()
        S.wait(m_tr)
        S.act(gcm[:, par, 0:4, :].rearrange("p a b -> p (a b)"), p_tr[:, :],
              ACT.Copy)
        S.act(gcm[:, par, 4:8, :].rearrange("p a b -> p (a b)"), p_tr2[:, :],
              ACT.Copy)
        m_gcm = S.mark()
        TE.wait(m_gcm)
        TE.wait(lg_prev["sct_q"])
        for ct in range(NCT):
            TE.mm(p_sc[0:NB, :], wgt[:, ct, :], gcm[:, par, ct, :],
                  ct == 0, ct == NCT - 1)
        TE.minc()
        m_mm = TE.mark()
        S.wait(m_mm)
        S.act(scem[:, :], p_sc[0:NB, :], ACT.Copy)
        m_scem = S.mark()
        TE.wait(m_scem)
        TE.trans(p_ex[:, 8:16], scem[:, :], identf8[:, :])
        TE.minc()
        m_sct = TE.mark()
        S.wait(m_sct)
        S.act(sct[:, :], p_ex[:, 8:16], ACT.Exp)
        m_exp = S.mark()
        V.wait(m_exp)
        V.reduce(stats[:, 10:11], sct[:, :], ALU.add)
        V.recip(stats[:, 11:12], stats[:, 10:11])
        V.ts(sct[:, :], sct[:, :], stats[:, 11:12])
        m_sm = V.mark()
        Q.wait(m_sm)
        scq = Q.dma(sc_mine.ap().rearrange("q (f e) -> q f e", e=NB)
                    [:, 8 * i:8 * i + 8, :].rearrange("q w e -> w q e"),
                    sct[:, :])
        lg_prev["tr"] = m_tr
        lg_prev["gstore"] = gst
        lg_prev["mm"] = m_mm
        lg_prev["sct_q"] = scq

    p1ctx = ExitStack()
    xt2 = sb("xt2", [128, 2, 1024], ctx=p1ctx)
    ldm = [None] * 34
    ldm[0] = Q.dma(xt2[:, 0, :], x_in.ap()[0:128, :])
    ldm[1] = Q.dma(xt2[:, 1, :], x_in.ap()[128:256, :])
    for i in range(32):
        lngate(i, xt2[:, i % 2, :], xwait=ldm[i])
        if i + 2 < 32:
            Q.wait(V.mark())
            ldm[i + 2] = Q.dma(xt2[:, i % 2, :],
                               x_in.ap()[(i + 2) * 128:(i + 3) * 128, :])
    p1ctx.close()
    m_lg1_q = Q.dmark()

    G.wait(m_lg1_q)
    m_ag_sc1 = G.cc("AllGather", ALU.bypass, [sc_mine.ap().opt()],
                    [sc_all.ap().opt()], s_cc, ccs)
    m_ag_g1 = G.cc("AllGather", ALU.bypass, [g_mine.ap().opt()],
                   [g_all.ap().opt()], s_cc, ccs)

    def emit_selection(m_ag_sc, selset, m_wfree=()):
        selsc, selmsk, seltie, selmb, selscr, selscr8 = selset
        Q.wait(m_ag_sc)
        Q.dma(selsc[:, :, :],
              sc_all.ap().rearrange("p (f e) -> p f e", e=NB))
        m_sel_ld = Q.dmark()
        for mk in m_wfree:
            G.wait(mk)
        G.dma(w_ids.ap(), zero16[:, :])
        G.dma(w_sc.ap(), zero16[:, :])

        V.wait(m_sel_ld)
        V.ts(tid[:, :], iota256[:, :], tidb[:, 0:1], None, ALU.add)
        V.memset(bnd[:, 0:8], 0.0)
        V.memset(bnd[:, 8:16], 1.0)

        for it in range(VAL_ITERS):
            V.tt(bnd[:, 16:24], bnd[:, 0:8], bnd[:, 8:16], ALU.add)
            V.ts(bnd[:, 16:24], bnd[:, 16:24], 0.5)
            TE.wait(V.mark())
            TE.mm(p_ex[:, 16:24], onesr[:, :], bnd[0:1, 16:24], True, True)
            TE.minc()
            V.wait(TE.mark())
            for e in range(NB):
                V.ts(selscr[:, :], selsc[:, :, e], p_ex[:, 16 + e:17 + e],
                     None, ALU.is_gt, ALU.add, accum=cnt[:, e:e + 1])
            TE.wait(V.mark())
            TE.mm(p_ex[0:1, 24:32], ones128[:, :], cnt[:, :], True, True)
            TE.minc()
            V.wait(TE.mark())
            V.ts(bnd[:, 24:32], p_ex[0:1, 24:32], float(CAP), None, ALU.is_ge)
            V.tt(bnd[:, 32:40], bnd[:, 16:24], bnd[:, 0:8], ALU.subtract)
            V.tt(bnd[:, 40:48], bnd[:, 24:32], bnd[:, 32:40], ALU.mult)
            V.tt(bnd[:, 0:8], bnd[:, 0:8], bnd[:, 40:48], ALU.add)
            V.tt(bnd[:, 32:40], bnd[:, 8:16], bnd[:, 16:24], ALU.subtract)
            V.tt(bnd[:, 40:48], bnd[:, 24:32], bnd[:, 32:40], ALU.mult)
            V.tt(bnd[:, 8:16], bnd[:, 16:24], bnd[:, 40:48], ALU.add)

        TE.wait(V.mark())
        TE.mm(p_ex[:, 16:24], onesr[:, :], bnd[0:1, 8:16], True, True)
        TE.mm(p_ex[:, 24:32], onesr[:, :], bnd[0:1, 0:8], True, True)
        TE.minc()
        V.wait(TE.mark())
        for e in range(NB):
            V.ts(selmsk[:, :, e], selsc[:, :, e], p_ex[:, 16 + e:17 + e],
                 None, ALU.is_gt, ALU.add, accum=cnt[:, e:e + 1])
        for e in range(NB):
            V.ts(seltie[:, :, e], selsc[:, :, e], p_ex[:, 24 + e:25 + e],
                 None, ALU.is_gt)
        for e in range(NB):
            V.tt(seltie[:, :, e], seltie[:, :, e], selmsk[:, :, e],
                 ALU.subtract)
        TE.wait(V.mark())
        TE.mm(p_ex[0:1, 24:32], ones128[:, :], cnt[:, :], True, True)
        TE.minc()
        V.wait(TE.mark())
        V.ts(bnd[:, 48:56], p_ex[0:1, 24:32], -1.0, float(CAP),
             ALU.mult, ALU.add)
        V.memset(bnd[:, 0:8], 0.0)
        V.memset(bnd[:, 8:16], float(NT))

        for it in range(TIE_ITERS):
            V.tt(bnd[:, 16:24], bnd[:, 0:8], bnd[:, 8:16], ALU.add)
            V.ts(bnd[:, 16:24], bnd[:, 16:24], 0.5)
            TE.wait(V.mark())
            TE.mm(p_ex[:, 16:24], onesr[:, :], bnd[0:1, 16:24], True, True)
            TE.minc()
            V.wait(TE.mark())
            for e in range(NB):
                V.stt(selscr[:, :], tid[:, :], p_ex[:, 16 + e:17 + e],
                      seltie[:, :, e], ALU.is_lt, ALU.mult,
                      accum=cnt[:, e:e + 1])
            TE.wait(V.mark())
            TE.mm(p_ex[0:1, 24:32], ones128[:, :], cnt[:, :], True, True)
            TE.minc()
            V.wait(TE.mark())
            V.tt(bnd[:, 24:32], p_ex[0:1, 24:32], bnd[:, 48:56], ALU.is_ge)
            V.tt(bnd[:, 32:40], bnd[:, 16:24], bnd[:, 8:16], ALU.subtract)
            V.tt(bnd[:, 40:48], bnd[:, 24:32], bnd[:, 32:40], ALU.mult)
            V.tt(bnd[:, 8:16], bnd[:, 8:16], bnd[:, 40:48], ALU.add)
            V.tt(bnd[:, 32:40], bnd[:, 0:8], bnd[:, 16:24], ALU.subtract)
            V.tt(bnd[:, 40:48], bnd[:, 24:32], bnd[:, 32:40], ALU.mult)
            V.tt(bnd[:, 0:8], bnd[:, 16:24], bnd[:, 40:48], ALU.add)

        TE.wait(V.mark())
        TE.mm(p_ex[:, 16:24], onesr[:, :], bnd[0:1, 8:16], True, True)
        TE.minc()
        V.wait(TE.mark())
        for e in range(NB):
            V.stt(selscr[:, :], tid[:, :], p_ex[:, 16 + e:17 + e],
                  seltie[:, :, e], ALU.is_lt, ALU.mult)
            V.tt(selmsk[:, :, e], selmsk[:, :, e], selscr[:, :], ALU.add)
        for e in range(NB):
            V.ts(selscr[:, :], selmsk[:, :, e], 1.0, None, ALU.mult, ALU.add,
                 accum=cnt[:, e:e + 1])
        V.wait(m_w128)
        V.tt(selmb[:, :, :], selmsk[:, :, :],
             w128[:, 0:1].rearrange("p (a b) -> p a b", a=1)
             .to_broadcast([128, 256, NB]), ALU.mult)
        m_fmask = V.mark()
        TE.wait(m_fmask)
        TE.mm(p_ex[0:8, 24:32], g16[:, :], cnt[:, :], True, True)
        TE.minc()
        S.wait(TE.mark())
        S.act(cstat[:, :], p_ex[0:8, 24:32], ACT.Copy)
        TE.wait(S.mark())
        TE.mm(p_ex[0:8, 16:24], tri[:, :], cstat[:, :], True, True)
        TE.minc()
        S.wait(TE.mark())
        S.act(starts[:, :], p_ex[0:8, 16:24], ACT.Copy)
        m_starts = S.mark()
        TE.wait(m_starts)
        TE.mm(p_ex[0:1, 16:24], ohcol[:, :], starts[:, :], True, True)
        TE.mm(p_ex[0:1, 24:32], ohcol[:, :], cstat[:, :], True, True)
        TE.minc()
        S.wait(TE.mark())
        S.act(myrow[:, 0:8], p_ex[0:1, 16:24], ACT.Copy)
        S.act(myrow[:, 8:16], p_ex[0:1, 24:32], ACT.Copy)
        m_myrow = S.mark()

        V.wait(m_starts)
        V.tt(scr8[:, :], starts[:, :], oh128[0:8, :], ALU.mult)
        V.reduce(sm8[:, :], scr8[:, :], ALU.add)
        ohb = oh128[:, 0:NB].rearrange("p (a e) -> p a e", a=1) \
            .to_broadcast([128, 256, NB])
        V.tt(selscr8[:, :, :], selmsk[:, :, :], ohb, ALU.mult)
        V.reduce(selscr[:, :], selscr8[:, :, :], ALU.add)
        V.tt(selscr8[:, :, :], selsc[:, :, :], ohb, ALU.mult)
        V.reduce(csc[:, :], selscr8[:, :, :], ALU.add)
        V.ts(cid[:, :], tid[:, :], 1.0, 2.0, ALU.mult, ALU.add)
        V.tt(cid[:, :], cid[:, :], selscr[:, :], ALU.mult)
        V.ts(cid[:, :], cid[:, :], 1.0, -1.0, ALU.mult, ALU.add)
        V.ts(csc[:, :], csc[:, :], 1.0, 1.0, ALU.mult, ALU.add)
        V.tt(csc[:, :], csc[:, :], selscr[:, :], ALU.mult)
        V.ts(csc[:, :], csc[:, :], 1.0, -1.0, ALU.mult, ALU.add)
        m_cands = V.mark()

        TE.wait(m_cands)
        for ncol in range(4):
            if ncol > 0:
                TE.wait(S.mark())
            TE.mm(p_mm0[0:16, :], h16[:, :],
                  selmb[:, :, :].rearrange("p f e -> p (f e)")
                  [:, ncol * 512:(ncol + 1) * 512], True, True)
            TE.minc()
            S.wait(TE.mark())
            S.act(lm16[:, :, :].rearrange("p f e -> p (f e)")
                  [:, ncol * 512:(ncol + 1) * 512], p_mm0[0:16, :], ACT.Copy)
        m_lm16 = S.mark()
        return dict(cands=m_cands, myrow=m_myrow, lm16=m_lm16)

    def emit_merge(msel):
        G.lib(LC.sparse_gather)
        G.wait(msel["cands"])
        m_prev_v = None
        m_prev_mg = None
        for u in range(NB):
            G.wait(m_prev_v)
            G.dma(sg_in[:, :], cid[16 * u:16 * u + 16, :])
            G.dma(sg_in2[:, :], csc[16 * u:16 * u + 16, :])
            m_mv = G.dmark()
            G.wait(m_mv)
            G.sg(sg_out[:, :], sg_in[:, :], sg_nf[:, :])
            m_sg1 = G.mark()
            V.wait(m_sg1)
            V.copy(nfb[:, :], sg_nf[:, :])
            TE.wait(V.mark())
            TE.mm(p_ex[0:16, 16:17], onesr[0:1, 0:16], nfb[0:1, 0:1],
                  True, True)
            TE.mm(p_ex[0:16, 17:18], selu16[:, 16 * u:16 * u + 16],
                  sm8[:, :], True, True)
            TE.minc()
            V.wait(TE.mark())
            V.wait(m_prev_mg)
            V.ts(m48[:, :], iota48w[:, :], p_ex[0:16, 16:17], None, ALU.is_lt)
            V.tt(sg_ids[:, :], sg_out[:, :], m48[:, :], ALU.mult)
            V.ts(mcol[:, 0:1], p_ex[0:16, 17:18], 1.0)
            for kpow in (2048.0, 1024.0, 512.0, 256.0, 128.0, 64.0, 32.0,
                         16.0):
                V.ts(mcol[:, 1:2], mcol[:, 0:1], kpow, None, ALU.is_ge)
                V.stt(mcol[:, 0:1], mcol[:, 1:2], -kpow, mcol[:, 0:1],
                      ALU.mult, ALU.add)
            V.tt(mcol[:, 1:2], p_ex[0:16, 17:18], mcol[:, 0:1], ALU.subtract)
            V.ts(mcol[:, 1:2], mcol[:, 1:2], 1.0 / 16)
            V.tt(mcol[:, 2:3], mcol[:, 0:1], iotap16[:, :], ALU.add)
            V.ts(mcol[:, 3:4], mcol[:, 2:3], 16.0, None, ALU.is_ge)
            V.stt(mcol[:, 2:3], mcol[:, 3:4], -16.0, mcol[:, 2:3],
                  ALU.mult, ALU.add)
            V.tt(mcol[:, 3:4], mcol[:, 1:2], mcol[:, 3:4], ALU.add)
            V.stt(mcol[:, 1:2], mcol[:, 2:3], float(WPAD), mcol[:, 3:4],
                  ALU.mult, ALU.add)
            V.copy(idxcol[0:16, 0:1], mcol[:, 1:2])
            m_idx = V.mark()
            G.wait(m_idx)
            G.imerge(w_ids.ap().rearrange("(a r) c -> a (r c)", a=1),
                     idxcol[0:16, 0:1], sg_ids[:, :])
            G.sg(sg_outf[:, :], sg_in2[:, :], sg_nf[:, :])
            m_sg2 = G.mark()
            V.wait(m_sg2)
            V.tt(sg_scv[:, :], sg_outf[:, :], m48[:, :], ALU.mult)
            m_scv = V.mark()
            G.wait(m_scv)
            G.imerge(w_sc.ap().rearrange("(a r) c -> a (r c)", a=1),
                     idxcol[0:16, 0:1], sg_scv[:, :])
            m_prev_v = V.mark()
            m_prev_mg = G.dmark()
        return G.dmark()

    def emit_owner_lists(msel, m_merge):
        V.wait(msel["lm16"])
        V.wait(msel["myrow"])
        V.wait(m_merge)
        for e in range(NB):
            V.ts(lsl[:, :], iotaloc[:, :], 1.0, 2.0, ALU.mult, ALU.add)
            V.tt(lsl[:, :], lsl[:, :], lm16[:, :, e], ALU.mult)
            V.ts(lsl[:, :], lsl[:, :], 1.0, -1.0, ALU.mult, ALU.add)
            m_c = V.mark()
            G.wait(m_c)
            G.sg(sg_out[:, :], lsl[:, :], sg_nf[:, :])
            m_sg = G.mark()
            V.wait(m_sg)
            V.copy(nfb[:, :], sg_nf[:, :])
            TE.wait(V.mark())
            TE.mm(p_ex[0:16, 16:17], onesr[0:1, 0:16], nfb[0:1, 0:1],
                  True, True)
            TE.mm(p_ex[0:16, 17:18], onesr[0:1, 0:16], myrow[0:1, e:e + 1],
                  True, True)
            TE.minc()
            V.wait(TE.mark())
            V.ts(m48[:, :], iota48w[:, :], p_ex[0:16, 16:17], None, ALU.is_lt)
            V.tt(sg_scv[:, :], sg_out[:, :], m48[:, :], ALU.mult)
            V.ts(sg_scv[:, :], sg_scv[:, :], 1.0, -1.0, ALU.mult, ALU.add)
            V.copy(ow_s[0:16, e, :], sg_scv[:, :])
            V.ts(sg_scv[:, :], iota48w[:, :], p_ex[0:16, 17:18],
                 float(e * CAP) + 1.0, ALU.add, ALU.add)
            V.tt(sg_scv[:, :], sg_scv[:, :], m48[:, :], ALU.mult)
            V.ts(sg_scv[:, :], sg_scv[:, :], 1.0, -1.0, ALU.mult, ALU.add)
            V.copy(ow_g[0:16, e, :], sg_scv[:, :])
        ohf = owcnt_f.rearrange("a (e h) -> a e h", h=2)
        V.ts(ohf[:, :, 0], myrow[0:1, 8:16], 384.0, None, ALU.min)
        V.tt(ohf[:, :, 1], myrow[0:1, 8:16], ohf[:, :, 0], ALU.subtract)
        V.copy(owcnt[:, :], owcnt_f[:, :])
        m_ow0 = V.mark()
        Q.wait(m_ow0)
        Q.dma(ow_s[16:32, :, :], ow_s[0:16, :, :])
        Q.dma(ow_g[16:32, :, :], ow_g[0:16, :, :])
        Q.wait(Q.dmark())
        Q.dma(ow_s[32:64, :, :], ow_s[0:32, :, :])
        Q.dma(ow_g[32:64, :, :], ow_g[0:32, :, :])
        Q.wait(Q.dmark())
        Q.dma(ow_s[64:128, :, :], ow_s[0:64, :, :])
        Q.dma(ow_g[64:128, :, :], ow_g[0:64, :, :])
        return Q.dmark()

    def emit_gl(m_merge, scrow, scband):
        Q.wait(m_merge)
        Q.dma(gl32[:, :], w_ids.ap()[:, 0:256])
        V.wait(Q.dmark())
        V.ts(glf[:, :], gl32[:, :], 1.0, -1.0, ALU.mult, ALU.add)
        V.copy(gl_idx[0:16, :], glf[:, :])
        m_gl0 = V.mark()
        Q.wait(m_gl0)
        Q.dma(gl_idx[16:32, :], gl_idx[0:16, :])
        Q.wait(Q.dmark())
        Q.dma(gl_idx[32:64, :], gl_idx[0:32, :])
        Q.wait(Q.dmark())
        Q.dma(gl_idx[64:128, :], gl_idx[0:64, :])
        m_gl = Q.dmark()
        for rr in range(16):
            Q.dma(lin_sc.ap().rearrange("a (c r) -> (a c) r", r=16)[:, rr:rr + 1],
                  w_sc.ap()[rr:rr + 1, 0:256].rearrange("r c -> c r"),
                  allow_slow_non_contiguous=True)
        m_lin = Q.dmark()
        ret_lin = m_lin
        G.lib(LC.mlp)
        G.wait(m_lin)
        m_scrow = G.dma(scrow[:, :], lin_sc.ap())
        G.wait(m_scrow)
        G.pbcast(scband[:, :], scrow[:, :], 128)
        return dict(gl=m_gl, scband=G.mark(), lin=ret_lin)

    def emit_ffn(mrg, m_ag_g, fset, m_eo_free):
        fw1, fw3, fw2db, rt, hbuf, sil, outt, scrow, scband = fset
        mw = G.dmark()
        G.dma(fw1[:, :, :, :],
              w1_in.ap().rearrange("(ct p) (ht j) -> p ct ht j", p=128, j=128))
        G.dma(fw3[:, :, :, :],
              w3_in.ap().rearrange("(ct p) (ht j) -> p ct ht j", p=128, j=128))
        m_w13 = G.dmark()
        G.wait(mrg["gl"])
        G.wait(m_ag_g)
        G.dma(fw2db[:, :, :], w2_in.ap().rearrange("(ht p) d -> p ht d", p=128))
        m_w2 = G.dmark()
        gmk = [None] * 4
        rt_free = None
        m_eo_prev = None
        for k in range(4):
            G.wait(rt_free)
            gmk[k] = G.gather(rt[:, :, :],
                              g_all.ap(), gl_idx[:, k * 64:(k + 1) * 64],
                              1024, C, transpose=True)
            V.wait(gmk[k])
            V.wait(mrg["scband"])
            V.tt(rt[:, :, :], rt[:, :, :],
                 scband[:, k * 1024:(k + 1) * 1024]
                 .rearrange("p (a b) -> p a b", a=1)
                 .to_broadcast([128, NCT, 1024]), ALU.mult)
            m_rt = V.mark()

            for n in range(2):
                kk = 2 * k + n
                TE.wait(m_rt)
                TE.wait(S.mark())
                TE.wait(m_w13)
                vm = [None] * NHT
                for ht in range(NHT):
                    pmu = p_mm0 if ht % 2 == 0 else p_mm2
                    pmv = p_mm1 if ht % 2 == 0 else p_mm3
                    if ht >= 2:
                        TE.wait(vm[ht - 2])
                    for ct in range(NCT):
                        TE.mm(pmu[:, :], fw1[:, ct, ht, :],
                              rt[:, ct, n * 512:(n + 1) * 512],
                              ct == 0, ct == NCT - 1)
                    for ct in range(NCT):
                        TE.mm(pmv[:, :], fw3[:, ct, ht, :],
                              rt[:, ct, n * 512:(n + 1) * 512],
                              ct == 0, ct == NCT - 1)
                    TE.minc()
                    m_u = TE.mark()
                    S.wait(m_u)
                    S.act(sil[:, ht % 2, :], pmu[:, :], ACT.Silu)
                    m_sil = S.mark()
                    V.wait(m_sil)
                    V.stt(hbuf[:, ht, :], pmv[:, :], 0.5, sil[:, ht % 2, :],
                          ALU.mult, ALU.mult)
                    vm[ht] = V.mark()
                m_h = V.mark()
                TE.wait(m_h)
                TE.wait(m_w2)
                sm4 = [None] * 4
                for st4 in range(4):
                    pma = p_mm2 if st4 % 2 == 0 else p_mm0
                    pmb = p_mm3 if st4 % 2 == 0 else p_mm1
                    if st4 >= 2:
                        TE.wait(sm4[st4 - 2])
                    if st4 == 0:
                        S.wait(m_eo_prev)
                    for dn in range(2):
                        pm = pma if dn == 0 else pmb
                        for ht in range(NHT):
                            TE.mm(pm[:, :],
                                  hbuf[:, ht, st4 * 128:(st4 + 1) * 128],
                                  fw2db[:, ht, dn * 512:(dn + 1) * 512],
                                  ht == 0, ht == NHT - 1)
                    TE.minc()
                    m_o = TE.mark()
                    S.wait(m_o)
                    S.act(outt[:, st4, 0:512], pma[:, :], ACT.Copy)
                    S.act(outt[:, st4, 512:1024], pmb[:, :], ACT.Copy)
                    sm4[st4] = S.mark()
                m_ev = S.mark()
                Q.wait(m_ev)
                if kk == 0:
                    Q.wait(m_eo_free)
                Q.dma(eo_mine.ap().rearrange("(a p) d -> p a d", p=128)
                      [:, kk * 4:(kk + 1) * 4, :], outt[:, :, :])
                m_eo_prev = Q.dmark()
            rt_free = TE.mark()
        m_eo = Q.dmark()
        G.wait(m_eo)
        m_ag_eo = G.cc("AllGather", ALU.bypass, [eo_mine.ap().opt()],
                       [eo_all.ap().opt()], s_cc, ccs)
        return m_ag_eo

    def emit_combine(m_ag_eo, m_owner, target_ap, m_target_ready, cset):
        cmb_b, cmb_f = cset
        G.wait(m_ag_eo)
        G.wait(m_owner)
        G.wait(m_target_ready)
        prev_cast = None
        for e in range(NB):
            for hhalf in range(2):
                G.wait(prev_cast)
                mg = G.gather(cmb_b[:, :, :], eo_all.ap(),
                              ow_g[:, e, hhalf * 24:(hhalf + 1) * 24],
                              384, C, transpose=False,
                              count_src=owcnt[0:1, 2 * e + hhalf:
                                              2 * e + hhalf + 1])
                V.wait(mg)
                V.copy(cmb_f[:, :, :].rearrange("p a b -> p (a b)"),
                       cmb_b[:, :, :].rearrange("p a b -> p (a b)"))
                m_cast = V.mark()
                prev_cast = m_cast
                G.wait(m_cast)
                G.scatter_add(target_ap, cmb_f[:, :, :],
                              ow_s[:, e, hhalf * 24:(hhalf + 1) * 24],
                              384, C,
                              count_src=owcnt[0:1, 2 * e + hhalf:
                                              2 * e + hhalf + 1])
        return G.dmark()

    sel_ctx = ExitStack()
    selset = (sb("selsc", [128, 256, NB], ctx=sel_ctx),
              sb("selmsk", [128, 256, NB], ctx=sel_ctx),
              sb("seltie", [128, 256, NB], ctx=sel_ctx),
              sb("selmb", [128, 256, NB], BF16, ctx=sel_ctx),
              sb("selscr", [128, 256], ctx=sel_ctx),
              sb("selscr8", [128, 256, NB], ctx=sel_ctx))
    msel1 = emit_selection(m_ag_sc1, selset)
    m_merge1 = emit_merge(msel1)
    m_owner1 = emit_owner_lists(msel1, m_merge1)
    sel_ctx.close()

    ffn_ctx = ExitStack()
    fset1 = (sb("fw1", [128, NCT, NHT, 128], BF16, ctx=ffn_ctx),
             sb("fw3", [128, NCT, NHT, 128], BF16, ctx=ffn_ctx),
             sb("fw2db", [128, NHT, 1024], BF16, ctx=ffn_ctx),
             sb("rt", [128, NCT, 1024], BF16, ctx=ffn_ctx),
             sb("hbuf", [128, NHT, 512], BF16, ctx=ffn_ctx),
             sb("sil", [128, 2, 512], BF16, ctx=ffn_ctx),
             sb("outt", [128, 4, 1024], BF16, ctx=ffn_ctx),
             sb("scrow", [1, CAP], BF16, ctx=ffn_ctx),
             sb("scband", [128, CAP], BF16, ctx=ffn_ctx))
    mgl1 = emit_gl(m_merge1, fset1[7], fset1[8])
    mrg1 = dict(gl=mgl1["gl"], scband=mgl1["scband"])
    m_ag_eo1 = emit_ffn(mrg1, m_ag_g1, fset1, None)
    ffn_ctx.close()
    cmb_ctx = ExitStack()
    cset1 = (sb("cmb_b", [128, 3, 1024], BF16, ctx=cmb_ctx),
             sb("cmb_f", [128, 3, 1024], ctx=cmb_ctx))
    m_cmb1 = emit_combine(m_ag_eo1, m_owner1, x2_plane.ap(), Q.dmark(), cset1)
    cmb_ctx.close()

    attn_ctx = ExitStack()
    wk4 = sb("wk4", [128, NCT, NCT, 128], BF16, ctx=attn_ctx)
    wv4 = sb("wv4", [128, NCT, NCT, 128], BF16, ctx=attn_ctx)
    wr4 = sb("wr4", [128, NCT, NCT, 128], BF16, ctx=attn_ctx)
    wo4 = sb("wo4", [128, NCT, 1024], BF16, ctx=attn_ctx)
    xt34 = sb("xt34", [128, 4, 1024], ctx=attn_ctx)
    g2cm = sb("g2cm", [128, NCT, CH + 1], BF16, ctx=attn_ctx)
    mixb = sb("mixb", [128, 3, CH], BF16, ctx=attn_ctx)
    xk = sb("xk", [128, NCT, CH], BF16, ctx=attn_ctx)
    xv = sb("xv", [128, NCT, CH], BF16, ctx=attn_ctx)
    xr = sb("xr", [128, NCT, CH], BF16, ctx=attn_ctx)
    kd = sb("kd", [128, CH], ctx=attn_ctx)
    vdd = sb("vdd", [128, CH], ctx=attn_ctx)
    ad = sb("ad", [128, CH], ctx=attn_ctx)
    avd = sb("avd", [128, CH], ctx=attn_ctx)
    srd = sb("srd", [128, CH], BF16, ctx=attn_ctx)
    numb = sb("numb", [128, CH], ctx=attn_ctx)
    denb = sb("denb", [128, CH], ctx=attn_ctx)
    yb = sb("yb", [128, CH], ctx=attn_ctx)
    zb = sb("zb", [128, NCT, CH], BF16, ctx=attn_ctx)
    pbuf = sb("pbuf", [128, CH + 1], ctx=attn_ctx)
    qbuf = sb("qbuf", [128, CH + 1], ctx=attn_ctx)
    pcar = sb("pcar", [128, NCT], ctx=attn_ctx)
    qcar = sb("qcar", [128, NCT], ctx=attn_ctx)
    ddfull = sb("ddfull", [128, NCT, CH], ctx=attn_ctx)

    G.lib(LC.standard)
    G.wait(m_cmb1)
    G.wait(TE.mark())
    G.wait(S.mark())
    G.wait(V.mark())
    G.dma(wk4[:, :, :, :],
          wkT_in.ap().rearrange("(ct p) (dt j) -> p ct dt j", p=128, j=128))
    G.dma(wv4[:, :, :, :],
          wvT_in.ap().rearrange("(ct p) (dt j) -> p ct dt j", p=128, j=128))
    G.dma(wr4[:, :, :, :],
          wrT_in.ap().rearrange("(ct p) (dt j) -> p ct dt j", p=128, j=128))
    G.dma(wo4[:, :, :],
          woT_in.ap().rearrange("(ct p) d -> p ct d", p=128))
    m_aw = G.dmark()

    V.wait(m_dvec)
    V.wait(m_aw)
    for dt in range(NCT):
        V.ts(ddfull[:, dt, :], dvec[:, dt, 0:1].to_broadcast([128, CH]), 1.0)
    V.memset(pcar[:, :], 0.0)
    V.memset(qcar[:, :], 0.0)
    V.memset(g2cm[:, :, 0:1].rearrange("p a b -> p (a b)"), 0.0)
    m_attn_init = V.mark()

    m_x2_ready = m_cmb1
    prev_xk_cons = None
    prev_xt_cons = None
    for c in range(NCH):
        Q.wait(m_x2_ready)
        Q.wait(prev_xt_cons)
        mx = Q.dma(xt34[:, :, :],
                   x2_plane.ap()[c * CH:(c + 1) * CH, :]
                   .rearrange("(a p) d -> p a d", p=128))
        if c > 0:
            V.copy(g2cm[:, :, 0:1].rearrange("p a b -> p (a b)"),
                   g2cm[:, :, CH:CH + 1].rearrange("p a b -> p (a b)"))
        for tt in range(4):
            V.wait(mx)
            S.wait(mx)
            V.wait(m_attn_init)
            V.ts(sq_scr[:, :], xt34[:, tt, :], 1.0, None, ALU.mult, ALU.add,
                 accum=stats[:, 0:1])
            S.act(sq_scr2[:, :], xt34[:, tt, :], ACT.Square,
                  accum=stats[:, 1:2])
            m_sq = S.mark()
            V.wait(m_sq)
            V.drain()
            V.ts(stats[:, 2:3], stats[:, 0:1], -1.0 / C)
            V.tt(stats[:, 4:5], stats[:, 2:3], stats[:, 2:3], ALU.mult)
            V.ts(stats[:, 6:7], stats[:, 1:2], 1.0 / C, EPS, ALU.mult,
                 ALU.add)
            V.tt(stats[:, 3:4], stats[:, 6:7], stats[:, 4:5], ALU.subtract)
            V.recip(stats[:, 7:8], stats[:, 3:4])
            m_rc = V.mark()
            S.wait(m_rc)
            S.act(stats[:, 8:9], stats[:, 7:8], ACT.Sqrt)
            S.nop()
            S.act(stats[:, 9:10], stats[:, 2:3], ACT.Identity,
                  scale=stats[:, 8:9])
            m_bias = S.mark()
            V.wait(m_bias)
            V.wait(lg_prev["tr"])
            V.ts(gb[:, tt % 2, :], xt34[:, tt, :], stats[:, 8:9],
                 stats[:, 9:10], ALU.mult, ALU.add)
            m_g2 = V.mark()
            TE.wait(m_g2)
            TE.wait(prev_xk_cons)
            for h in range(2):
                pt = p_tr if h == 0 else p_tr2
                for j in range(4):
                    ct = h * 4 + j
                    TE.trans(pt[:, j * 128:(j + 1) * 128],
                             gb[:, tt % 2, ct * 128:(ct + 1) * 128],
                             ident[:, :])
            TE.minc()
            m_tr = TE.mark()
            S.wait(m_tr)
            for h in range(2):
                pt = p_tr if h == 0 else p_tr2
                for j in range(4):
                    ct = h * 4 + j
                    S.act(g2cm[:, ct, 1 + tt * 128:1 + (tt + 1) * 128],
                          pt[:, j * 128:(j + 1) * 128], ACT.Copy)
            lg_prev["tr"] = m_tr
        m_g2cm = S.mark()

        V.wait(m_g2cm)
        S.wait(m_g2cm)
        for ct in range(NCT):
            for mi, t1col in enumerate([5, 6, 7]):
                S.act(mixb[:, mi, :], g2cm[:, ct, 0:CH], ACT.Identity,
                      scale=chv[:, ct, t1col:t1col + 1])
            m_mx = S.mark()
            V.wait(m_mx)
            V.stt(xk[:, ct, :], g2cm[:, ct, 1:CH + 1], chv[:, ct, 2:3],
                  mixb[:, 0, :], ALU.mult, ALU.add)
            V.stt(xv[:, ct, :], g2cm[:, ct, 1:CH + 1], chv[:, ct, 3:4],
                  mixb[:, 1, :], ALU.mult, ALU.add)
            V.stt(xr[:, ct, :], g2cm[:, ct, 1:CH + 1], chv[:, ct, 4:5],
                  mixb[:, 2, :], ALU.mult, ALU.add)
            S.wait(V.mark())
        m_mix = V.mark()

        m_z = None
        m_prev_evac = None
        for dt in range(NCT):
            TE.wait(m_mix)
            TE.wait(m_aw)
            TE.wait(m_prev_evac)
            for ct in range(NCT):
                TE.mm(p_mm0[:, :], wk4[:, ct, dt, :], xk[:, ct, :],
                      ct == 0, ct == NCT - 1)
            for ct in range(NCT):
                TE.mm(p_mm1[:, :], wv4[:, ct, dt, :], xv[:, ct, :],
                      ct == 0, ct == NCT - 1)
            for ct in range(NCT):
                TE.mm(p_mm2[:, :], wr4[:, ct, dt, :], xr[:, ct, :],
                      ct == 0, ct == NCT - 1)
            TE.minc()
            m_kvr = TE.mark()
            S.wait(m_kvr)
            S.wait(V.mark())
            S.act(kd[:, :], p_mm0[:, :], ACT.Copy)
            S.act(vdd[:, :], p_mm1[:, :], ACT.Copy)
            S.act(srd[:, :], p_mm2[:, :], ACT.Sigmoid)
            S.nop()
            S.act(ad[:, :], kd[:, :], ACT.Exp)
            m_sc_dt = S.mark()
            m_prev_evac = m_sc_dt
            V.wait(m_sc_dt)
            V.tt(avd[:, :], ad[:, :], vdd[:, :], ALU.mult)
            V.copy(pbuf[:, 0:1], pcar[:, dt:dt + 1])
            V.copy(qbuf[:, 0:1], qcar[:, dt:dt + 1])
            V.scan(pbuf[:, 1:CH + 1], ddfull[:, dt, :], avd[:, :],
                   pbuf[:, 0:1])
            V.scan(qbuf[:, 1:CH + 1], ddfull[:, dt, :], ad[:, :],
                   qbuf[:, 0:1])
            V.copy(pcar[:, dt:dt + 1], pbuf[:, CH:CH + 1])
            V.copy(qcar[:, dt:dt + 1], qbuf[:, CH:CH + 1])
            V.stt(numb[:, :], avd[:, :], dvec[:, dt, 1:2], pbuf[:, 0:CH],
                  ALU.mult, ALU.add)
            V.stt(denb[:, :], ad[:, :], dvec[:, dt, 1:2], qbuf[:, 0:CH],
                  ALU.mult, ALU.add)
            V.recip(yb[:, :], denb[:, :])
            V.tt(yb[:, :], yb[:, :], numb[:, :], ALU.mult)
            V.tt(zb[:, dt, :], yb[:, :], srd[:, :], ALU.mult)
            m_z = V.mark()
        prev_xk_cons = m_z

        for tt in range(4):
            TE.wait(m_z)
            for dn in range(2):
                pm = p_mm2 if dn == 0 else p_mm3
                for ct in range(NCT):
                    TE.mm(pm[:, :], zb[:, ct, tt * 128:(tt + 1) * 128],
                          wo4[:, ct, dn * 512:(dn + 1) * 512],
                          ct == 0, ct == NCT - 1)
            TE.minc()
            m_wo = TE.mark()
            V.wait(m_wo)
            V.tt(xt34[:, tt, 0:512], xt34[:, tt, 0:512], p_mm2[:, :], ALU.add)
            V.tt(xt34[:, tt, 512:1024], xt34[:, tt, 512:1024], p_mm3[:, :],
                 ALU.add)
            m_x3 = V.mark()
            TE.wait(m_x3)
            Q.wait(m_x3)
            Q.dma(out_ext.ap()[(c * 4 + tt) * 128:(c * 4 + tt + 1) * 128, :],
                  xt34[:, tt, :])
            lngate(c * 4 + tt, xt34[:, tt, :], xwait=m_x3)
        prev_xt_cons = V.mark()
    m_attn_q = Q.dmark()
    m_attn_done = (V.mark(), TE.mark(), S.mark(), G.mark())
    attn_ctx.close()

    G.wait(m_attn_q)
    m_ag_sc2 = G.cc("AllGather", ALU.bypass, [sc_mine.ap().opt()],
                    [sc_all.ap().opt()], s_cc, ccs)
    m_ag_g2 = G.cc("AllGather", ALU.bypass, [g_mine.ap().opt()],
                   [g_all.ap().opt()], s_cc, ccs)

    sel_ctx2 = ExitStack()
    selset2 = (sb("selsc2", [128, 256, NB], ctx=sel_ctx2),
               sb("selmsk2", [128, 256, NB], ctx=sel_ctx2),
               sb("seltie2", [128, 256, NB], ctx=sel_ctx2),
               sb("selmb2", [128, 256, NB], BF16, ctx=sel_ctx2),
               sb("selscr2", [128, 256], ctx=sel_ctx2),
               sb("selscr82", [128, 256, NB], ctx=sel_ctx2))
    msel2 = emit_selection(m_ag_sc2, selset2,
                           m_wfree=(mgl1['gl'], mgl1['lin']))
    m_merge2 = emit_merge(msel2)
    m_owner2 = emit_owner_lists(msel2, m_merge2)
    sel_ctx2.close()

    ffn_ctx2 = ExitStack()
    fset2 = (sb("fw1b", [128, NCT, NHT, 128], BF16, ctx=ffn_ctx2),
             sb("fw3b", [128, NCT, NHT, 128], BF16, ctx=ffn_ctx2),
             sb("fw2db2", [128, NHT, 1024], BF16, ctx=ffn_ctx2),
             sb("rt2", [128, NCT, 1024], BF16, ctx=ffn_ctx2),
             sb("hbuf2", [128, NHT, 512], BF16, ctx=ffn_ctx2),
             sb("sil2", [128, 2, 512], BF16, ctx=ffn_ctx2),
             sb("outt2", [128, 4, 1024], BF16, ctx=ffn_ctx2),
             sb("scrow2", [1, CAP], BF16, ctx=ffn_ctx2),
             sb("scband2", [128, CAP], BF16, ctx=ffn_ctx2))
    mgl2 = emit_gl(m_merge2, fset2[7], fset2[8])
    mrg2 = dict(gl=mgl2["gl"], scband=mgl2["scband"])
    m_ag_eo2 = emit_ffn(mrg2, m_ag_g2, fset2, m_ag_eo1)
    ffn_ctx2.close()
    cmb_ctx2 = ExitStack()
    cset2 = (sb("cmb_b2", [128, 3, 1024], BF16, ctx=cmb_ctx2),
             sb("cmb_f2", [128, 3, 1024], ctx=cmb_ctx2))
    m_cmb2 = emit_combine(m_ag_eo2, m_owner2, out_ext.ap(), m_attn_q, cset2)
    cmb_ctx2.close()

    G.wait(m_cmb2)
    G.wait(Q.dmark())
    G.wait(V.mark())
    G.wait(S.mark())
    G.wait(TE.mark())

    globals()["_RECS"] = dict(Q=Q, V=V, S=S, TE=TE, G=G)
    with top, nc.Block() as block:
        @block.sync
        def _(e):
            Q.replay(e)

        @block.vector
        def _(e):
            V.replay(e)

        @block.scalar
        def _(e):
            S.replay(e)

        @block.tensor
        def _(e):
            TE.replay(e)

        @block.gpsimd
        def _(e):
            G.replay(e)

    return nc


_NC_CACHE = {}


def kernel(L0_attn_x, L0_attn_time_decay, L0_attn_time_first,
           L0_attn_time_mix_k, L0_attn_time_mix_v, L0_attn_time_mix_r,
           L0_attn_Wk, L0_attn_Wv, L0_attn_Wr, L0_attn_Wo,
           L0_ffn_wg, L0_ffn_w1, L0_ffn_w3, L0_ffn_w2):
    x = np.ascontiguousarray(np.asarray(L0_attn_x, dtype=np.float32))
    td = np.asarray(L0_attn_time_decay, np.float32).reshape(C)
    tf = np.asarray(L0_attn_time_first, np.float32).reshape(C)
    tmk = np.asarray(L0_attn_time_mix_k, np.float32).reshape(C)
    tmv = np.asarray(L0_attn_time_mix_v, np.float32).reshape(C)
    tmr = np.asarray(L0_attn_time_mix_r, np.float32).reshape(C)

    def chmaj(v):
        return v.reshape(NCT, 128).T

    chvec = np.ascontiguousarray(
        np.stack([chmaj(td), chmaj(tf), chmaj(tmk), chmaj(tmv), chmaj(tmr)],
                 axis=2).astype(np.float32))

    wkT = np.ascontiguousarray(np.asarray(L0_attn_Wk, np.float32).T)
    wvT = np.ascontiguousarray(np.asarray(L0_attn_Wv, np.float32).T)
    wrT = np.ascontiguousarray(np.asarray(L0_attn_Wr, np.float32).T)
    woT = np.ascontiguousarray(np.asarray(L0_attn_Wo, np.float32).T)
    wg = np.ascontiguousarray(np.asarray(L0_ffn_wg, np.float32))
    w1 = np.asarray(L0_ffn_w1, np.float32)
    w3 = np.asarray(L0_ffn_w3, np.float32)
    w2 = np.asarray(L0_ffn_w2, np.float32)

    identbf = np.eye(128, dtype=ml_dtypes.bfloat16)
    p = np.arange(128)
    g16c = (p[:, None] // 16 == np.arange(NB)[None, :]).astype(np.float32)
    h16c = (p[:, None] % 16 == np.arange(16)[None, :]).astype(ml_dtypes.bfloat16)
    tric = (np.arange(NB)[:, None] < np.arange(NB)[None, :]).astype(np.float32)
    onesrow = np.ones((1, 128), np.float32)
    ones128c = np.ones((128, 1), np.float32)
    tidbase = (4096.0 * (p // 16) + (p % 16)).astype(np.float32).reshape(128, 1)
    selu = np.zeros((NB, 128), np.float32)
    for u in range(NB):
        selu[u, 16 * u:16 * u + 16] = 1.0

    in_maps = []
    for r in range(NB):
        rkvc = np.zeros((1, 16), np.float32)
        rkvc[0, r] = 1.0
        rkvc[0, 8] = r
        rkvc[0, 9] = 4096.0 * r
        ohcolc = np.zeros((NB, 1), np.float32)
        ohcolc[r, 0] = 1.0
        in_maps.append({
            "x": np.ascontiguousarray(x[r]),
            "wkT": wkT, "wvT": wvT, "wrT": wrT, "woT": woT,
            "w1": np.ascontiguousarray(w1[r]),
            "w3": np.ascontiguousarray(w3[r]),
            "w2": np.ascontiguousarray(w2[r]),
            "wg": wg,
            "chvec": chvec,
            "rkv": rkvc,
            "ohcol": ohcolc,
            "identbf": identbf,
            "g16": g16c,
            "h16": h16c,
            "triexcl": tric,
            "onesrow": onesrow,
            "ones128": ones128c,
            "tidbase": tidbase,
            "selu16": selu,
            "identf8": np.eye(NB, dtype=np.float32),
        })

    if "prog" not in _NC_CACHE:
        nc = build_program()
        nc.finalize()
        _NC_CACHE["prog"] = nc
    nc = _NC_CACHE["prog"]
    res = run_bass_kernel_spmd(nc, in_maps, core_ids=list(range(NB)))
    out = np.stack([np.asarray(res.results[r]["out"], np.float32)
                    for r in range(NB)], axis=0)
    return out


# version 09
# speedup vs baseline: 1.0705x; 1.0081x over previous
import numpy as np
import ml_dtypes
from contextlib import ExitStack

import concourse.bacc as bacc
import concourse.mybir as mybir
from concourse.bass import IndirectOffsetOnAxis
from concourse.bass_utils import run_bass_kernel_spmd
from concourse import library_config as LC

F32 = mybir.dt.float32
BF16 = mybir.dt.bfloat16
I16 = mybir.dt.int16
I32 = mybir.dt.int32
U32 = mybir.dt.uint32
ALU = mybir.AluOpType
ACT = mybir.ActivationFunctionType
AXX = mybir.AxisListType

NB = 8
T = 4096
NT = NB * T
C = 1024
NCT = 8
HID = 2048
NHT = 16
CAP = 4096
EPS = 1e-5
VAL_ITERS = 45
TIE_ITERS = 16
WPAD = 304
CH = 512
NCH = T // CH


class Rec:
    def __init__(self, name, sem, dsem):
        self.name = name
        self.sem = sem
        self.dsem = dsem
        self.q = []
        self.n = 0
        self.nd = 0
        self.last_out = set()
        self.recent = []
        self.waited = {}
        self.ev = []

    def wait(self, mark):
        if mark is None:
            return
        sem, count = mark
        if count <= 0:
            return
        key = id(sem)
        if self.waited.get(key, -1) >= count:
            return
        self.waited[key] = count
        self.ev.append(("w", id(sem), count, len(self.q)))
        self.q.append(lambda e: e.wait_ge(sem, count))

    def mark(self):
        return (self.sem, self.n)

    def dmark(self):
        return (self.dsem, self.nd)

    @staticmethod
    def _names(aps):
        out = set()
        for ap in aps:
            if ap is None or isinstance(ap, (int, float)):
                continue
            t = getattr(ap, "tensor", None)
            if t is not None:
                out.add(t.name)
        return out

    def _hz(self, ins, outs):
        if self.name not in ("vector", "scalar", "gpsimd"):
            return
        rd = self._names(ins)
        wr = self._names(outs)
        rws = set()
        rds = set()
        for (r, w) in self.recent:
            rws |= w
            rds |= r
        if (rd & rws) or (wr & rws) or (wr & rds):
            if self.name == "vector":
                self.q.append(lambda e: e.drain())
            else:
                sem, n = self.sem, self.n
                self.ev.append(("w", id(sem), n, len(self.q)))
                self.q.append(lambda e: e.wait_ge(sem, n))
            self.recent = []
        self.recent = (self.recent + [(rd, wr)])[-3:]

    def drain(self):
        self.q.append(lambda e: e.drain())
        self.recent = []

    def _inc(self, call):
        sem = self.sem
        self.ev.append(("i", id(sem), 1, len(self.q)))
        self.q.append(lambda e: call(e).then_inc(sem, 1))
        self.n += 1

    def ts(self, out, in0, s1, s2=None, op0=ALU.mult, op1=None, accum=None):
        self._hz([in0, s1, s2], [out, accum])
        kw = dict(out=out, in0=in0, scalar1=s1, scalar2=s2, op0=op0)
        if op1 is not None:
            kw["op1"] = op1
        if accum is not None:
            kw["accum_out"] = accum
        self._inc(lambda e: e.tensor_scalar(**kw))

    def tt(self, out, in0, in1, op):
        self._hz([in0, in1], [out])
        self._inc(lambda e: e.tensor_tensor(out=out, in0=in0, in1=in1, op=op))

    def stt(self, out, in0, s, in1, op0, op1, accum=None):
        self._hz([in0, in1, s], [out, accum])
        kw = dict(out=out, in0=in0, scalar=s, in1=in1, op0=op0, op1=op1)
        if accum is not None:
            kw["accum_out"] = accum
        self._inc(lambda e: e.scalar_tensor_tensor(**kw))

    def copy(self, out, in_):
        self._hz([in_], [out])
        self._inc(lambda e: e.tensor_copy(out=out, in_=in_))

    def memset(self, ap, val):
        self._hz([], [ap])
        self._inc(lambda e: e.memset(ap, val))

    def recip(self, out, in_):
        self._hz([in_], [out])
        self._inc(lambda e: e.reciprocal(out=out, in_=in_))

    def reduce(self, out, in_, op, axis=AXX.X):
        self._hz([in_], [out])
        self._inc(lambda e: e.tensor_reduce(out=out, in_=in_, axis=axis, op=op))

    def scan(self, out, d0, d1, initial):
        self._hz([d0, d1, initial], [out])
        self._inc(lambda e: e.tensor_tensor_scan(
            out=out, data0=d0, data1=d1, initial=initial,
            op0=ALU.mult, op1=ALU.add))

    def act(self, out, in_, func, bias=None, scale=None, accum=None):
        self._hz([in_, bias, scale], [out, accum])
        kw = dict(out=out, in_=in_, func=func)
        if bias is not None:
            kw["bias"] = bias
        if scale is not None:
            kw["scale"] = scale
        if accum is not None:
            kw["accum_out"] = accum
        self._inc(lambda e: e.activation(**kw))

    def nop(self):
        self.q.append(lambda e: e.nop())

    def gdiv(self, out, in0, in1):
        self._inc(lambda e: e.tensor_tensor(out=out, in0=in0, in1=in1,
                                            op=ALU.divide))

    def mm(self, out, lhsT, rhs, start, stop):
        ent = {"k": "mm", "out": out, "lhsT": lhsT, "rhs": rhs,
               "start": start, "stop": stop, "inc": False, "sem": self.sem}
        self.q.append(ent)

    def trans(self, out, in_, ident):
        ent = {"k": "tr", "out": out, "in_": in_, "ident": ident,
               "inc": False, "sem": self.sem}
        self.q.append(ent)

    def minc(self):
        assert isinstance(self.q[-1], dict)
        self.q[-1]["inc"] = True
        self.ev.append(("i", id(self.sem), 1, len(self.q) - 1))
        self.n += 1

    def dma(self, out, in_, **kw):
        dsem = self.dsem
        self.wait(self.dmark())
        self.ev.append(("i", id(dsem), 16, len(self.q)))
        self.q.append(lambda e: e.dma_start(out=out, in_=in_, **kw)
                      .then_inc(dsem, 16))
        self.nd += 16
        if self.name in ("vector", "scalar", "gpsimd"):
            self.recent = (self.recent + [(set(), self._names([out]))])[-3:]
        return self.dmark()

    def sg(self, out, in_, nf):
        self._hz([in_], [out, nf])
        self._inc(lambda e: e.sparse_gather(out=out, in_=in_, num_found=nf))

    def lib(self, lib):
        self.q.append(lambda e: e.load_library(lib))

    def pbcast(self, out, in_, channels):
        self._hz([in_], [out])
        self._inc(lambda e: e.partition_broadcast(out_ap=out, in_ap=in_,
                                                  channels=channels))

    def gather(self, out, tab, idxs, n, elem, transpose=False, count_src=None):
        dsem = self.dsem
        self.wait(self.dmark())
        self.ev.append(("i", id(dsem), 16, len(self.q)))

        def emit(e):
            nreg = n if count_src is None else e.value_load(
                count_src, min_val=0, max_val=n)
            e.dma_gather(
                out_ap=out, in_ap=tab, idxs_ap=idxs, num_idxs=n,
                num_idxs_reg=nreg, elem_size=elem, transpose=transpose,
                single_packet=False).then_inc(dsem, 16)
        self.q.append(emit)
        self.nd += 16
        return self.dmark()

    def scatter_add(self, out, in_, idxs, n, elem, count_src=None):
        dsem = self.dsem
        self.wait(self.dmark())
        self.ev.append(("i", id(dsem), 16, len(self.q)))

        def emit(e):
            nreg = n if count_src is None else e.value_load(
                count_src, min_val=0, max_val=n)
            e.dma_scatter_add(
                out_ap=out, in_ap=in_, idxs_ap=idxs, num_idxs=n,
                num_idxs_reg=nreg, elem_size=elem,
                single_packet=False).then_inc(dsem, 16)
        self.q.append(emit)
        self.nd += 16
        return self.dmark()

    def imerge(self, out_ap, idx_ap, in_ap):
        dsem = self.dsem
        self.wait(self.dmark())
        self.ev.append(("i", id(dsem), 16, len(self.q)))
        self.q.append(lambda e: e.indirect_dma_start(
            out=out_ap, out_offset=IndirectOffsetOnAxis(ap=idx_ap, axis=1),
            in_=in_ap, in_offset=None,
            compute_op=ALU.add).then_inc(dsem, 16))
        self.nd += 16
        return self.dmark()

    def iota(self, out, pattern, base, cm):
        self._hz([], [out])
        self._inc(lambda e: e.iota(out=out, pattern=pattern, base=base,
                                   channel_multiplier=cm,
                                   allow_small_or_imprecise_dtypes=True))

    def cc(self, kind, op, ins, outs, ccsem, state):
        state[0] += 1
        count = state[0]
        self.ev.append(("i", id(ccsem), 1, len(self.q)))
        self.q.append(lambda e: e.collective_compute(
            kind, op, replica_groups=[list(range(NB))], ins=ins, outs=outs)
            .then_inc(ccsem, 1))
        return (ccsem, count)

    def replay(self, raw):
        for fn in self.q:
            if isinstance(fn, dict):
                if fn["k"] == "mm":
                    r = raw.matmul(out=fn["out"], lhsT=fn["lhsT"],
                                   rhs=fn["rhs"], start=fn["start"],
                                   stop=fn["stop"])
                else:
                    r = raw.transpose(out=fn["out"], in_=fn["in_"],
                                      identity=fn["ident"])
                if fn["inc"]:
                    r.then_inc(fn["sem"], 1)
            else:
                fn(raw)


def build_program():
    nc = bacc.Bacc("TRN2", target_bir_lowering=False)

    def ein(name, shape, dt=F32):
        return nc.dram_tensor(name, shape, dt, kind="ExternalInput")

    x_in = ein("x", [T, C])
    wkT_in = ein("wkT", [C, C]); wvT_in = ein("wvT", [C, C])
    wrT_in = ein("wrT", [C, C]); woT_in = ein("woT", [C, C])
    w1_in = ein("w1", [C, HID]); w3_in = ein("w3", [C, HID])
    w2_in = ein("w2", [HID, C])
    wg_in = ein("wg", [C, NB])
    chvec_in = ein("chvec", [128, NCT, 5])
    rkv_in = ein("rkv", [1, 16])
    ohcol_in = ein("ohcol", [NB, 1])
    ident_in = ein("identbf", [128, 128], BF16)
    g16_in = ein("g16", [128, NB])
    h16_in = ein("h16", [128, 16], BF16)
    tri_in = ein("triexcl", [NB, NB])
    onesr_in = ein("onesrow", [1, 128])
    ones128_in = ein("ones128", [128, 1])
    tidb_in = ein("tidbase", [128, 1])
    selu_in = ein("selu16", [NB, 128])
    identf8_in = ein("identf8", [NB, NB])
    out_ext = nc.dram_tensor("out", [T, C], F32, kind="ExternalOutput")

    def dram(name, shape, dt):
        return nc.dram_tensor(name, shape, dt)

    g_mine = dram("g_mine", [T, C], BF16)
    g_all = dram("g_all", [NT, C], BF16)
    sc_mine = dram("sc_mine", [16, 256 * NB], F32)
    sc_all = dram("sc_all", [128, 256 * NB], F32)
    eo_mine = dram("eo_mine", [CAP, C], BF16)
    eo_all = dram("eo_all", [NB * CAP, C], BF16)
    x2_plane = dram("x2_plane", [T, C], F32)
    w_ids = dram("w_ids", [16, WPAD], I32)
    w_sc = dram("w_sc", [16, WPAD], F32)
    lin_sc = dram("lin_sc", [1, CAP], F32)

    top = ExitStack()

    def sb(name, shape, dt=F32, ctx=None):
        return (ctx or top).enter_context(nc.sbuf_tensor(name, shape, dt))

    def mkpsum(name, shape, dt=F32):
        return top.enter_context(nc.psum_tensor(name, shape, dt))

    def mksem(name):
        return top.enter_context(nc.semaphore(name))

    ident = sb("ident_s", [128, 128], BF16)
    chv = sb("chv", [128, NCT, 8])
    dvec = sb("dvec", [128, NCT, 2])
    wgt = sb("wgt", [128, NCT, NB], BF16)
    rkv = sb("rkv_s", [1, 16])
    ohcol = sb("ohcol_s", [NB, 1])
    g16 = sb("g16_s", [128, NB])
    h16 = sb("h16_s", [128, 16], BF16)
    tri = sb("tri_s", [NB, NB])
    onesr = sb("onesr_s", [1, 128])
    ones128 = sb("ones128_s", [128, 1])
    oh128 = sb("oh128", [128, NB])
    w128 = sb("w128", [128, 1])
    tidb = sb("tidb_s", [128, 1])
    selu16 = sb("selu16_s", [NB, 128])
    identf8 = sb("identf8_s", [NB, NB])
    iota256 = sb("iota256", [128, 256])
    iota48w = sb("iota48w", [16, 48])
    iotap16 = sb("iotap16", [16, 1])
    iotaloc = sb("iotaloc", [16, 256])
    idxcol = sb("idxcol", [128, 1], U32)
    zero16 = sb("zero16", [16, WPAD])
    cid = sb("cid", [128, 256])
    csc = sb("csc", [128, 256])
    tid = sb("tid", [128, 256])
    bnd = sb("bnd", [1, 64])
    cnt = sb("cnt", [128, NB])
    cstat = sb("cstat", [NB, NB])
    starts = sb("starts", [NB, NB])
    sm8 = sb("sm8", [NB, 1])
    scr8 = sb("scr8", [NB, NB])
    myrow = sb("myrow", [1, 2 * NB])
    owcnt_f = sb("owcnt_f", [1, 2 * NB])
    owcnt = sb("owcnt", [1, 2 * NB], I32)
    lm16 = sb("lm16", [16, 256, NB])
    sg_in = sb("sg_in", [16, 256])
    sg_in2 = sb("sg_in2", [16, 256])
    sg_out = sb("sg_out", [16, 48], I32)
    sg_outf = sb("sg_outf", [16, 48])
    sg_nf = sb("sg_nf", [1, 1], U32)
    nfb = sb("nfb", [1, 1])
    m48 = sb("m48", [16, 48])
    sg_ids = sb("sg_ids", [16, 48])
    sg_scv = sb("sg_scv", [16, 48])
    mcol = sb("mcol", [16, 4])
    gl32 = sb("gl32", [16, 256], I32)
    glf = sb("glf", [16, 256])
    gl_idx = sb("gl_idx", [128, 256], I16)
    ow_g = sb("ow_g", [128, NB, 48], I16)
    ow_s = sb("ow_s", [128, NB, 48], I16)
    lsl = sb("lsl", [16, 256])
    stats = sb("stats", [128, 16])
    sq_scr = sb("sq_scr", [128, 1024])
    sq_scr2 = sb("sq_scr2", [128, 1024])
    gb = sb("gb", [128, 2, 1024], BF16)
    gcm = sb("gcm", [128, 2, NCT, 128], BF16)
    scem = sb("scem", [NB, 128])
    sct = sb("sct", [128, NB])
    p_ex = mkpsum("p_ex", [128, 32])
    p_tr = mkpsum("p_tr", [128, 512], BF16)
    p_tr2 = mkpsum("p_tr2", [128, 512], BF16)
    p_mm0 = mkpsum("p_mm0", [128, 512])
    p_mm1 = mkpsum("p_mm1", [128, 512])
    p_mm2 = mkpsum("p_mm2", [128, 512])
    p_mm3 = mkpsum("p_mm3", [128, 512])
    p_sc = mkpsum("p_sc", [128, 128])

    s_q = mksem("s_q"); s_qd = mksem("s_qd")
    s_v = mksem("s_v"); s_vd = mksem("s_vd")
    s_s = mksem("s_s"); s_sd = mksem("s_sd")
    s_t = mksem("s_t"); s_td = mksem("s_td")
    s_g = mksem("s_g"); s_gd = mksem("s_gd")
    s_cc = mksem("s_cc")

    Q = Rec("sync", s_q, s_qd)
    V = Rec("vector", s_v, s_vd)
    S = Rec("scalar", s_s, s_sd)
    TE = Rec("tensor", s_t, s_td)
    G = Rec("gpsimd", s_g, s_gd)
    ccs = [0]

    Q.dma(x2_plane.ap(), x_in.ap())
    for dst, src in [(ident[:, :], ident_in.ap()),
                     (chv[:, :, 0:5], chvec_in.ap()),
                     (rkv[:, :], rkv_in.ap()),
                     (ohcol[:, :], ohcol_in.ap()),
                     (g16[:, :], g16_in.ap()),
                     (h16[:, :], h16_in.ap()),
                     (tri[:, :], tri_in.ap()),
                     (onesr[:, :], onesr_in.ap()),
                     (ones128[:, :], ones128_in.ap()),
                     (tidb[:, :], tidb_in.ap()),
                     (selu16[:, :], selu_in.ap()),
                     (identf8[:, :], identf8_in.ap())]:
        G.dma(dst, src)
    G.dma(wgt[:, :, :], wg_in.ap().rearrange("(ct p) e -> p ct e", p=128))
    m_consts = G.dmark()
    G.iota(iota256[:, :], [[16, 256]], 0, 0)
    G.iota(iota48w[:, :], [[16, 48]], 0, 1)
    G.iota(iotap16[:, :], [[0, 1]], 0, 1)
    G.iota(iotaloc[:, :], [[16, 256]], 0, 1)

    V.wait(m_consts)
    V.memset(zero16[:, :], 0.0)
    V.ts(chv[:, :, 5:8], chv[:, :, 2:5], -1.0, 1.0, ALU.mult, ALU.add)

    S.wait(m_consts)
    S.act(stats[:, 0:8], chv[:, :, 0:1].rearrange("p a b -> p (a b)"), ACT.Exp)
    S.nop()
    S.act(dvec[:, :, 0:1].rearrange("p a b -> p (a b)"), stats[:, 0:8],
          ACT.Exp, scale=-1.0)
    S.act(dvec[:, :, 1:2].rearrange("p a b -> p (a b)"),
          chv[:, :, 1:2].rearrange("p a b -> p (a b)"), ACT.Exp)
    m_dvec = S.mark()

    TE.wait(m_consts)
    TE.mm(p_ex[:, 0:8], onesr[:, :], rkv[0:1, 0:8], True, True)
    TE.minc()
    S.wait(TE.mark())
    S.act(oh128[:, :], p_ex[:, 0:8], ACT.Copy)
    m_oh128 = S.mark()
    V.wait(m_oh128)
    V.tt(cnt[:, :], g16[:, :], oh128[:, :], ALU.mult)
    V.reduce(w128[:, :], cnt[:, :], ALU.add)
    m_w128 = V.mark()

    lg_prev = {"tr": None, "gstore": None, "mm": None, "sct_q": None}

    def lngate(i, xin_ap, xwait=None):
        if xwait is not None:
            V.wait(xwait)
            S.wait(xwait)
        V.ts(sq_scr[:, :], xin_ap, 1.0, None, ALU.mult, ALU.add,
             accum=stats[:, 0:1])
        S.act(sq_scr2[:, :], xin_ap, ACT.Square, accum=stats[:, 1:2])
        m_sq = S.mark()
        V.wait(m_sq)
        V.drain()
        V.ts(stats[:, 2:3], stats[:, 0:1], -1.0 / C)
        V.tt(stats[:, 4:5], stats[:, 2:3], stats[:, 2:3], ALU.mult)
        V.ts(stats[:, 6:7], stats[:, 1:2], 1.0 / C, EPS, ALU.mult, ALU.add)
        V.tt(stats[:, 3:4], stats[:, 6:7], stats[:, 4:5], ALU.subtract)
        V.recip(stats[:, 7:8], stats[:, 3:4])
        m_rc = V.mark()
        S.wait(m_rc)
        S.act(stats[:, 8:9], stats[:, 7:8], ACT.Sqrt)
        S.nop()
        S.act(stats[:, 9:10], stats[:, 2:3], ACT.Identity, scale=stats[:, 8:9])
        m_bias = S.mark()
        V.wait(m_bias)
        V.wait(lg_prev["tr"])
        V.wait(lg_prev["gstore"])
        par = i % 2
        V.ts(gb[:, par, :], xin_ap, stats[:, 8:9], stats[:, 9:10],
             ALU.mult, ALU.add)
        m_g = V.mark()
        Q.wait(m_g)
        gst = Q.dma(g_mine.ap()[i * 128:(i + 1) * 128, :], gb[:, par, :])
        TE.wait(m_g)
        TE.wait(lg_prev["mm"])
        for h in range(2):
            pt = p_tr if h == 0 else p_tr2
            for j in range(4):
                ct = h * 4 + j
                TE.trans(pt[:, j * 128:(j + 1) * 128],
                         gb[:, par, ct * 128:(ct + 1) * 128], ident[:, :])
        TE.minc()
        m_tr = TE.mark()
        S.wait(m_tr)
        S.act(gcm[:, par, 0:4, :].rearrange("p a b -> p (a b)"), p_tr[:, :],
              ACT.Copy)
        S.act(gcm[:, par, 4:8, :].rearrange("p a b -> p (a b)"), p_tr2[:, :],
              ACT.Copy)
        m_gcm = S.mark()
        TE.wait(m_gcm)
        TE.wait(lg_prev["sct_q"])
        for ct in range(NCT):
            TE.mm(p_sc[0:NB, :], wgt[:, ct, :], gcm[:, par, ct, :],
                  ct == 0, ct == NCT - 1)
        TE.minc()
        m_mm = TE.mark()
        S.wait(m_mm)
        S.act(scem[:, :], p_sc[0:NB, :], ACT.Copy)
        m_scem = S.mark()
        TE.wait(m_scem)
        TE.trans(p_ex[:, 8:16], scem[:, :], identf8[:, :])
        TE.minc()
        m_sct = TE.mark()
        S.wait(m_sct)
        S.act(sct[:, :], p_ex[:, 8:16], ACT.Exp)
        m_exp = S.mark()
        V.wait(m_exp)
        V.reduce(stats[:, 10:11], sct[:, :], ALU.add)
        V.recip(stats[:, 11:12], stats[:, 10:11])
        V.ts(sct[:, :], sct[:, :], stats[:, 11:12])
        m_sm = V.mark()
        Q.wait(m_sm)
        scq = Q.dma(sc_mine.ap().rearrange("q (f e) -> q f e", e=NB)
                    [:, 8 * i:8 * i + 8, :].rearrange("q w e -> w q e"),
                    sct[:, :])
        lg_prev["tr"] = m_tr
        lg_prev["gstore"] = gst
        lg_prev["mm"] = m_mm
        lg_prev["sct_q"] = scq

    p1ctx = ExitStack()
    xt2 = sb("xt2", [128, 2, 1024], ctx=p1ctx)
    ldm = [None] * 34
    ldm[0] = Q.dma(xt2[:, 0, :], x_in.ap()[0:128, :])
    ldm[1] = Q.dma(xt2[:, 1, :], x_in.ap()[128:256, :])
    for i in range(32):
        lngate(i, xt2[:, i % 2, :], xwait=ldm[i])
        if i + 2 < 32:
            Q.wait(V.mark())
            ldm[i + 2] = Q.dma(xt2[:, i % 2, :],
                               x_in.ap()[(i + 2) * 128:(i + 3) * 128, :])
    p1ctx.close()
    m_lg1_q = Q.dmark()

    G.wait(m_lg1_q)
    m_ag_sc1 = G.cc("AllGather", ALU.bypass, [sc_mine.ap().opt()],
                    [sc_all.ap().opt()], s_cc, ccs)
    m_ag_g1 = G.cc("AllGather", ALU.bypass, [g_mine.ap().opt()],
                   [g_all.ap().opt()], s_cc, ccs)

    def emit_selection(m_ag_sc, selset, m_wfree=()):
        selsc, selmsk, seltie, selmb, selscr, selscr8 = selset
        Q.wait(m_ag_sc)
        Q.dma(selsc[:, :, :],
              sc_all.ap().rearrange("p (f e) -> p f e", e=NB))
        m_sel_ld = Q.dmark()
        for mk in m_wfree:
            G.wait(mk)
        G.dma(w_ids.ap(), zero16[:, :])
        G.dma(w_sc.ap(), zero16[:, :])

        V.wait(m_sel_ld)
        V.ts(tid[:, :], iota256[:, :], tidb[:, 0:1], None, ALU.add)
        V.memset(bnd[:, 0:8], 0.0)
        V.memset(bnd[:, 8:16], 1.0)

        for it in range(VAL_ITERS):
            V.tt(bnd[:, 16:24], bnd[:, 0:8], bnd[:, 8:16], ALU.add)
            V.ts(bnd[:, 16:24], bnd[:, 16:24], 0.5)
            TE.wait(V.mark())
            TE.mm(p_ex[:, 16:24], onesr[:, :], bnd[0:1, 16:24], True, True)
            TE.minc()
            V.wait(TE.mark())
            for e in range(NB):
                V.ts(selscr[:, :], selsc[:, :, e], p_ex[:, 16 + e:17 + e],
                     None, ALU.is_gt, ALU.add, accum=cnt[:, e:e + 1])
            TE.wait(V.mark())
            TE.mm(p_ex[0:1, 24:32], ones128[:, :], cnt[:, :], True, True)
            TE.minc()
            V.wait(TE.mark())
            V.ts(bnd[:, 24:32], p_ex[0:1, 24:32], float(CAP), None, ALU.is_ge)
            V.tt(bnd[:, 32:40], bnd[:, 16:24], bnd[:, 0:8], ALU.subtract)
            V.tt(bnd[:, 40:48], bnd[:, 24:32], bnd[:, 32:40], ALU.mult)
            V.tt(bnd[:, 0:8], bnd[:, 0:8], bnd[:, 40:48], ALU.add)
            V.tt(bnd[:, 32:40], bnd[:, 8:16], bnd[:, 16:24], ALU.subtract)
            V.tt(bnd[:, 40:48], bnd[:, 24:32], bnd[:, 32:40], ALU.mult)
            V.tt(bnd[:, 8:16], bnd[:, 16:24], bnd[:, 40:48], ALU.add)

        TE.wait(V.mark())
        TE.mm(p_ex[:, 16:24], onesr[:, :], bnd[0:1, 8:16], True, True)
        TE.mm(p_ex[:, 24:32], onesr[:, :], bnd[0:1, 0:8], True, True)
        TE.minc()
        V.wait(TE.mark())
        for e in range(NB):
            V.ts(selmsk[:, :, e], selsc[:, :, e], p_ex[:, 16 + e:17 + e],
                 None, ALU.is_gt, ALU.add, accum=cnt[:, e:e + 1])
        for e in range(NB):
            V.ts(seltie[:, :, e], selsc[:, :, e], p_ex[:, 24 + e:25 + e],
                 None, ALU.is_gt)
        for e in range(NB):
            V.tt(seltie[:, :, e], seltie[:, :, e], selmsk[:, :, e],
                 ALU.subtract)
        TE.wait(V.mark())
        TE.mm(p_ex[0:1, 24:32], ones128[:, :], cnt[:, :], True, True)
        TE.minc()
        V.wait(TE.mark())
        V.ts(bnd[:, 48:56], p_ex[0:1, 24:32], -1.0, float(CAP),
             ALU.mult, ALU.add)
        V.memset(bnd[:, 0:8], 0.0)
        V.memset(bnd[:, 8:16], float(NT))

        for it in range(TIE_ITERS):
            V.tt(bnd[:, 16:24], bnd[:, 0:8], bnd[:, 8:16], ALU.add)
            V.ts(bnd[:, 16:24], bnd[:, 16:24], 0.5)
            TE.wait(V.mark())
            TE.mm(p_ex[:, 16:24], onesr[:, :], bnd[0:1, 16:24], True, True)
            TE.minc()
            V.wait(TE.mark())
            for e in range(NB):
                V.stt(selscr[:, :], tid[:, :], p_ex[:, 16 + e:17 + e],
                      seltie[:, :, e], ALU.is_lt, ALU.mult,
                      accum=cnt[:, e:e + 1])
            TE.wait(V.mark())
            TE.mm(p_ex[0:1, 24:32], ones128[:, :], cnt[:, :], True, True)
            TE.minc()
            V.wait(TE.mark())
            V.tt(bnd[:, 24:32], p_ex[0:1, 24:32], bnd[:, 48:56], ALU.is_ge)
            V.tt(bnd[:, 32:40], bnd[:, 16:24], bnd[:, 8:16], ALU.subtract)
            V.tt(bnd[:, 40:48], bnd[:, 24:32], bnd[:, 32:40], ALU.mult)
            V.tt(bnd[:, 8:16], bnd[:, 8:16], bnd[:, 40:48], ALU.add)
            V.tt(bnd[:, 32:40], bnd[:, 0:8], bnd[:, 16:24], ALU.subtract)
            V.tt(bnd[:, 40:48], bnd[:, 24:32], bnd[:, 32:40], ALU.mult)
            V.tt(bnd[:, 0:8], bnd[:, 16:24], bnd[:, 40:48], ALU.add)

        TE.wait(V.mark())
        TE.mm(p_ex[:, 16:24], onesr[:, :], bnd[0:1, 8:16], True, True)
        TE.minc()
        V.wait(TE.mark())
        for e in range(NB):
            V.stt(selscr[:, :], tid[:, :], p_ex[:, 16 + e:17 + e],
                  seltie[:, :, e], ALU.is_lt, ALU.mult)
            V.tt(selmsk[:, :, e], selmsk[:, :, e], selscr[:, :], ALU.add)
        for e in range(NB):
            V.ts(selscr[:, :], selmsk[:, :, e], 1.0, None, ALU.mult, ALU.add,
                 accum=cnt[:, e:e + 1])
        V.wait(m_w128)
        V.tt(selmb[:, :, :], selmsk[:, :, :],
             w128[:, 0:1].rearrange("p (a b) -> p a b", a=1)
             .to_broadcast([128, 256, NB]), ALU.mult)
        m_fmask = V.mark()
        TE.wait(m_fmask)
        TE.mm(p_ex[0:8, 24:32], g16[:, :], cnt[:, :], True, True)
        TE.minc()
        S.wait(TE.mark())
        S.act(cstat[:, :], p_ex[0:8, 24:32], ACT.Copy)
        TE.wait(S.mark())
        TE.mm(p_ex[0:8, 16:24], tri[:, :], cstat[:, :], True, True)
        TE.minc()
        S.wait(TE.mark())
        S.act(starts[:, :], p_ex[0:8, 16:24], ACT.Copy)
        m_starts = S.mark()
        TE.wait(m_starts)
        TE.mm(p_ex[0:1, 16:24], ohcol[:, :], starts[:, :], True, True)
        TE.mm(p_ex[0:1, 24:32], ohcol[:, :], cstat[:, :], True, True)
        TE.minc()
        S.wait(TE.mark())
        S.act(myrow[:, 0:8], p_ex[0:1, 16:24], ACT.Copy)
        S.act(myrow[:, 8:16], p_ex[0:1, 24:32], ACT.Copy)
        m_myrow = S.mark()

        V.wait(m_starts)
        V.tt(scr8[:, :], starts[:, :], oh128[0:8, :], ALU.mult)
        V.reduce(sm8[:, :], scr8[:, :], ALU.add)
        ohb = oh128[:, 0:NB].rearrange("p (a e) -> p a e", a=1) \
            .to_broadcast([128, 256, NB])
        V.tt(selscr8[:, :, :], selmsk[:, :, :], ohb, ALU.mult)
        V.reduce(selscr[:, :], selscr8[:, :, :], ALU.add)
        V.tt(selscr8[:, :, :], selsc[:, :, :], ohb, ALU.mult)
        V.reduce(csc[:, :], selscr8[:, :, :], ALU.add)
        V.ts(cid[:, :], tid[:, :], 1.0, 2.0, ALU.mult, ALU.add)
        V.tt(cid[:, :], cid[:, :], selscr[:, :], ALU.mult)
        V.ts(cid[:, :], cid[:, :], 1.0, -1.0, ALU.mult, ALU.add)
        V.ts(csc[:, :], csc[:, :], 1.0, 1.0, ALU.mult, ALU.add)
        V.tt(csc[:, :], csc[:, :], selscr[:, :], ALU.mult)
        V.ts(csc[:, :], csc[:, :], 1.0, -1.0, ALU.mult, ALU.add)
        m_cands = V.mark()

        TE.wait(m_cands)
        for ncol in range(4):
            if ncol > 0:
                TE.wait(S.mark())
            TE.mm(p_mm0[0:16, :], h16[:, :],
                  selmb[:, :, :].rearrange("p f e -> p (f e)")
                  [:, ncol * 512:(ncol + 1) * 512], True, True)
            TE.minc()
            S.wait(TE.mark())
            S.act(lm16[:, :, :].rearrange("p f e -> p (f e)")
                  [:, ncol * 512:(ncol + 1) * 512], p_mm0[0:16, :], ACT.Copy)
        m_lm16 = S.mark()
        return dict(cands=m_cands, myrow=m_myrow, lm16=m_lm16)

    def emit_merge(msel):
        G.lib(LC.sparse_gather)
        G.wait(msel["cands"])
        m_prev_v = None
        m_prev_mg = None
        for u in range(NB):
            G.wait(m_prev_v)
            G.dma(sg_in[:, :], cid[16 * u:16 * u + 16, :])
            G.dma(sg_in2[:, :], csc[16 * u:16 * u + 16, :])
            m_mv = G.dmark()
            G.wait(m_mv)
            G.sg(sg_out[:, :], sg_in[:, :], sg_nf[:, :])
            m_sg1 = G.mark()
            V.wait(m_sg1)
            V.copy(nfb[:, :], sg_nf[:, :])
            TE.wait(V.mark())
            TE.mm(p_ex[0:16, 16:17], onesr[0:1, 0:16], nfb[0:1, 0:1],
                  True, True)
            TE.mm(p_ex[0:16, 17:18], selu16[:, 16 * u:16 * u + 16],
                  sm8[:, :], True, True)
            TE.minc()
            V.wait(TE.mark())
            V.wait(m_prev_mg)
            V.ts(m48[:, :], iota48w[:, :], p_ex[0:16, 16:17], None, ALU.is_lt)
            V.tt(sg_ids[:, :], sg_out[:, :], m48[:, :], ALU.mult)
            V.ts(mcol[:, 0:1], p_ex[0:16, 17:18], 1.0)
            for kpow in (2048.0, 1024.0, 512.0, 256.0, 128.0, 64.0, 32.0,
                         16.0):
                V.ts(mcol[:, 1:2], mcol[:, 0:1], kpow, None, ALU.is_ge)
                V.stt(mcol[:, 0:1], mcol[:, 1:2], -kpow, mcol[:, 0:1],
                      ALU.mult, ALU.add)
            V.tt(mcol[:, 1:2], p_ex[0:16, 17:18], mcol[:, 0:1], ALU.subtract)
            V.ts(mcol[:, 1:2], mcol[:, 1:2], 1.0 / 16)
            V.tt(mcol[:, 2:3], mcol[:, 0:1], iotap16[:, :], ALU.add)
            V.ts(mcol[:, 3:4], mcol[:, 2:3], 16.0, None, ALU.is_ge)
            V.stt(mcol[:, 2:3], mcol[:, 3:4], -16.0, mcol[:, 2:3],
                  ALU.mult, ALU.add)
            V.tt(mcol[:, 3:4], mcol[:, 1:2], mcol[:, 3:4], ALU.add)
            V.stt(mcol[:, 1:2], mcol[:, 2:3], float(WPAD), mcol[:, 3:4],
                  ALU.mult, ALU.add)
            V.copy(idxcol[0:16, 0:1], mcol[:, 1:2])
            m_idx = V.mark()
            G.wait(m_idx)
            G.imerge(w_ids.ap().rearrange("(a r) c -> a (r c)", a=1),
                     idxcol[0:16, 0:1], sg_ids[:, :])
            G.sg(sg_outf[:, :], sg_in2[:, :], sg_nf[:, :])
            m_sg2 = G.mark()
            V.wait(m_sg2)
            V.tt(sg_scv[:, :], sg_outf[:, :], m48[:, :], ALU.mult)
            m_scv = V.mark()
            G.wait(m_scv)
            G.imerge(w_sc.ap().rearrange("(a r) c -> a (r c)", a=1),
                     idxcol[0:16, 0:1], sg_scv[:, :])
            m_prev_v = V.mark()
            m_prev_mg = G.dmark()
        return G.dmark()

    def emit_owner_lists(msel, m_merge):
        V.wait(msel["lm16"])
        V.wait(msel["myrow"])
        V.wait(m_merge)
        for e in range(NB):
            V.ts(lsl[:, :], iotaloc[:, :], 1.0, 2.0, ALU.mult, ALU.add)
            V.tt(lsl[:, :], lsl[:, :], lm16[:, :, e], ALU.mult)
            V.ts(lsl[:, :], lsl[:, :], 1.0, -1.0, ALU.mult, ALU.add)
            m_c = V.mark()
            G.wait(m_c)
            G.sg(sg_out[:, :], lsl[:, :], sg_nf[:, :])
            m_sg = G.mark()
            V.wait(m_sg)
            V.copy(nfb[:, :], sg_nf[:, :])
            TE.wait(V.mark())
            TE.mm(p_ex[0:16, 16:17], onesr[0:1, 0:16], nfb[0:1, 0:1],
                  True, True)
            TE.mm(p_ex[0:16, 17:18], onesr[0:1, 0:16], myrow[0:1, e:e + 1],
                  True, True)
            TE.minc()
            V.wait(TE.mark())
            V.ts(m48[:, :], iota48w[:, :], p_ex[0:16, 16:17], None, ALU.is_lt)
            V.tt(sg_scv[:, :], sg_out[:, :], m48[:, :], ALU.mult)
            V.ts(sg_scv[:, :], sg_scv[:, :], 1.0, -1.0, ALU.mult, ALU.add)
            V.copy(ow_s[0:16, e, :], sg_scv[:, :])
            V.ts(sg_scv[:, :], iota48w[:, :], p_ex[0:16, 17:18],
                 float(e * CAP) + 1.0, ALU.add, ALU.add)
            V.tt(sg_scv[:, :], sg_scv[:, :], m48[:, :], ALU.mult)
            V.ts(sg_scv[:, :], sg_scv[:, :], 1.0, -1.0, ALU.mult, ALU.add)
            V.copy(ow_g[0:16, e, :], sg_scv[:, :])
        ohf = owcnt_f.rearrange("a (e h) -> a e h", h=2)
        V.ts(ohf[:, :, 0], myrow[0:1, 8:16], 384.0, None, ALU.min)
        V.tt(ohf[:, :, 1], myrow[0:1, 8:16], ohf[:, :, 0], ALU.subtract)
        V.copy(owcnt[:, :], owcnt_f[:, :])
        m_ow0 = V.mark()
        Q.wait(m_ow0)
        Q.dma(ow_s[16:32, :, :], ow_s[0:16, :, :])
        Q.dma(ow_g[16:32, :, :], ow_g[0:16, :, :])
        Q.wait(Q.dmark())
        Q.dma(ow_s[32:64, :, :], ow_s[0:32, :, :])
        Q.dma(ow_g[32:64, :, :], ow_g[0:32, :, :])
        Q.wait(Q.dmark())
        Q.dma(ow_s[64:128, :, :], ow_s[0:64, :, :])
        Q.dma(ow_g[64:128, :, :], ow_g[0:64, :, :])
        return Q.dmark()

    def emit_gl(m_merge, scrow, scband):
        Q.wait(m_merge)
        Q.dma(gl32[:, :], w_ids.ap()[:, 0:256])
        V.wait(Q.dmark())
        V.ts(glf[:, :], gl32[:, :], 1.0, -1.0, ALU.mult, ALU.add)
        V.copy(gl_idx[0:16, :], glf[:, :])
        m_gl0 = V.mark()
        Q.wait(m_gl0)
        Q.dma(gl_idx[16:32, :], gl_idx[0:16, :])
        Q.wait(Q.dmark())
        Q.dma(gl_idx[32:64, :], gl_idx[0:32, :])
        Q.wait(Q.dmark())
        Q.dma(gl_idx[64:128, :], gl_idx[0:64, :])
        m_gl = Q.dmark()
        for rr in range(16):
            Q.dma(lin_sc.ap().rearrange("a (c r) -> (a c) r", r=16)[:, rr:rr + 1],
                  w_sc.ap()[rr:rr + 1, 0:256].rearrange("r c -> c r"),
                  allow_slow_non_contiguous=True)
        m_lin = Q.dmark()
        ret_lin = m_lin
        G.lib(LC.mlp)
        G.wait(m_lin)
        m_scrow = G.dma(scrow[:, :], lin_sc.ap())
        G.wait(m_scrow)
        G.pbcast(scband[:, :], scrow[:, :], 128)
        return dict(gl=m_gl, scband=G.mark(), lin=ret_lin)

    def emit_ffn(mrg, m_ag_g, fset, m_eo_free):
        fw1, fw3, fw2db, rt, hbuf, sil, outt, scrow, scband = fset
        mw = G.dmark()
        G.dma(fw1[:, :, :, :],
              w1_in.ap().rearrange("(ct p) (ht j) -> p ct ht j", p=128, j=128))
        G.dma(fw3[:, :, :, :],
              w3_in.ap().rearrange("(ct p) (ht j) -> p ct ht j", p=128, j=128))
        m_w13 = G.dmark()
        G.wait(mrg["gl"])
        G.wait(m_ag_g)
        G.dma(fw2db[:, :, :], w2_in.ap().rearrange("(ht p) d -> p ht d", p=128))
        m_w2 = G.dmark()
        gmk = [None] * 4
        rt_free = None
        m_eo_prev = None
        for k in range(4):
            G.wait(rt_free)
            gmk[k] = G.gather(rt[:, :, :],
                              g_all.ap(), gl_idx[:, k * 64:(k + 1) * 64],
                              1024, C, transpose=True)
            V.wait(gmk[k])
            V.wait(mrg["scband"])
            V.tt(rt[:, :, :], rt[:, :, :],
                 scband[:, k * 1024:(k + 1) * 1024]
                 .rearrange("p (a b) -> p a b", a=1)
                 .to_broadcast([128, NCT, 1024]), ALU.mult)
            m_rt = V.mark()

            for n in range(2):
                kk = 2 * k + n
                TE.wait(m_rt)
                TE.wait(S.mark())
                TE.wait(m_w13)
                vm = [None] * NHT
                for ht in range(NHT):
                    pmu = p_mm0 if ht % 2 == 0 else p_mm2
                    pmv = p_mm1 if ht % 2 == 0 else p_mm3
                    if ht >= 2:
                        TE.wait(vm[ht - 2])
                    for ct in range(NCT):
                        TE.mm(pmu[:, :], fw1[:, ct, ht, :],
                              rt[:, ct, n * 512:(n + 1) * 512],
                              ct == 0, ct == NCT - 1)
                    for ct in range(NCT):
                        TE.mm(pmv[:, :], fw3[:, ct, ht, :],
                              rt[:, ct, n * 512:(n + 1) * 512],
                              ct == 0, ct == NCT - 1)
                    TE.minc()
                    m_u = TE.mark()
                    S.wait(m_u)
                    S.act(sil[:, ht % 2, :], pmu[:, :], ACT.Silu)
                    m_sil = S.mark()
                    V.wait(m_sil)
                    V.stt(hbuf[:, ht, :], pmv[:, :], 0.5, sil[:, ht % 2, :],
                          ALU.mult, ALU.mult)
                    vm[ht] = V.mark()
                m_h = V.mark()
                TE.wait(m_h)
                TE.wait(m_w2)
                sm4 = [None] * 4
                for st4 in range(4):
                    pma = p_mm2 if st4 % 2 == 0 else p_mm0
                    pmb = p_mm3 if st4 % 2 == 0 else p_mm1
                    if st4 >= 2:
                        TE.wait(sm4[st4 - 2])
                    if st4 == 0:
                        S.wait(m_eo_prev)
                    for dn in range(2):
                        pm = pma if dn == 0 else pmb
                        for ht in range(NHT):
                            TE.mm(pm[:, :],
                                  hbuf[:, ht, st4 * 128:(st4 + 1) * 128],
                                  fw2db[:, ht, dn * 512:(dn + 1) * 512],
                                  ht == 0, ht == NHT - 1)
                    TE.minc()
                    m_o = TE.mark()
                    S.wait(m_o)
                    S.act(outt[:, st4, 0:512], pma[:, :], ACT.Copy)
                    S.act(outt[:, st4, 512:1024], pmb[:, :], ACT.Copy)
                    sm4[st4] = S.mark()
                m_ev = S.mark()
                Q.wait(m_ev)
                if kk == 0:
                    Q.wait(m_eo_free)
                Q.dma(eo_mine.ap().rearrange("(a p) d -> p a d", p=128)
                      [:, kk * 4:(kk + 1) * 4, :], outt[:, :, :])
                m_eo_prev = Q.dmark()
            rt_free = TE.mark()
        m_eo = Q.dmark()
        G.wait(m_eo)
        m_ag_eo = G.cc("AllGather", ALU.bypass, [eo_mine.ap().opt()],
                       [eo_all.ap().opt()], s_cc, ccs)
        return m_ag_eo

    def emit_combine(m_ag_eo, m_owner, target_ap, m_target_ready, cset):
        cmb_b, cmb_f = cset
        G.wait(m_ag_eo)
        G.wait(m_owner)
        G.wait(m_target_ready)
        prev_cast = None
        for e in range(NB):
            for hhalf in range(2):
                G.wait(prev_cast)
                mg = G.gather(cmb_b[:, :, :], eo_all.ap(),
                              ow_g[:, e, hhalf * 24:(hhalf + 1) * 24],
                              384, C, transpose=False,
                              count_src=owcnt[0:1, 2 * e + hhalf:
                                              2 * e + hhalf + 1])
                V.wait(mg)
                V.copy(cmb_f[:, :, :].rearrange("p a b -> p (a b)"),
                       cmb_b[:, :, :].rearrange("p a b -> p (a b)"))
                m_cast = V.mark()
                prev_cast = m_cast
                G.wait(m_cast)
                G.scatter_add(target_ap, cmb_f[:, :, :],
                              ow_s[:, e, hhalf * 24:(hhalf + 1) * 24],
                              384, C,
                              count_src=owcnt[0:1, 2 * e + hhalf:
                                              2 * e + hhalf + 1])
        return G.dmark()

    sel_ctx = ExitStack()
    selset = (sb("selsc", [128, 256, NB], ctx=sel_ctx),
              sb("selmsk", [128, 256, NB], ctx=sel_ctx),
              sb("seltie", [128, 256, NB], ctx=sel_ctx),
              sb("selmb", [128, 256, NB], BF16, ctx=sel_ctx),
              sb("selscr", [128, 256], ctx=sel_ctx),
              sb("selscr8", [128, 256, NB], ctx=sel_ctx))
    msel1 = emit_selection(m_ag_sc1, selset)
    m_merge1 = emit_merge(msel1)
    m_owner1 = emit_owner_lists(msel1, m_merge1)
    sel_ctx.close()

    ffn_ctx = ExitStack()
    fset1 = (sb("fw1", [128, NCT, NHT, 128], BF16, ctx=ffn_ctx),
             sb("fw3", [128, NCT, NHT, 128], BF16, ctx=ffn_ctx),
             sb("fw2db", [128, NHT, 1024], BF16, ctx=ffn_ctx),
             sb("rt", [128, NCT, 1024], BF16, ctx=ffn_ctx),
             sb("hbuf", [128, NHT, 512], BF16, ctx=ffn_ctx),
             sb("sil", [128, 2, 512], BF16, ctx=ffn_ctx),
             sb("outt", [128, 4, 1024], BF16, ctx=ffn_ctx),
             sb("scrow", [1, CAP], BF16, ctx=ffn_ctx),
             sb("scband", [128, CAP], BF16, ctx=ffn_ctx))
    mgl1 = emit_gl(m_merge1, fset1[7], fset1[8])
    mrg1 = dict(gl=mgl1["gl"], scband=mgl1["scband"])
    m_ag_eo1 = emit_ffn(mrg1, m_ag_g1, fset1, None)
    ffn_ctx.close()
    cmb_ctx = ExitStack()
    cset1 = (sb("cmb_b", [128, 3, 1024], BF16, ctx=cmb_ctx),
             sb("cmb_f", [128, 3, 1024], ctx=cmb_ctx))
    m_cmb1 = emit_combine(m_ag_eo1, m_owner1, x2_plane.ap(), Q.dmark(), cset1)
    cmb_ctx.close()

    attn_ctx = ExitStack()
    wk4 = sb("wk4", [128, NCT, NCT, 128], BF16, ctx=attn_ctx)
    wv4 = sb("wv4", [128, NCT, NCT, 128], BF16, ctx=attn_ctx)
    wr4 = sb("wr4", [128, NCT, NCT, 128], BF16, ctx=attn_ctx)
    wo4 = sb("wo4", [128, NCT, 1024], BF16, ctx=attn_ctx)
    xt34 = sb("xt34", [128, 4, 1024], ctx=attn_ctx)
    g2cm = sb("g2cm", [128, NCT, CH + 1], BF16, ctx=attn_ctx)
    mixb = sb("mixb", [128, 3, CH], BF16, ctx=attn_ctx)
    xk = sb("xk", [128, NCT, CH], BF16, ctx=attn_ctx)
    xv = sb("xv", [128, NCT, CH], BF16, ctx=attn_ctx)
    xr = sb("xr", [128, NCT, CH], BF16, ctx=attn_ctx)
    kd = sb("kd", [128, CH], ctx=attn_ctx)
    vdd = sb("vdd", [128, 2, CH], ctx=attn_ctx)
    ad = sb("ad", [128, 2, CH], ctx=attn_ctx)
    avd = sb("avd", [128, CH], ctx=attn_ctx)
    srd = sb("srd", [128, 2, CH], BF16, ctx=attn_ctx)
    numb = sb("numb", [128, CH], ctx=attn_ctx)
    denb = sb("denb", [128, CH], ctx=attn_ctx)
    yb = sb("yb", [128, CH], ctx=attn_ctx)
    zb = sb("zb", [128, NCT, CH], BF16, ctx=attn_ctx)
    pbuf = sb("pbuf", [128, CH + 1], ctx=attn_ctx)
    qbuf = sb("qbuf", [128, CH + 1], ctx=attn_ctx)
    pcar = sb("pcar", [128, NCT], ctx=attn_ctx)
    qcar = sb("qcar", [128, NCT], ctx=attn_ctx)
    ddcur = sb("ddcur", [128, CH], ctx=attn_ctx)

    G.lib(LC.standard)
    G.wait(m_cmb1)
    G.wait(TE.mark())
    G.wait(S.mark())
    G.wait(V.mark())
    G.dma(wk4[:, :, :, :],
          wkT_in.ap().rearrange("(ct p) (dt j) -> p ct dt j", p=128, j=128))
    G.dma(wv4[:, :, :, :],
          wvT_in.ap().rearrange("(ct p) (dt j) -> p ct dt j", p=128, j=128))
    G.dma(wr4[:, :, :, :],
          wrT_in.ap().rearrange("(ct p) (dt j) -> p ct dt j", p=128, j=128))
    G.dma(wo4[:, :, :],
          woT_in.ap().rearrange("(ct p) d -> p ct d", p=128))
    m_aw = G.dmark()

    V.wait(m_dvec)
    V.wait(m_aw)
    V.memset(pcar[:, :], 0.0)
    V.memset(qcar[:, :], 0.0)
    V.memset(g2cm[:, :, 0:1].rearrange("p a b -> p (a b)"), 0.0)
    m_attn_init = V.mark()

    m_x2_ready = m_cmb1
    prev_xk_cons = None
    prev_xt_cons = None
    for c in range(NCH):
        Q.wait(m_x2_ready)
        Q.wait(prev_xt_cons)
        mx = Q.dma(xt34[:, :, :],
                   x2_plane.ap()[c * CH:(c + 1) * CH, :]
                   .rearrange("(a p) d -> p a d", p=128))
        if c > 0:
            V.copy(g2cm[:, :, 0:1].rearrange("p a b -> p (a b)"),
                   g2cm[:, :, CH:CH + 1].rearrange("p a b -> p (a b)"))
        for tt in range(4):
            V.wait(mx)
            S.wait(mx)
            V.wait(m_attn_init)
            V.ts(sq_scr[:, :], xt34[:, tt, :], 1.0, None, ALU.mult, ALU.add,
                 accum=stats[:, 0:1])
            S.act(sq_scr2[:, :], xt34[:, tt, :], ACT.Square,
                  accum=stats[:, 1:2])
            m_sq = S.mark()
            V.wait(m_sq)
            V.drain()
            V.ts(stats[:, 2:3], stats[:, 0:1], -1.0 / C)
            V.tt(stats[:, 4:5], stats[:, 2:3], stats[:, 2:3], ALU.mult)
            V.ts(stats[:, 6:7], stats[:, 1:2], 1.0 / C, EPS, ALU.mult,
                 ALU.add)
            V.tt(stats[:, 3:4], stats[:, 6:7], stats[:, 4:5], ALU.subtract)
            V.recip(stats[:, 7:8], stats[:, 3:4])
            m_rc = V.mark()
            S.wait(m_rc)
            S.act(stats[:, 8:9], stats[:, 7:8], ACT.Sqrt)
            S.nop()
            S.act(stats[:, 9:10], stats[:, 2:3], ACT.Identity,
                  scale=stats[:, 8:9])
            m_bias = S.mark()
            V.wait(m_bias)
            V.wait(lg_prev["tr"])
            V.ts(gb[:, tt % 2, :], xt34[:, tt, :], stats[:, 8:9],
                 stats[:, 9:10], ALU.mult, ALU.add)
            m_g2 = V.mark()
            TE.wait(m_g2)
            TE.wait(prev_xk_cons)
            for h in range(2):
                pt = p_tr if h == 0 else p_tr2
                for j in range(4):
                    ct = h * 4 + j
                    TE.trans(pt[:, j * 128:(j + 1) * 128],
                             gb[:, tt % 2, ct * 128:(ct + 1) * 128],
                             ident[:, :])
            TE.minc()
            m_tr = TE.mark()
            S.wait(m_tr)
            for h in range(2):
                pt = p_tr if h == 0 else p_tr2
                for j in range(4):
                    ct = h * 4 + j
                    S.act(g2cm[:, ct, 1 + tt * 128:1 + (tt + 1) * 128],
                          pt[:, j * 128:(j + 1) * 128], ACT.Copy)
            lg_prev["tr"] = m_tr
        m_g2cm = S.mark()

        V.wait(m_g2cm)
        S.wait(m_g2cm)
        for ct in range(NCT):
            for mi, t1col in enumerate([5, 6, 7]):
                S.act(mixb[:, mi, :], g2cm[:, ct, 0:CH], ACT.Identity,
                      scale=chv[:, ct, t1col:t1col + 1])
            m_mx = S.mark()
            V.wait(m_mx)
            V.stt(xk[:, ct, :], g2cm[:, ct, 1:CH + 1], chv[:, ct, 2:3],
                  mixb[:, 0, :], ALU.mult, ALU.add)
            V.stt(xv[:, ct, :], g2cm[:, ct, 1:CH + 1], chv[:, ct, 3:4],
                  mixb[:, 1, :], ALU.mult, ALU.add)
            V.stt(xr[:, ct, :], g2cm[:, ct, 1:CH + 1], chv[:, ct, 4:5],
                  mixb[:, 2, :], ALU.mult, ALU.add)
            S.wait(V.mark())
        m_mix = V.mark()

        m_z = None
        m_prev_evac = None
        m_z_hist = [None] * NCT
        for dt in range(NCT):
            par = dt % 2
            TE.wait(m_mix)
            TE.wait(m_aw)
            TE.wait(m_prev_evac)
            for ct in range(NCT):
                TE.mm(p_mm0[:, :], wk4[:, ct, dt, :], xk[:, ct, :],
                      ct == 0, ct == NCT - 1)
            for ct in range(NCT):
                TE.mm(p_mm1[:, :], wv4[:, ct, dt, :], xv[:, ct, :],
                      ct == 0, ct == NCT - 1)
            for ct in range(NCT):
                TE.mm(p_mm2[:, :], wr4[:, ct, dt, :], xr[:, ct, :],
                      ct == 0, ct == NCT - 1)
            TE.minc()
            m_kvr = TE.mark()
            S.wait(m_kvr)
            if dt >= 2:
                S.wait(m_z_hist[dt - 2])
            S.act(kd[:, :], p_mm0[:, :], ACT.Copy)
            S.act(vdd[:, par, :], p_mm1[:, :], ACT.Copy)
            S.act(srd[:, par, :], p_mm2[:, :], ACT.Sigmoid)
            S.nop()
            S.act(ad[:, par, :], kd[:, :], ACT.Exp)
            m_sc_dt = S.mark()
            m_prev_evac = m_sc_dt
            V.wait(m_sc_dt)
            V.ts(ddcur[:, :], dvec[:, dt, 0:1].to_broadcast([128, CH]), 1.0)
            V.tt(avd[:, :], ad[:, par, :], vdd[:, par, :], ALU.mult)
            V.copy(pbuf[:, 0:1], pcar[:, dt:dt + 1])
            V.copy(qbuf[:, 0:1], qcar[:, dt:dt + 1])
            V.scan(pbuf[:, 1:CH + 1], ddcur[:, :], avd[:, :],
                   pbuf[:, 0:1])
            V.scan(qbuf[:, 1:CH + 1], ddcur[:, :], ad[:, par, :],
                   qbuf[:, 0:1])
            V.copy(pcar[:, dt:dt + 1], pbuf[:, CH:CH + 1])
            V.copy(qcar[:, dt:dt + 1], qbuf[:, CH:CH + 1])
            V.stt(numb[:, :], avd[:, :], dvec[:, dt, 1:2],
                  pbuf[:, 0:CH], ALU.mult, ALU.add)
            V.stt(denb[:, :], ad[:, par, :], dvec[:, dt, 1:2],
                  qbuf[:, 0:CH], ALU.mult, ALU.add)
            V.recip(yb[:, :], denb[:, :])
            V.tt(yb[:, :], yb[:, :], numb[:, :], ALU.mult)
            V.tt(zb[:, dt, :], yb[:, :], srd[:, par, :], ALU.mult)
            m_z = V.mark()
            m_z_hist[dt] = m_z
        prev_xk_cons = m_z

        for tt in range(4):
            TE.wait(m_z)
            for dn in range(2):
                pm = p_mm2 if dn == 0 else p_mm3
                for ct in range(NCT):
                    TE.mm(pm[:, :], zb[:, ct, tt * 128:(tt + 1) * 128],
                          wo4[:, ct, dn * 512:(dn + 1) * 512],
                          ct == 0, ct == NCT - 1)
            TE.minc()
            m_wo = TE.mark()
            V.wait(m_wo)
            V.tt(xt34[:, tt, 0:512], xt34[:, tt, 0:512], p_mm2[:, :], ALU.add)
            V.tt(xt34[:, tt, 512:1024], xt34[:, tt, 512:1024], p_mm3[:, :],
                 ALU.add)
            m_x3 = V.mark()
            TE.wait(m_x3)
            Q.wait(m_x3)
            Q.dma(out_ext.ap()[(c * 4 + tt) * 128:(c * 4 + tt + 1) * 128, :],
                  xt34[:, tt, :])
            lngate(c * 4 + tt, xt34[:, tt, :], xwait=m_x3)
        prev_xt_cons = V.mark()
    m_attn_q = Q.dmark()
    m_attn_done = (V.mark(), TE.mark(), S.mark(), G.mark())
    attn_ctx.close()

    G.wait(m_attn_q)
    m_ag_sc2 = G.cc("AllGather", ALU.bypass, [sc_mine.ap().opt()],
                    [sc_all.ap().opt()], s_cc, ccs)
    m_ag_g2 = G.cc("AllGather", ALU.bypass, [g_mine.ap().opt()],
                   [g_all.ap().opt()], s_cc, ccs)

    sel_ctx2 = ExitStack()
    selset2 = (sb("selsc2", [128, 256, NB], ctx=sel_ctx2),
               sb("selmsk2", [128, 256, NB], ctx=sel_ctx2),
               sb("seltie2", [128, 256, NB], ctx=sel_ctx2),
               sb("selmb2", [128, 256, NB], BF16, ctx=sel_ctx2),
               sb("selscr2", [128, 256], ctx=sel_ctx2),
               sb("selscr82", [128, 256, NB], ctx=sel_ctx2))
    msel2 = emit_selection(m_ag_sc2, selset2,
                           m_wfree=(mgl1['gl'], mgl1['lin']))
    m_merge2 = emit_merge(msel2)
    m_owner2 = emit_owner_lists(msel2, m_merge2)
    sel_ctx2.close()

    ffn_ctx2 = ExitStack()
    fset2 = (sb("fw1b", [128, NCT, NHT, 128], BF16, ctx=ffn_ctx2),
             sb("fw3b", [128, NCT, NHT, 128], BF16, ctx=ffn_ctx2),
             sb("fw2db2", [128, NHT, 1024], BF16, ctx=ffn_ctx2),
             sb("rt2", [128, NCT, 1024], BF16, ctx=ffn_ctx2),
             sb("hbuf2", [128, NHT, 512], BF16, ctx=ffn_ctx2),
             sb("sil2", [128, 2, 512], BF16, ctx=ffn_ctx2),
             sb("outt2", [128, 4, 1024], BF16, ctx=ffn_ctx2),
             sb("scrow2", [1, CAP], BF16, ctx=ffn_ctx2),
             sb("scband2", [128, CAP], BF16, ctx=ffn_ctx2))
    mgl2 = emit_gl(m_merge2, fset2[7], fset2[8])
    mrg2 = dict(gl=mgl2["gl"], scband=mgl2["scband"])
    m_ag_eo2 = emit_ffn(mrg2, m_ag_g2, fset2, m_ag_eo1)
    ffn_ctx2.close()
    cmb_ctx2 = ExitStack()
    cset2 = (sb("cmb_b2", [128, 3, 1024], BF16, ctx=cmb_ctx2),
             sb("cmb_f2", [128, 3, 1024], ctx=cmb_ctx2))
    m_cmb2 = emit_combine(m_ag_eo2, m_owner2, out_ext.ap(), m_attn_q, cset2)
    cmb_ctx2.close()

    G.wait(m_cmb2)
    G.wait(Q.dmark())
    G.wait(V.mark())
    G.wait(S.mark())
    G.wait(TE.mark())

    globals()["_RECS"] = dict(Q=Q, V=V, S=S, TE=TE, G=G)
    with top, nc.Block() as block:
        @block.sync
        def _(e):
            Q.replay(e)

        @block.vector
        def _(e):
            V.replay(e)

        @block.scalar
        def _(e):
            S.replay(e)

        @block.tensor
        def _(e):
            TE.replay(e)

        @block.gpsimd
        def _(e):
            G.replay(e)

    return nc


_NC_CACHE = {}


def kernel(L0_attn_x, L0_attn_time_decay, L0_attn_time_first,
           L0_attn_time_mix_k, L0_attn_time_mix_v, L0_attn_time_mix_r,
           L0_attn_Wk, L0_attn_Wv, L0_attn_Wr, L0_attn_Wo,
           L0_ffn_wg, L0_ffn_w1, L0_ffn_w3, L0_ffn_w2):
    x = np.ascontiguousarray(np.asarray(L0_attn_x, dtype=np.float32))
    td = np.asarray(L0_attn_time_decay, np.float32).reshape(C)
    tf = np.asarray(L0_attn_time_first, np.float32).reshape(C)
    tmk = np.asarray(L0_attn_time_mix_k, np.float32).reshape(C)
    tmv = np.asarray(L0_attn_time_mix_v, np.float32).reshape(C)
    tmr = np.asarray(L0_attn_time_mix_r, np.float32).reshape(C)

    def chmaj(v):
        return v.reshape(NCT, 128).T

    chvec = np.ascontiguousarray(
        np.stack([chmaj(td), chmaj(tf), chmaj(tmk), chmaj(tmv), chmaj(tmr)],
                 axis=2).astype(np.float32))

    wkT = np.ascontiguousarray(np.asarray(L0_attn_Wk, np.float32).T)
    wvT = np.ascontiguousarray(np.asarray(L0_attn_Wv, np.float32).T)
    wrT = np.ascontiguousarray(np.asarray(L0_attn_Wr, np.float32).T)
    woT = np.ascontiguousarray(np.asarray(L0_attn_Wo, np.float32).T)
    wg = np.ascontiguousarray(np.asarray(L0_ffn_wg, np.float32))
    w1 = np.asarray(L0_ffn_w1, np.float32)
    w3 = np.asarray(L0_ffn_w3, np.float32)
    w2 = np.asarray(L0_ffn_w2, np.float32)

    identbf = np.eye(128, dtype=ml_dtypes.bfloat16)
    p = np.arange(128)
    g16c = (p[:, None] // 16 == np.arange(NB)[None, :]).astype(np.float32)
    h16c = (p[:, None] % 16 == np.arange(16)[None, :]).astype(ml_dtypes.bfloat16)
    tric = (np.arange(NB)[:, None] < np.arange(NB)[None, :]).astype(np.float32)
    onesrow = np.ones((1, 128), np.float32)
    ones128c = np.ones((128, 1), np.float32)
    tidbase = (4096.0 * (p // 16) + (p % 16)).astype(np.float32).reshape(128, 1)
    selu = np.zeros((NB, 128), np.float32)
    for u in range(NB):
        selu[u, 16 * u:16 * u + 16] = 1.0

    in_maps = []
    for r in range(NB):
        rkvc = np.zeros((1, 16), np.float32)
        rkvc[0, r] = 1.0
        rkvc[0, 8] = r
        rkvc[0, 9] = 4096.0 * r
        ohcolc = np.zeros((NB, 1), np.float32)
        ohcolc[r, 0] = 1.0
        in_maps.append({
            "x": np.ascontiguousarray(x[r]),
            "wkT": wkT, "wvT": wvT, "wrT": wrT, "woT": woT,
            "w1": np.ascontiguousarray(w1[r]),
            "w3": np.ascontiguousarray(w3[r]),
            "w2": np.ascontiguousarray(w2[r]),
            "wg": wg,
            "chvec": chvec,
            "rkv": rkvc,
            "ohcol": ohcolc,
            "identbf": identbf,
            "g16": g16c,
            "h16": h16c,
            "triexcl": tric,
            "onesrow": onesrow,
            "ones128": ones128c,
            "tidbase": tidbase,
            "selu16": selu,
            "identf8": np.eye(NB, dtype=np.float32),
        })

    if "prog" not in _NC_CACHE:
        nc = build_program()
        nc.finalize()
        _NC_CACHE["prog"] = nc
    nc = _NC_CACHE["prog"]
    res = run_bass_kernel_spmd(nc, in_maps, core_ids=list(range(NB)))
    out = np.stack([np.asarray(res.results[r]["out"], np.float32)
                    for r in range(NB)], axis=0)
    return out


# version 10
# speedup vs baseline: 1.1567x; 1.0806x over previous
import numpy as np
import ml_dtypes
from contextlib import ExitStack

import concourse.bacc as bacc
import concourse.mybir as mybir
from concourse.bass import IndirectOffsetOnAxis
from concourse.bass_utils import run_bass_kernel_spmd
from concourse import library_config as LC

F32 = mybir.dt.float32
BF16 = mybir.dt.bfloat16
I16 = mybir.dt.int16
I32 = mybir.dt.int32
U32 = mybir.dt.uint32
ALU = mybir.AluOpType
ACT = mybir.ActivationFunctionType
AXX = mybir.AxisListType

NB = 8
T = 4096
NT = NB * T
C = 1024
NCT = 8
HID = 2048
NHT = 16
CAP = 4096
EPS = 1e-5
VAL_ITERS = 36
TIE_ITERS = 16
WPAD = 304
CH = 512
NCH = T // CH


class Rec:
    def __init__(self, name, sem, dsem):
        self.name = name
        self.sem = sem
        self.dsem = dsem
        self.q = []
        self.n = 0
        self.nd = 0
        self.last_out = set()
        self.recent = []
        self.waited = {}
        self.ev = []

    def wait(self, mark):
        if mark is None:
            return
        sem, count = mark
        if count <= 0:
            return
        key = id(sem)
        if self.waited.get(key, -1) >= count:
            return
        self.waited[key] = count
        self.ev.append(("w", id(sem), count, len(self.q)))
        self.q.append(lambda e: e.wait_ge(sem, count))

    def mark(self):
        return (self.sem, self.n)

    def dmark(self):
        return (self.dsem, self.nd)

    @staticmethod
    def _names(aps):
        out = set()
        for ap in aps:
            if ap is None or isinstance(ap, (int, float)):
                continue
            t = getattr(ap, "tensor", None)
            if t is not None:
                out.add(t.name)
        return out

    def _hz(self, ins, outs):
        if self.name not in ("vector", "scalar", "gpsimd"):
            return
        rd = self._names(ins)
        wr = self._names(outs)
        rws = set()
        rds = set()
        for (r, w) in self.recent:
            rws |= w
            rds |= r
        if (rd & rws) or (wr & rws) or (wr & rds):
            if self.name == "vector":
                self.q.append(lambda e: e.drain())
            else:
                sem, n = self.sem, self.n
                self.ev.append(("w", id(sem), n, len(self.q)))
                self.q.append(lambda e: e.wait_ge(sem, n))
            self.recent = []
        self.recent = (self.recent + [(rd, wr)])[-3:]

    def drain(self):
        self.q.append(lambda e: e.drain())
        self.recent = []

    def _inc(self, call):
        sem = self.sem
        self.ev.append(("i", id(sem), 1, len(self.q)))
        self.q.append(lambda e: call(e).then_inc(sem, 1))
        self.n += 1

    def ts(self, out, in0, s1, s2=None, op0=ALU.mult, op1=None, accum=None):
        self._hz([in0, s1, s2], [out, accum])
        kw = dict(out=out, in0=in0, scalar1=s1, scalar2=s2, op0=op0)
        if op1 is not None:
            kw["op1"] = op1
        if accum is not None:
            kw["accum_out"] = accum
        self._inc(lambda e: e.tensor_scalar(**kw))

    def tt(self, out, in0, in1, op):
        self._hz([in0, in1], [out])
        self._inc(lambda e: e.tensor_tensor(out=out, in0=in0, in1=in1, op=op))

    def stt(self, out, in0, s, in1, op0, op1, accum=None):
        self._hz([in0, in1, s], [out, accum])
        kw = dict(out=out, in0=in0, scalar=s, in1=in1, op0=op0, op1=op1)
        if accum is not None:
            kw["accum_out"] = accum
        self._inc(lambda e: e.scalar_tensor_tensor(**kw))

    def copy(self, out, in_):
        self._hz([in_], [out])
        self._inc(lambda e: e.tensor_copy(out=out, in_=in_))

    def memset(self, ap, val):
        self._hz([], [ap])
        self._inc(lambda e: e.memset(ap, val))

    def recip(self, out, in_):
        self._hz([in_], [out])
        self._inc(lambda e: e.reciprocal(out=out, in_=in_))

    def reduce(self, out, in_, op, axis=AXX.X):
        self._hz([in_], [out])
        self._inc(lambda e: e.tensor_reduce(out=out, in_=in_, axis=axis, op=op))

    def scan(self, out, d0, d1, initial):
        self._hz([d0, d1, initial], [out])
        self._inc(lambda e: e.tensor_tensor_scan(
            out=out, data0=d0, data1=d1, initial=initial,
            op0=ALU.mult, op1=ALU.add))

    def act(self, out, in_, func, bias=None, scale=None, accum=None):
        self._hz([in_, bias, scale], [out, accum])
        kw = dict(out=out, in_=in_, func=func)
        if bias is not None:
            kw["bias"] = bias
        if scale is not None:
            kw["scale"] = scale
        if accum is not None:
            kw["accum_out"] = accum
        self._inc(lambda e: e.activation(**kw))

    def nop(self):
        self.q.append(lambda e: e.nop())

    def gdiv(self, out, in0, in1):
        self._inc(lambda e: e.tensor_tensor(out=out, in0=in0, in1=in1,
                                            op=ALU.divide))

    def mm(self, out, lhsT, rhs, start, stop):
        ent = {"k": "mm", "out": out, "lhsT": lhsT, "rhs": rhs,
               "start": start, "stop": stop, "inc": False, "sem": self.sem}
        self.q.append(ent)

    def trans(self, out, in_, ident):
        ent = {"k": "tr", "out": out, "in_": in_, "ident": ident,
               "inc": False, "sem": self.sem}
        self.q.append(ent)

    def minc(self):
        assert isinstance(self.q[-1], dict)
        self.q[-1]["inc"] = True
        self.ev.append(("i", id(self.sem), 1, len(self.q) - 1))
        self.n += 1

    def dma(self, out, in_, **kw):
        dsem = self.dsem
        self.wait(self.dmark())
        self.ev.append(("i", id(dsem), 16, len(self.q)))
        self.q.append(lambda e: e.dma_start(out=out, in_=in_, **kw)
                      .then_inc(dsem, 16))
        self.nd += 16
        if self.name in ("vector", "scalar", "gpsimd"):
            self.recent = (self.recent + [(set(), self._names([out]))])[-3:]
        return self.dmark()

    def sg(self, out, in_, nf):
        self._hz([in_], [out, nf])
        self._inc(lambda e: e.sparse_gather(out=out, in_=in_, num_found=nf))

    def lib(self, lib):
        self.q.append(lambda e: e.load_library(lib))

    def pbcast(self, out, in_, channels):
        self._hz([in_], [out])
        self._inc(lambda e: e.partition_broadcast(out_ap=out, in_ap=in_,
                                                  channels=channels))

    def gather(self, out, tab, idxs, n, elem, transpose=False, count_src=None):
        dsem = self.dsem
        self.wait(self.dmark())
        self.ev.append(("i", id(dsem), 16, len(self.q)))

        def emit(e):
            nreg = n if count_src is None else e.value_load(
                count_src, min_val=0, max_val=n)
            e.dma_gather(
                out_ap=out, in_ap=tab, idxs_ap=idxs, num_idxs=n,
                num_idxs_reg=nreg, elem_size=elem, transpose=transpose,
                single_packet=False).then_inc(dsem, 16)
        self.q.append(emit)
        self.nd += 16
        return self.dmark()

    def scatter_add(self, out, in_, idxs, n, elem, count_src=None):
        dsem = self.dsem
        self.wait(self.dmark())
        self.ev.append(("i", id(dsem), 16, len(self.q)))

        def emit(e):
            nreg = n if count_src is None else e.value_load(
                count_src, min_val=0, max_val=n)
            e.dma_scatter_add(
                out_ap=out, in_ap=in_, idxs_ap=idxs, num_idxs=n,
                num_idxs_reg=nreg, elem_size=elem,
                single_packet=False).then_inc(dsem, 16)
        self.q.append(emit)
        self.nd += 16
        return self.dmark()

    def imerge(self, out_ap, idx_ap, in_ap):
        dsem = self.dsem
        self.wait(self.dmark())
        self.ev.append(("i", id(dsem), 16, len(self.q)))
        self.q.append(lambda e: e.indirect_dma_start(
            out=out_ap, out_offset=IndirectOffsetOnAxis(ap=idx_ap, axis=1),
            in_=in_ap, in_offset=None,
            compute_op=ALU.add).then_inc(dsem, 16))
        self.nd += 16
        return self.dmark()

    def iota(self, out, pattern, base, cm):
        self._hz([], [out])
        self._inc(lambda e: e.iota(out=out, pattern=pattern, base=base,
                                   channel_multiplier=cm,
                                   allow_small_or_imprecise_dtypes=True))

    def cc(self, kind, op, ins, outs, ccsem, state):
        state[0] += 1
        count = state[0]
        self.ev.append(("i", id(ccsem), 1, len(self.q)))
        self.q.append(lambda e: e.collective_compute(
            kind, op, replica_groups=[list(range(NB))], ins=ins, outs=outs)
            .then_inc(ccsem, 1))
        return (ccsem, count)

    def replay(self, raw):
        for fn in self.q:
            if isinstance(fn, dict):
                if fn["k"] == "mm":
                    r = raw.matmul(out=fn["out"], lhsT=fn["lhsT"],
                                   rhs=fn["rhs"], start=fn["start"],
                                   stop=fn["stop"])
                else:
                    r = raw.transpose(out=fn["out"], in_=fn["in_"],
                                      identity=fn["ident"])
                if fn["inc"]:
                    r.then_inc(fn["sem"], 1)
            else:
                fn(raw)


def build_program():
    nc = bacc.Bacc("TRN2", target_bir_lowering=False)

    def ein(name, shape, dt=F32):
        return nc.dram_tensor(name, shape, dt, kind="ExternalInput")

    x_in = ein("x", [T, C])
    wkT_in = ein("wkT", [C, C]); wvT_in = ein("wvT", [C, C])
    wrT_in = ein("wrT", [C, C]); woT_in = ein("woT", [C, C])
    w1_in = ein("w1", [C, HID]); w3_in = ein("w3", [C, HID])
    w2_in = ein("w2", [HID, C])
    wg_in = ein("wg", [C, NB])
    chvec_in = ein("chvec", [128, NCT, 5])
    rkv_in = ein("rkv", [1, 16])
    ohcol_in = ein("ohcol", [NB, 1])
    ident_in = ein("identbf", [128, 128], BF16)
    g16_in = ein("g16", [128, NB])
    h16_in = ein("h16", [128, 16], BF16)
    tri_in = ein("triexcl", [NB, NB])
    onesr_in = ein("onesrow", [1, 128])
    ones128_in = ein("ones128", [128, 1])
    tidb_in = ein("tidbase", [128, 1])
    selu_in = ein("selu16", [NB, 128])
    ones2d_in = ein("ones2d", [128, 128])
    identf8_in = ein("identf8", [NB, NB])
    out_ext = nc.dram_tensor("out", [T, C], F32, kind="ExternalOutput")

    def dram(name, shape, dt):
        return nc.dram_tensor(name, shape, dt)

    g_mine = dram("g_mine", [T, C], BF16)
    g_all = dram("g_all", [NT, C], BF16)
    sc_mine = dram("sc_mine", [16, 256 * NB], F32)
    sc_all = dram("sc_all", [128, 256 * NB], F32)
    eo_mine = dram("eo_mine", [CAP, C], BF16)
    eo_all = dram("eo_all", [NB * CAP, C], BF16)
    x2_plane = dram("x2_plane", [T, C], F32)
    w_ids = dram("w_ids", [16, WPAD], I32)
    w_sc = dram("w_sc", [16, WPAD], F32)
    lin_sc = dram("lin_sc", [1, CAP], F32)

    top = ExitStack()

    def sb(name, shape, dt=F32, ctx=None):
        return (ctx or top).enter_context(nc.sbuf_tensor(name, shape, dt))

    def mkpsum(name, shape, dt=F32):
        return top.enter_context(nc.psum_tensor(name, shape, dt))

    def mksem(name):
        return top.enter_context(nc.semaphore(name))

    ident = sb("ident_s", [128, 128], BF16)
    chv = sb("chv", [128, NCT, 8])
    dvec = sb("dvec", [128, NCT, 2])
    wgt = sb("wgt", [128, NCT, NB], BF16)
    rkv = sb("rkv_s", [1, 16])
    ohcol = sb("ohcol_s", [NB, 1])
    g16 = sb("g16_s", [128, NB])
    h16 = sb("h16_s", [128, 16], BF16)
    tri = sb("tri_s", [NB, NB])
    onesr = sb("onesr_s", [1, 128])
    ones128 = sb("ones128_s", [128, 1])
    oh128 = sb("oh128", [128, NB])
    w128 = sb("w128", [128, 1])
    tidb = sb("tidb_s", [128, 1])
    selu16 = sb("selu16_s", [NB, 128])
    identf8 = sb("identf8_s", [NB, NB])
    ones2d = sb("ones2d_s", [128, 128])
    bnds = sb("bnds", [128, 48])
    scr8b = sb("scr8b", [128, 8])
    iota256 = sb("iota256", [128, 256])
    iota48w = sb("iota48w", [16, 48])
    iotap16 = sb("iotap16", [16, 1])
    iotaloc = sb("iotaloc", [16, 256])
    idxcol = sb("idxcol", [128, 1], U32)
    zero16 = sb("zero16", [16, WPAD])
    cid = sb("cid", [128, 256])
    csc = sb("csc", [128, 256])
    tid = sb("tid", [128, 256])
    bnd = sb("bnd", [1, 64])
    cnt = sb("cnt", [128, NB])
    cstat = sb("cstat", [NB, NB])
    starts = sb("starts", [NB, NB])
    sm8 = sb("sm8", [NB, 1])
    scr8 = sb("scr8", [NB, NB])
    myrow = sb("myrow", [1, 2 * NB])
    owcnt_f = sb("owcnt_f", [1, 2 * NB])
    owcnt = sb("owcnt", [1, 2 * NB], I32)
    lm16 = sb("lm16", [16, 256, NB])
    sg_in = sb("sg_in", [16, 256])
    sg_in2 = sb("sg_in2", [16, 256])
    sg_out = sb("sg_out", [16, 48], I32)
    sg_outf = sb("sg_outf", [16, 48])
    sg_nf = sb("sg_nf", [1, 1], U32)
    nfb = sb("nfb", [1, 1])
    m48 = sb("m48", [16, 48])
    m48b = sb("m48b", [16, 48])
    m48c = sb("m48c", [16, 48])
    sg_ids = sb("sg_ids", [16, 48])
    sg_scv = sb("sg_scv", [16, 48])
    mcol = sb("mcol", [16, 4])
    gl32 = sb("gl32", [16, 256], I32)
    glf = sb("glf", [16, 256])
    gl_idx = sb("gl_idx", [128, 256], I16)
    ow_g = sb("ow_g", [128, NB, 48], I16)
    ow_s = sb("ow_s", [128, NB, 48], I16)
    lsl = sb("lsl", [16, 256])
    stats = sb("stats", [128, 16])
    sq_scr = sb("sq_scr", [128, 1024])
    sq_scr2 = sb("sq_scr2", [128, 1024])
    gb = sb("gb", [128, 2, 1024], BF16)
    gcm = sb("gcm", [128, 2, NCT, 128], BF16)
    scem = sb("scem", [NB, 128])
    sct = sb("sct", [128, NB])
    p_ex = mkpsum("p_ex", [128, 32])
    p_tr = mkpsum("p_tr", [128, 512], BF16)
    p_tr2 = mkpsum("p_tr2", [128, 512], BF16)
    p_mm0 = mkpsum("p_mm0", [128, 512])
    p_mm1 = mkpsum("p_mm1", [128, 512])
    p_mm2 = mkpsum("p_mm2", [128, 512])
    p_mm3 = mkpsum("p_mm3", [128, 512])
    p_sc = mkpsum("p_sc", [128, 128])

    s_q = mksem("s_q"); s_qd = mksem("s_qd")
    s_v = mksem("s_v"); s_vd = mksem("s_vd")
    s_s = mksem("s_s"); s_sd = mksem("s_sd")
    s_t = mksem("s_t"); s_td = mksem("s_td")
    s_g = mksem("s_g"); s_gd = mksem("s_gd")
    s_cc = mksem("s_cc")

    Q = Rec("sync", s_q, s_qd)
    V = Rec("vector", s_v, s_vd)
    S = Rec("scalar", s_s, s_sd)
    TE = Rec("tensor", s_t, s_td)
    G = Rec("gpsimd", s_g, s_gd)
    ccs = [0]

    Q.dma(x2_plane.ap(), x_in.ap())
    for dst, src in [(ident[:, :], ident_in.ap()),
                     (chv[:, :, 0:5], chvec_in.ap()),
                     (rkv[:, :], rkv_in.ap()),
                     (ohcol[:, :], ohcol_in.ap()),
                     (g16[:, :], g16_in.ap()),
                     (h16[:, :], h16_in.ap()),
                     (tri[:, :], tri_in.ap()),
                     (onesr[:, :], onesr_in.ap()),
                     (ones128[:, :], ones128_in.ap()),
                     (tidb[:, :], tidb_in.ap()),
                     (selu16[:, :], selu_in.ap()),
                     (identf8[:, :], identf8_in.ap()),
                     (ones2d[:, :], ones2d_in.ap())]:
        G.dma(dst, src)
    G.dma(wgt[:, :, :], wg_in.ap().rearrange("(ct p) e -> p ct e", p=128))
    m_consts = G.dmark()
    G.iota(iota256[:, :], [[16, 256]], 0, 0)
    G.iota(iota48w[:, :], [[16, 48]], 0, 1)
    G.iota(iotap16[:, :], [[0, 1]], 0, 1)
    G.iota(iotaloc[:, :], [[16, 256]], 0, 1)

    V.wait(m_consts)
    V.memset(zero16[:, :], 0.0)
    V.ts(chv[:, :, 5:8], chv[:, :, 2:5], -1.0, 1.0, ALU.mult, ALU.add)

    S.wait(m_consts)
    S.act(stats[:, 0:8], chv[:, :, 0:1].rearrange("p a b -> p (a b)"), ACT.Exp)
    S.nop()
    S.act(dvec[:, :, 0:1].rearrange("p a b -> p (a b)"), stats[:, 0:8],
          ACT.Exp, scale=-1.0)
    S.act(dvec[:, :, 1:2].rearrange("p a b -> p (a b)"),
          chv[:, :, 1:2].rearrange("p a b -> p (a b)"), ACT.Exp)
    m_dvec = S.mark()

    TE.wait(m_consts)
    TE.mm(p_ex[:, 0:8], onesr[:, :], rkv[0:1, 0:8], True, True)
    TE.minc()
    S.wait(TE.mark())
    S.act(oh128[:, :], p_ex[:, 0:8], ACT.Copy)
    m_oh128 = S.mark()
    V.wait(m_oh128)
    V.tt(cnt[:, :], g16[:, :], oh128[:, :], ALU.mult)
    V.reduce(w128[:, :], cnt[:, :], ALU.add)
    m_w128 = V.mark()

    lg_prev = {"tr": None, "gstore": None, "mm": None, "sct_q": None}

    def lngate(i, xin_ap, xwait=None):
        if xwait is not None:
            V.wait(xwait)
            S.wait(xwait)
        V.ts(sq_scr[:, :], xin_ap, 1.0, None, ALU.mult, ALU.add,
             accum=stats[:, 0:1])
        S.act(sq_scr2[:, :], xin_ap, ACT.Square, accum=stats[:, 1:2])
        m_sq = S.mark()
        V.wait(m_sq)
        V.drain()
        V.ts(stats[:, 2:3], stats[:, 0:1], -1.0 / C)
        V.tt(stats[:, 4:5], stats[:, 2:3], stats[:, 2:3], ALU.mult)
        V.ts(stats[:, 6:7], stats[:, 1:2], 1.0 / C, EPS, ALU.mult, ALU.add)
        V.tt(stats[:, 3:4], stats[:, 6:7], stats[:, 4:5], ALU.subtract)
        V.recip(stats[:, 7:8], stats[:, 3:4])
        m_rc = V.mark()
        S.wait(m_rc)
        S.act(stats[:, 8:9], stats[:, 7:8], ACT.Sqrt)
        S.nop()
        S.act(stats[:, 9:10], stats[:, 2:3], ACT.Identity, scale=stats[:, 8:9])
        m_bias = S.mark()
        V.wait(m_bias)
        V.wait(lg_prev["tr"])
        V.wait(lg_prev["gstore"])
        par = i % 2
        V.ts(gb[:, par, :], xin_ap, stats[:, 8:9], stats[:, 9:10],
             ALU.mult, ALU.add)
        m_g = V.mark()
        Q.wait(m_g)
        gst = Q.dma(g_mine.ap()[i * 128:(i + 1) * 128, :], gb[:, par, :])
        TE.wait(m_g)
        TE.wait(lg_prev["mm"])
        for h in range(2):
            pt = p_tr if h == 0 else p_tr2
            for j in range(4):
                ct = h * 4 + j
                TE.trans(pt[:, j * 128:(j + 1) * 128],
                         gb[:, par, ct * 128:(ct + 1) * 128], ident[:, :])
        TE.minc()
        m_tr = TE.mark()
        S.wait(m_tr)
        S.act(gcm[:, par, 0:4, :].rearrange("p a b -> p (a b)"), p_tr[:, :],
              ACT.Copy)
        S.act(gcm[:, par, 4:8, :].rearrange("p a b -> p (a b)"), p_tr2[:, :],
              ACT.Copy)
        m_gcm = S.mark()
        TE.wait(m_gcm)
        TE.wait(lg_prev["sct_q"])
        for ct in range(NCT):
            TE.mm(p_sc[0:NB, :], wgt[:, ct, :], gcm[:, par, ct, :],
                  ct == 0, ct == NCT - 1)
        TE.minc()
        m_mm = TE.mark()
        S.wait(m_mm)
        S.act(scem[:, :], p_sc[0:NB, :], ACT.Copy)
        m_scem = S.mark()
        TE.wait(m_scem)
        TE.trans(p_ex[:, 8:16], scem[:, :], identf8[:, :])
        TE.minc()
        m_sct = TE.mark()
        S.wait(m_sct)
        S.act(sct[:, :], p_ex[:, 8:16], ACT.Exp)
        m_exp = S.mark()
        V.wait(m_exp)
        V.reduce(stats[:, 10:11], sct[:, :], ALU.add)
        V.recip(stats[:, 11:12], stats[:, 10:11])
        V.ts(sct[:, :], sct[:, :], stats[:, 11:12])
        m_sm = V.mark()
        Q.wait(m_sm)
        scq = Q.dma(sc_mine.ap().rearrange("q (f e) -> q f e", e=NB)
                    [:, 8 * i:8 * i + 8, :].rearrange("q w e -> w q e"),
                    sct[:, :])
        lg_prev["tr"] = m_tr
        lg_prev["gstore"] = gst
        lg_prev["mm"] = m_mm
        lg_prev["sct_q"] = scq

    p1ctx = ExitStack()
    xt2 = sb("xt2", [128, 2, 1024], ctx=p1ctx)
    ldm = [None] * 34
    ldm[0] = Q.dma(xt2[:, 0, :], x_in.ap()[0:128, :])
    ldm[1] = Q.dma(xt2[:, 1, :], x_in.ap()[128:256, :])
    for i in range(32):
        lngate(i, xt2[:, i % 2, :], xwait=ldm[i])
        if i + 2 < 32:
            Q.wait(V.mark())
            ldm[i + 2] = Q.dma(xt2[:, i % 2, :],
                               x_in.ap()[(i + 2) * 128:(i + 3) * 128, :])
    p1ctx.close()
    m_lg1_q = Q.dmark()

    G.wait(m_lg1_q)
    m_ag_sc1 = G.cc("AllGather", ALU.bypass, [sc_mine.ap().opt()],
                    [sc_all.ap().opt()], s_cc, ccs)
    m_ag_g1 = G.cc("AllGather", ALU.bypass, [g_mine.ap().opt()],
                   [g_all.ap().opt()], s_cc, ccs)

    def emit_selection(m_ag_sc, selset, m_wfree=()):
        selsc, selmsk, seltie, selmb, selscr, selscr8 = selset
        Q.wait(m_ag_sc)
        Q.dma(selsc[:, :, :],
              sc_all.ap().rearrange("p (f e) -> p f e", e=NB))
        m_sel_ld = Q.dmark()
        for mk in m_wfree:
            G.wait(mk)
        G.dma(w_ids.ap(), zero16[:, :])
        G.dma(w_sc.ap(), zero16[:, :])

        V.wait(m_sel_ld)
        V.ts(tid[:, :], iota256[:, :], tidb[:, 0:1], None, ALU.add)
        V.memset(bnds[:, 0:8], 0.0)
        V.memset(bnds[:, 8:16], 1.0)

        for it in range(VAL_ITERS):
            V.tt(bnds[:, 16:24], bnds[:, 0:8], bnds[:, 8:16], ALU.add)
            V.ts(bnds[:, 16:24], bnds[:, 16:24], 0.5)
            for e in range(NB):
                V.ts(selscr[:, :], selsc[:, :, e], bnds[:, 16 + e:17 + e],
                     None, ALU.is_gt, ALU.add, accum=cnt[:, e:e + 1])
            TE.wait(V.mark())
            TE.mm(p_ex[:, 16:24], ones2d[:, :], cnt[:, :], True, True)
            TE.minc()
            V.wait(TE.mark())
            V.ts(bnds[:, 24:32], p_ex[:, 16:24], float(CAP), None, ALU.is_ge)
            V.tt(bnds[:, 32:40], bnds[:, 16:24], bnds[:, 0:8], ALU.subtract)
            V.tt(bnds[:, 40:48], bnds[:, 24:32], bnds[:, 32:40], ALU.mult)
            V.tt(bnds[:, 0:8], bnds[:, 0:8], bnds[:, 40:48], ALU.add)
            V.tt(bnds[:, 32:40], bnds[:, 8:16], bnds[:, 16:24], ALU.subtract)
            V.tt(bnds[:, 40:48], bnds[:, 24:32], bnds[:, 32:40], ALU.mult)
            V.tt(bnds[:, 8:16], bnds[:, 16:24], bnds[:, 40:48], ALU.add)

        for e in range(NB):
            V.ts(selmsk[:, :, e], selsc[:, :, e], bnds[:, 8 + e:9 + e],
                 None, ALU.is_gt, ALU.add, accum=cnt[:, e:e + 1])
        for e in range(NB):
            V.ts(seltie[:, :, e], selsc[:, :, e], bnds[:, e:e + 1],
                 None, ALU.is_gt)
        for e in range(NB):
            V.tt(seltie[:, :, e], seltie[:, :, e], selmsk[:, :, e],
                 ALU.subtract)
        TE.wait(V.mark())
        TE.mm(p_ex[:, 16:24], ones2d[:, :], cnt[:, :], True, True)
        TE.minc()
        V.wait(TE.mark())
        V.ts(bnds[:, 40:48], p_ex[:, 16:24], -1.0, float(CAP),
             ALU.mult, ALU.add)
        V.copy(selscr8[:, 0:1, 0], bnds[:, 40:48].rearrange(
            "p (a e) -> p a e", a=1)[:, 0, 0:1])
        V.copy(bnd[:, 48:56], bnds[0:1, 40:48])
        V.memset(bnds[:, 0:8], 0.0)
        V.memset(bnds[:, 8:16], float(NT))

        for it in range(TIE_ITERS):
            V.tt(bnds[:, 16:24], bnds[:, 0:8], bnds[:, 8:16], ALU.add)
            V.ts(bnds[:, 16:24], bnds[:, 16:24], 0.5)
            for e in range(NB):
                V.stt(selscr[:, :], tid[:, :], bnds[:, 16 + e:17 + e],
                      seltie[:, :, e], ALU.is_lt, ALU.mult,
                      accum=cnt[:, e:e + 1])
            TE.wait(V.mark())
            TE.mm(p_ex[:, 16:24], ones2d[:, :], cnt[:, :], True, True)
            TE.minc()
            V.wait(TE.mark())
            V.tt(bnds[:, 24:32], p_ex[:, 16:24], bnds[:, 40:48], ALU.is_ge)
            V.tt(bnds[:, 32:40], bnds[:, 16:24], bnds[:, 8:16], ALU.subtract)
            V.tt(scr8b[:, 0:8], bnds[:, 24:32], bnds[:, 32:40], ALU.mult)
            V.tt(bnds[:, 8:16], bnds[:, 8:16], scr8b[:, 0:8], ALU.add)
            V.tt(bnds[:, 32:40], bnds[:, 0:8], bnds[:, 16:24], ALU.subtract)
            V.tt(scr8b[:, 0:8], bnds[:, 24:32], bnds[:, 32:40], ALU.mult)
            V.tt(bnds[:, 0:8], bnds[:, 16:24], scr8b[:, 0:8], ALU.add)

        for e in range(NB):
            V.stt(selscr[:, :], tid[:, :], bnds[:, 8 + e:9 + e],
                  seltie[:, :, e], ALU.is_lt, ALU.mult)
            V.tt(selmsk[:, :, e], selmsk[:, :, e], selscr[:, :], ALU.add)
        for e in range(NB):
            V.ts(selscr[:, :], selmsk[:, :, e], 1.0, None, ALU.mult, ALU.add,
                 accum=cnt[:, e:e + 1])
        V.wait(m_w128)
        V.tt(selmb[:, :, :], selmsk[:, :, :],
             w128[:, 0:1].rearrange("p (a b) -> p a b", a=1)
             .to_broadcast([128, 256, NB]), ALU.mult)
        m_fmask = V.mark()
        TE.wait(m_fmask)
        TE.mm(p_ex[0:8, 24:32], g16[:, :], cnt[:, :], True, True)
        TE.minc()
        S.wait(TE.mark())
        S.act(cstat[:, :], p_ex[0:8, 24:32], ACT.Copy)
        TE.wait(S.mark())
        TE.mm(p_ex[0:8, 16:24], tri[:, :], cstat[:, :], True, True)
        TE.minc()
        S.wait(TE.mark())
        S.act(starts[:, :], p_ex[0:8, 16:24], ACT.Copy)
        m_starts = S.mark()
        TE.wait(m_starts)
        TE.mm(p_ex[0:1, 16:24], ohcol[:, :], starts[:, :], True, True)
        TE.mm(p_ex[0:1, 24:32], ohcol[:, :], cstat[:, :], True, True)
        TE.minc()
        S.wait(TE.mark())
        S.act(myrow[:, 0:8], p_ex[0:1, 16:24], ACT.Copy)
        S.act(myrow[:, 8:16], p_ex[0:1, 24:32], ACT.Copy)
        m_myrow = S.mark()

        V.wait(m_starts)
        V.tt(scr8[:, :], starts[:, :], oh128[0:8, :], ALU.mult)
        V.reduce(sm8[:, :], scr8[:, :], ALU.add)
        ohb = oh128[:, 0:NB].rearrange("p (a e) -> p a e", a=1) \
            .to_broadcast([128, 256, NB])
        V.tt(selscr8[:, :, :], selmsk[:, :, :], ohb, ALU.mult)
        V.reduce(selscr[:, :], selscr8[:, :, :], ALU.add)
        V.tt(selscr8[:, :, :], selsc[:, :, :], ohb, ALU.mult)
        V.reduce(csc[:, :], selscr8[:, :, :], ALU.add)
        V.ts(cid[:, :], tid[:, :], 1.0, 2.0, ALU.mult, ALU.add)
        V.tt(cid[:, :], cid[:, :], selscr[:, :], ALU.mult)
        V.ts(cid[:, :], cid[:, :], 1.0, -1.0, ALU.mult, ALU.add)
        V.ts(csc[:, :], csc[:, :], 1.0, 1.0, ALU.mult, ALU.add)
        V.tt(csc[:, :], csc[:, :], selscr[:, :], ALU.mult)
        V.ts(csc[:, :], csc[:, :], 1.0, -1.0, ALU.mult, ALU.add)
        m_cands = V.mark()

        TE.wait(m_cands)
        for ncol in range(4):
            if ncol > 0:
                TE.wait(S.mark())
            TE.mm(p_mm0[0:16, :], h16[:, :],
                  selmb[:, :, :].rearrange("p f e -> p (f e)")
                  [:, ncol * 512:(ncol + 1) * 512], True, True)
            TE.minc()
            S.wait(TE.mark())
            S.act(lm16[:, :, :].rearrange("p f e -> p (f e)")
                  [:, ncol * 512:(ncol + 1) * 512], p_mm0[0:16, :], ACT.Copy)
        m_lm16 = S.mark()
        return dict(cands=m_cands, myrow=m_myrow, lm16=m_lm16)

    def emit_merge(msel):
        G.lib(LC.sparse_gather)
        G.wait(msel["cands"])
        m_prev_v = None
        m_prev_mg = None
        for u in range(NB):
            G.wait(m_prev_v)
            G.dma(sg_in[:, :], cid[16 * u:16 * u + 16, :])
            G.dma(sg_in2[:, :], csc[16 * u:16 * u + 16, :])
            m_mv = G.dmark()
            G.wait(m_mv)
            G.sg(sg_out[:, :], sg_in[:, :], sg_nf[:, :])
            m_sg1 = G.mark()
            V.wait(m_sg1)
            V.copy(nfb[:, :], sg_nf[:, :])
            TE.wait(V.mark())
            TE.mm(p_ex[0:16, 16:17], onesr[0:1, 0:16], nfb[0:1, 0:1],
                  True, True)
            TE.mm(p_ex[0:16, 17:18], selu16[:, 16 * u:16 * u + 16],
                  sm8[:, :], True, True)
            TE.minc()
            V.wait(TE.mark())
            V.wait(m_prev_mg)
            V.ts(m48[:, :], iota48w[:, :], p_ex[0:16, 16:17], None, ALU.is_lt)
            V.tt(sg_ids[:, :], sg_out[:, :], m48[:, :], ALU.mult)
            V.ts(mcol[:, 0:1], p_ex[0:16, 17:18], 1.0)
            for kpow in (2048.0, 1024.0, 512.0, 256.0, 128.0, 64.0, 32.0,
                         16.0):
                V.ts(mcol[:, 1:2], mcol[:, 0:1], kpow, None, ALU.is_ge)
                V.stt(mcol[:, 0:1], mcol[:, 1:2], -kpow, mcol[:, 0:1],
                      ALU.mult, ALU.add)
            V.tt(mcol[:, 1:2], p_ex[0:16, 17:18], mcol[:, 0:1], ALU.subtract)
            V.ts(mcol[:, 1:2], mcol[:, 1:2], 1.0 / 16)
            V.tt(mcol[:, 2:3], mcol[:, 0:1], iotap16[:, :], ALU.add)
            V.ts(mcol[:, 3:4], mcol[:, 2:3], 16.0, None, ALU.is_ge)
            V.stt(mcol[:, 2:3], mcol[:, 3:4], -16.0, mcol[:, 2:3],
                  ALU.mult, ALU.add)
            V.tt(mcol[:, 3:4], mcol[:, 1:2], mcol[:, 3:4], ALU.add)
            V.stt(mcol[:, 1:2], mcol[:, 2:3], float(WPAD), mcol[:, 3:4],
                  ALU.mult, ALU.add)
            V.copy(idxcol[0:16, 0:1], mcol[:, 1:2])
            m_idx = V.mark()
            G.wait(m_idx)
            G.imerge(w_ids.ap().rearrange("(a r) c -> a (r c)", a=1),
                     idxcol[0:16, 0:1], sg_ids[:, :])
            G.sg(sg_outf[:, :], sg_in2[:, :], sg_nf[:, :])
            m_sg2 = G.mark()
            V.wait(m_sg2)
            V.tt(sg_scv[:, :], sg_outf[:, :], m48[:, :], ALU.mult)
            m_scv = V.mark()
            G.wait(m_scv)
            G.imerge(w_sc.ap().rearrange("(a r) c -> a (r c)", a=1),
                     idxcol[0:16, 0:1], sg_scv[:, :])
            m_prev_v = V.mark()
            m_prev_mg = G.dmark()
        return G.dmark()

    def emit_owner_lists(msel, m_merge):
        V.wait(msel["lm16"])
        V.wait(msel["myrow"])
        V.wait(m_merge)
        for e in range(NB):
            V.ts(lsl[:, :], iotaloc[:, :], 1.0, 2.0, ALU.mult, ALU.add)
            V.tt(lsl[:, :], lsl[:, :], lm16[:, :, e], ALU.mult)
            V.ts(lsl[:, :], lsl[:, :], 1.0, -1.0, ALU.mult, ALU.add)
            m_c = V.mark()
            G.wait(m_c)
            G.sg(sg_out[:, :], lsl[:, :], sg_nf[:, :])
            m_sg = G.mark()
            V.wait(m_sg)
            V.copy(nfb[:, :], sg_nf[:, :])
            TE.wait(V.mark())
            TE.mm(p_ex[0:16, 16:17], onesr[0:1, 0:16], nfb[0:1, 0:1],
                  True, True)
            TE.mm(p_ex[0:16, 17:18], onesr[0:1, 0:16], myrow[0:1, e:e + 1],
                  True, True)
            TE.minc()
            V.wait(TE.mark())
            V.ts(m48[:, :], iota48w[:, :], p_ex[0:16, 16:17], None, ALU.is_lt)
            V.tt(sg_scv[:, :], sg_out[:, :], m48[:, :], ALU.mult)
            V.ts(sg_scv[:, :], sg_scv[:, :], 1.0, -1.0, ALU.mult, ALU.add)
            V.copy(ow_s[0:16, e, :], sg_scv[:, :])
            V.ts(sg_scv[:, :], iota48w[:, :], p_ex[0:16, 17:18], None, ALU.add)
            V.ts(m48b[:, :], sg_scv[:, :], 2048.0, None, ALU.is_ge)
            V.stt(sg_scv[:, :], m48b[:, :], -2048.0, sg_scv[:, :],
                  ALU.mult, ALU.add)
            V.ts(m48c[:, :], sg_scv[:, :], 1024.0, None, ALU.is_ge)
            V.stt(sg_scv[:, :], m48c[:, :], -1024.0, sg_scv[:, :],
                  ALU.mult, ALU.add)
            V.stt(m48c[:, :], m48b[:, :], 2.0, m48c[:, :], ALU.mult, ALU.add)
            V.stt(sg_scv[:, :], m48c[:, :], 8192.0, sg_scv[:, :],
                  ALU.mult, ALU.add)
            V.ts(sg_scv[:, :], sg_scv[:, :], 1.0, float(e * 1024) + 1.0,
                 ALU.mult, ALU.add)
            V.tt(sg_scv[:, :], sg_scv[:, :], m48[:, :], ALU.mult)
            V.ts(sg_scv[:, :], sg_scv[:, :], 1.0, -1.0, ALU.mult, ALU.add)
            V.copy(ow_g[0:16, e, :], sg_scv[:, :])
        ohf = owcnt_f.rearrange("a (e h) -> a e h", h=2)
        V.ts(ohf[:, :, 0], myrow[0:1, 8:16], 384.0, None, ALU.min)
        V.tt(ohf[:, :, 1], myrow[0:1, 8:16], ohf[:, :, 0], ALU.subtract)
        V.copy(owcnt[:, :], owcnt_f[:, :])
        m_ow0 = V.mark()
        Q.wait(m_ow0)
        Q.dma(ow_s[16:32, :, :], ow_s[0:16, :, :])
        Q.dma(ow_g[16:32, :, :], ow_g[0:16, :, :])
        Q.wait(Q.dmark())
        Q.dma(ow_s[32:64, :, :], ow_s[0:32, :, :])
        Q.dma(ow_g[32:64, :, :], ow_g[0:32, :, :])
        Q.wait(Q.dmark())
        Q.dma(ow_s[64:128, :, :], ow_s[0:64, :, :])
        Q.dma(ow_g[64:128, :, :], ow_g[0:64, :, :])
        return Q.dmark()

    def emit_gl(m_merge, scrow, scband):
        Q.wait(m_merge)
        Q.dma(gl32[:, :], w_ids.ap()[:, 0:256])
        V.wait(Q.dmark())
        V.ts(glf[:, :], gl32[:, :], 1.0, -1.0, ALU.mult, ALU.add)
        V.copy(gl_idx[0:16, :], glf[:, :])
        m_gl0 = V.mark()
        Q.wait(m_gl0)
        Q.dma(gl_idx[16:32, :], gl_idx[0:16, :])
        Q.wait(Q.dmark())
        Q.dma(gl_idx[32:64, :], gl_idx[0:32, :])
        Q.wait(Q.dmark())
        Q.dma(gl_idx[64:128, :], gl_idx[0:64, :])
        m_gl = Q.dmark()
        for rr in range(16):
            Q.dma(lin_sc.ap().rearrange("a (c r) -> (a c) r", r=16)[:, rr:rr + 1],
                  w_sc.ap()[rr:rr + 1, 0:256].rearrange("r c -> c r"),
                  allow_slow_non_contiguous=True)
        m_lin = Q.dmark()
        ret_lin = m_lin
        G.lib(LC.mlp)
        G.wait(m_lin)
        m_scrow = G.dma(scrow[:, :], lin_sc.ap())
        G.wait(m_scrow)
        G.pbcast(scband[:, :], scrow[:, :], 128)
        return dict(gl=m_gl, scband=G.mark(), lin=ret_lin)

    def emit_ffn(mrg, m_ag_g, fset, m_eo_free):
        fw1, fw3, fw2db, rt, hbuf, sil, outt, scrow, scband = fset
        mw = G.dmark()
        G.dma(fw1[:, :, :, :],
              w1_in.ap().rearrange("(ct p) (ht j) -> p ct ht j", p=128, j=128))
        G.dma(fw3[:, :, :, :],
              w3_in.ap().rearrange("(ct p) (ht j) -> p ct ht j", p=128, j=128))
        m_w13 = G.dmark()
        G.wait(mrg["gl"])
        G.wait(m_ag_g)
        G.dma(fw2db[:, :, :], w2_in.ap().rearrange("(ht p) d -> p ht d", p=128))
        m_w2 = G.dmark()
        gmk = [None] * 4
        rt_free = None
        m_eo_prev = None
        ag_marks = []
        for k in range(4):
            G.wait(rt_free)
            gmk[k] = G.gather(rt[:, :, :],
                              g_all.ap(), gl_idx[:, k * 64:(k + 1) * 64],
                              1024, C, transpose=True)
            V.wait(gmk[k])
            V.wait(mrg["scband"])
            V.tt(rt[:, :, :], rt[:, :, :],
                 scband[:, k * 1024:(k + 1) * 1024]
                 .rearrange("p (a b) -> p a b", a=1)
                 .to_broadcast([128, NCT, 1024]), ALU.mult)
            m_rt = V.mark()

            for n in range(2):
                kk = 2 * k + n
                TE.wait(m_rt)
                TE.wait(S.mark())
                TE.wait(m_w13)
                vm = [None] * NHT
                for ht in range(NHT):
                    pmu = p_mm0 if ht % 2 == 0 else p_mm2
                    pmv = p_mm1 if ht % 2 == 0 else p_mm3
                    if ht >= 2:
                        TE.wait(vm[ht - 2])
                    for ct in range(NCT):
                        TE.mm(pmu[:, :], fw1[:, ct, ht, :],
                              rt[:, ct, n * 512:(n + 1) * 512],
                              ct == 0, ct == NCT - 1)
                    for ct in range(NCT):
                        TE.mm(pmv[:, :], fw3[:, ct, ht, :],
                              rt[:, ct, n * 512:(n + 1) * 512],
                              ct == 0, ct == NCT - 1)
                    TE.minc()
                    m_u = TE.mark()
                    S.wait(m_u)
                    S.act(sil[:, ht % 2, :], pmu[:, :], ACT.Silu)
                    m_sil = S.mark()
                    V.wait(m_sil)
                    V.stt(hbuf[:, ht, :], pmv[:, :], 0.5, sil[:, ht % 2, :],
                          ALU.mult, ALU.mult)
                    vm[ht] = V.mark()
                m_h = V.mark()
                TE.wait(m_h)
                TE.wait(m_w2)
                sm4 = [None] * 4
                for st4 in range(4):
                    pma = p_mm2 if st4 % 2 == 0 else p_mm0
                    pmb = p_mm3 if st4 % 2 == 0 else p_mm1
                    if st4 >= 2:
                        TE.wait(sm4[st4 - 2])
                    if st4 == 0:
                        S.wait(m_eo_prev)
                    for dn in range(2):
                        pm = pma if dn == 0 else pmb
                        for ht in range(NHT):
                            TE.mm(pm[:, :],
                                  hbuf[:, ht, st4 * 128:(st4 + 1) * 128],
                                  fw2db[:, ht, dn * 512:(dn + 1) * 512],
                                  ht == 0, ht == NHT - 1)
                    TE.minc()
                    m_o = TE.mark()
                    S.wait(m_o)
                    S.act(outt[:, st4, 0:512], pma[:, :], ACT.Copy)
                    S.act(outt[:, st4, 512:1024], pmb[:, :], ACT.Copy)
                    sm4[st4] = S.mark()
                m_ev = S.mark()
                Q.wait(m_ev)
                if kk == 0:
                    Q.wait(m_eo_free)
                Q.dma(eo_mine.ap().rearrange("(a p) d -> p a d", p=128)
                      [:, kk * 4:(kk + 1) * 4, :], outt[:, :, :])
                m_eo_prev = Q.dmark()
                if kk % 2 == 1:
                    kc = kk // 2
                    G.wait(Q.dmark())
                    ag_marks.append(G.cc(
                        "AllGather", ALU.bypass,
                        [eo_mine.ap()[1024 * kc:1024 * (kc + 1), :].opt()],
                        [eo_all.ap()[8192 * kc:8192 * (kc + 1), :].opt()],
                        s_cc, ccs))
            rt_free = TE.mark()
        return ag_marks[-1]

    def emit_combine(m_ag_eo, m_owner, target_ap, m_target_ready, cset):
        cmb_b, cmb_f = cset
        G.wait(m_ag_eo)
        G.wait(m_owner)
        G.wait(m_target_ready)
        prev_cast = None
        for e in range(NB):
            for hhalf in range(2):
                G.wait(prev_cast)
                mg = G.gather(cmb_b[:, :, :], eo_all.ap(),
                              ow_g[:, e, hhalf * 24:(hhalf + 1) * 24],
                              384, C, transpose=False,
                              count_src=owcnt[0:1, 2 * e + hhalf:
                                              2 * e + hhalf + 1])
                V.wait(mg)
                V.copy(cmb_f[:, :, :].rearrange("p a b -> p (a b)"),
                       cmb_b[:, :, :].rearrange("p a b -> p (a b)"))
                m_cast = V.mark()
                prev_cast = m_cast
                G.wait(m_cast)
                G.scatter_add(target_ap, cmb_f[:, :, :],
                              ow_s[:, e, hhalf * 24:(hhalf + 1) * 24],
                              384, C,
                              count_src=owcnt[0:1, 2 * e + hhalf:
                                              2 * e + hhalf + 1])
        return G.dmark()

    sel_ctx = ExitStack()
    selset = (sb("selsc", [128, 256, NB], ctx=sel_ctx),
              sb("selmsk", [128, 256, NB], ctx=sel_ctx),
              sb("seltie", [128, 256, NB], ctx=sel_ctx),
              sb("selmb", [128, 256, NB], BF16, ctx=sel_ctx),
              sb("selscr", [128, 256], ctx=sel_ctx),
              sb("selscr8", [128, 256, NB], ctx=sel_ctx))
    msel1 = emit_selection(m_ag_sc1, selset)
    m_merge1 = emit_merge(msel1)
    m_owner1 = emit_owner_lists(msel1, m_merge1)
    sel_ctx.close()

    ffn_ctx = ExitStack()
    fset1 = (sb("fw1", [128, NCT, NHT, 128], BF16, ctx=ffn_ctx),
             sb("fw3", [128, NCT, NHT, 128], BF16, ctx=ffn_ctx),
             sb("fw2db", [128, NHT, 1024], BF16, ctx=ffn_ctx),
             sb("rt", [128, NCT, 1024], BF16, ctx=ffn_ctx),
             sb("hbuf", [128, NHT, 512], BF16, ctx=ffn_ctx),
             sb("sil", [128, 2, 512], BF16, ctx=ffn_ctx),
             sb("outt", [128, 4, 1024], BF16, ctx=ffn_ctx),
             sb("scrow", [1, CAP], BF16, ctx=ffn_ctx),
             sb("scband", [128, CAP], BF16, ctx=ffn_ctx))
    mgl1 = emit_gl(m_merge1, fset1[7], fset1[8])
    mrg1 = dict(gl=mgl1["gl"], scband=mgl1["scband"])
    m_ag_eo1 = emit_ffn(mrg1, m_ag_g1, fset1, None)
    ffn_ctx.close()
    cmb_ctx = ExitStack()
    cset1 = (sb("cmb_b", [128, 3, 1024], BF16, ctx=cmb_ctx),
             sb("cmb_f", [128, 3, 1024], ctx=cmb_ctx))
    m_cmb1 = emit_combine(m_ag_eo1, m_owner1, x2_plane.ap(), Q.dmark(), cset1)
    cmb_ctx.close()

    attn_ctx = ExitStack()
    wk4 = sb("wk4", [128, NCT, NCT, 128], BF16, ctx=attn_ctx)
    wv4 = sb("wv4", [128, NCT, NCT, 128], BF16, ctx=attn_ctx)
    wr4 = sb("wr4", [128, NCT, NCT, 128], BF16, ctx=attn_ctx)
    wo4 = sb("wo4", [128, NCT, 1024], BF16, ctx=attn_ctx)
    xt34 = sb("xt34", [128, 4, 1024], ctx=attn_ctx)
    g2cm = sb("g2cm", [128, NCT, CH + 1], BF16, ctx=attn_ctx)
    mixb = sb("mixb", [128, 3, CH], BF16, ctx=attn_ctx)
    xk = sb("xk", [128, NCT, CH], BF16, ctx=attn_ctx)
    xv = sb("xv", [128, NCT, CH], BF16, ctx=attn_ctx)
    xr = sb("xr", [128, NCT, CH], BF16, ctx=attn_ctx)
    kd = sb("kd", [128, CH], ctx=attn_ctx)
    vdd = sb("vdd", [128, 2, CH], ctx=attn_ctx)
    ad = sb("ad", [128, 2, CH], ctx=attn_ctx)
    avd = sb("avd", [128, CH], ctx=attn_ctx)
    srd = sb("srd", [128, 2, CH], BF16, ctx=attn_ctx)
    numb = sb("numb", [128, CH], ctx=attn_ctx)
    denb = sb("denb", [128, CH], ctx=attn_ctx)
    yb = sb("yb", [128, CH], ctx=attn_ctx)
    zb = sb("zb", [128, NCT, CH], BF16, ctx=attn_ctx)
    pbuf = sb("pbuf", [128, CH + 1], ctx=attn_ctx)
    qbuf = sb("qbuf", [128, CH + 1], ctx=attn_ctx)
    pcar = sb("pcar", [128, NCT], ctx=attn_ctx)
    qcar = sb("qcar", [128, NCT], ctx=attn_ctx)
    ddcur = sb("ddcur", [128, CH], ctx=attn_ctx)

    G.lib(LC.standard)
    G.wait(m_cmb1)
    G.wait(TE.mark())
    G.wait(S.mark())
    G.wait(V.mark())
    G.dma(wk4[:, :, :, :],
          wkT_in.ap().rearrange("(ct p) (dt j) -> p ct dt j", p=128, j=128))
    G.dma(wv4[:, :, :, :],
          wvT_in.ap().rearrange("(ct p) (dt j) -> p ct dt j", p=128, j=128))
    G.dma(wr4[:, :, :, :],
          wrT_in.ap().rearrange("(ct p) (dt j) -> p ct dt j", p=128, j=128))
    G.dma(wo4[:, :, :],
          woT_in.ap().rearrange("(ct p) d -> p ct d", p=128))
    m_aw = G.dmark()

    V.wait(m_dvec)
    V.wait(m_aw)
    V.memset(pcar[:, :], 0.0)
    V.memset(qcar[:, :], 0.0)
    V.memset(g2cm[:, :, 0:1].rearrange("p a b -> p (a b)"), 0.0)
    m_attn_init = V.mark()

    m_x2_ready = m_cmb1
    prev_xk_cons = None
    prev_xt_cons = None
    for c in range(NCH):
        Q.wait(m_x2_ready)
        Q.wait(prev_xt_cons)
        mx = Q.dma(xt34[:, :, :],
                   x2_plane.ap()[c * CH:(c + 1) * CH, :]
                   .rearrange("(a p) d -> p a d", p=128))
        if c > 0:
            V.copy(g2cm[:, :, 0:1].rearrange("p a b -> p (a b)"),
                   g2cm[:, :, CH:CH + 1].rearrange("p a b -> p (a b)"))
        for tt in range(4):
            V.wait(mx)
            S.wait(mx)
            V.wait(m_attn_init)
            V.ts(sq_scr[:, :], xt34[:, tt, :], 1.0, None, ALU.mult, ALU.add,
                 accum=stats[:, 0:1])
            S.act(sq_scr2[:, :], xt34[:, tt, :], ACT.Square,
                  accum=stats[:, 1:2])
            m_sq = S.mark()
            V.wait(m_sq)
            V.drain()
            V.ts(stats[:, 2:3], stats[:, 0:1], -1.0 / C)
            V.tt(stats[:, 4:5], stats[:, 2:3], stats[:, 2:3], ALU.mult)
            V.ts(stats[:, 6:7], stats[:, 1:2], 1.0 / C, EPS, ALU.mult,
                 ALU.add)
            V.tt(stats[:, 3:4], stats[:, 6:7], stats[:, 4:5], ALU.subtract)
            V.recip(stats[:, 7:8], stats[:, 3:4])
            m_rc = V.mark()
            S.wait(m_rc)
            S.act(stats[:, 8:9], stats[:, 7:8], ACT.Sqrt)
            S.nop()
            S.act(stats[:, 9:10], stats[:, 2:3], ACT.Identity,
                  scale=stats[:, 8:9])
            m_bias = S.mark()
            V.wait(m_bias)
            V.wait(lg_prev["tr"])
            V.ts(gb[:, tt % 2, :], xt34[:, tt, :], stats[:, 8:9],
                 stats[:, 9:10], ALU.mult, ALU.add)
            m_g2 = V.mark()
            TE.wait(m_g2)
            TE.wait(prev_xk_cons)
            for h in range(2):
                pt = p_tr if h == 0 else p_tr2
                for j in range(4):
                    ct = h * 4 + j
                    TE.trans(pt[:, j * 128:(j + 1) * 128],
                             gb[:, tt % 2, ct * 128:(ct + 1) * 128],
                             ident[:, :])
            TE.minc()
            m_tr = TE.mark()
            S.wait(m_tr)
            for h in range(2):
                pt = p_tr if h == 0 else p_tr2
                for j in range(4):
                    ct = h * 4 + j
                    S.act(g2cm[:, ct, 1 + tt * 128:1 + (tt + 1) * 128],
                          pt[:, j * 128:(j + 1) * 128], ACT.Copy)
            lg_prev["tr"] = m_tr
        m_g2cm = S.mark()

        V.wait(m_g2cm)
        S.wait(m_g2cm)
        for ct in range(NCT):
            for mi, t1col in enumerate([5, 6, 7]):
                S.act(mixb[:, mi, :], g2cm[:, ct, 0:CH], ACT.Identity,
                      scale=chv[:, ct, t1col:t1col + 1])
            m_mx = S.mark()
            V.wait(m_mx)
            V.stt(xk[:, ct, :], g2cm[:, ct, 1:CH + 1], chv[:, ct, 2:3],
                  mixb[:, 0, :], ALU.mult, ALU.add)
            V.stt(xv[:, ct, :], g2cm[:, ct, 1:CH + 1], chv[:, ct, 3:4],
                  mixb[:, 1, :], ALU.mult, ALU.add)
            V.stt(xr[:, ct, :], g2cm[:, ct, 1:CH + 1], chv[:, ct, 4:5],
                  mixb[:, 2, :], ALU.mult, ALU.add)
            S.wait(V.mark())
        m_mix = V.mark()

        m_z = None
        m_prev_evac = None
        m_z_hist = [None] * NCT
        for dt in range(NCT):
            par = dt % 2
            TE.wait(m_mix)
            TE.wait(m_aw)
            TE.wait(m_prev_evac)
            for ct in range(NCT):
                TE.mm(p_mm0[:, :], wk4[:, ct, dt, :], xk[:, ct, :],
                      ct == 0, ct == NCT - 1)
            for ct in range(NCT):
                TE.mm(p_mm1[:, :], wv4[:, ct, dt, :], xv[:, ct, :],
                      ct == 0, ct == NCT - 1)
            for ct in range(NCT):
                TE.mm(p_mm2[:, :], wr4[:, ct, dt, :], xr[:, ct, :],
                      ct == 0, ct == NCT - 1)
            TE.minc()
            m_kvr = TE.mark()
            S.wait(m_kvr)
            if dt >= 2:
                S.wait(m_z_hist[dt - 2])
            S.act(kd[:, :], p_mm0[:, :], ACT.Copy)
            S.act(vdd[:, par, :], p_mm1[:, :], ACT.Copy)
            S.act(srd[:, par, :], p_mm2[:, :], ACT.Sigmoid)
            S.nop()
            S.act(ad[:, par, :], kd[:, :], ACT.Exp)
            m_sc_dt = S.mark()
            m_prev_evac = m_sc_dt
            V.wait(m_sc_dt)
            V.ts(ddcur[:, :], dvec[:, dt, 0:1].to_broadcast([128, CH]), 1.0)
            V.tt(avd[:, :], ad[:, par, :], vdd[:, par, :], ALU.mult)
            V.copy(pbuf[:, 0:1], pcar[:, dt:dt + 1])
            V.copy(qbuf[:, 0:1], qcar[:, dt:dt + 1])
            V.scan(pbuf[:, 1:CH + 1], ddcur[:, :], avd[:, :],
                   pbuf[:, 0:1])
            V.scan(qbuf[:, 1:CH + 1], ddcur[:, :], ad[:, par, :],
                   qbuf[:, 0:1])
            V.copy(pcar[:, dt:dt + 1], pbuf[:, CH:CH + 1])
            V.copy(qcar[:, dt:dt + 1], qbuf[:, CH:CH + 1])
            V.stt(numb[:, :], avd[:, :], dvec[:, dt, 1:2],
                  pbuf[:, 0:CH], ALU.mult, ALU.add)
            V.stt(denb[:, :], ad[:, par, :], dvec[:, dt, 1:2],
                  qbuf[:, 0:CH], ALU.mult, ALU.add)
            V.recip(yb[:, :], denb[:, :])
            V.tt(yb[:, :], yb[:, :], numb[:, :], ALU.mult)
            V.tt(zb[:, dt, :], yb[:, :], srd[:, par, :], ALU.mult)
            m_z = V.mark()
            m_z_hist[dt] = m_z
        prev_xk_cons = m_z

        for tt in range(4):
            TE.wait(m_z)
            for dn in range(2):
                pm = p_mm2 if dn == 0 else p_mm3
                for ct in range(NCT):
                    TE.mm(pm[:, :], zb[:, ct, tt * 128:(tt + 1) * 128],
                          wo4[:, ct, dn * 512:(dn + 1) * 512],
                          ct == 0, ct == NCT - 1)
            TE.minc()
            m_wo = TE.mark()
            V.wait(m_wo)
            V.tt(xt34[:, tt, 0:512], xt34[:, tt, 0:512], p_mm2[:, :], ALU.add)
            V.tt(xt34[:, tt, 512:1024], xt34[:, tt, 512:1024], p_mm3[:, :],
                 ALU.add)
            m_x3 = V.mark()
            TE.wait(m_x3)
            Q.wait(m_x3)
            Q.dma(out_ext.ap()[(c * 4 + tt) * 128:(c * 4 + tt + 1) * 128, :],
                  xt34[:, tt, :])
            lngate(c * 4 + tt, xt34[:, tt, :], xwait=m_x3)
        prev_xt_cons = V.mark()
    m_attn_q = Q.dmark()
    m_attn_done = (V.mark(), TE.mark(), S.mark(), G.mark())
    attn_ctx.close()

    G.wait(m_attn_q)
    m_ag_sc2 = G.cc("AllGather", ALU.bypass, [sc_mine.ap().opt()],
                    [sc_all.ap().opt()], s_cc, ccs)
    m_ag_g2 = G.cc("AllGather", ALU.bypass, [g_mine.ap().opt()],
                   [g_all.ap().opt()], s_cc, ccs)

    sel_ctx2 = ExitStack()
    selset2 = (sb("selsc2", [128, 256, NB], ctx=sel_ctx2),
               sb("selmsk2", [128, 256, NB], ctx=sel_ctx2),
               sb("seltie2", [128, 256, NB], ctx=sel_ctx2),
               sb("selmb2", [128, 256, NB], BF16, ctx=sel_ctx2),
               sb("selscr2", [128, 256], ctx=sel_ctx2),
               sb("selscr82", [128, 256, NB], ctx=sel_ctx2))
    msel2 = emit_selection(m_ag_sc2, selset2,
                           m_wfree=(mgl1['gl'], mgl1['lin']))
    m_merge2 = emit_merge(msel2)
    m_owner2 = emit_owner_lists(msel2, m_merge2)
    sel_ctx2.close()

    ffn_ctx2 = ExitStack()
    fset2 = (sb("fw1b", [128, NCT, NHT, 128], BF16, ctx=ffn_ctx2),
             sb("fw3b", [128, NCT, NHT, 128], BF16, ctx=ffn_ctx2),
             sb("fw2db2", [128, NHT, 1024], BF16, ctx=ffn_ctx2),
             sb("rt2", [128, NCT, 1024], BF16, ctx=ffn_ctx2),
             sb("hbuf2", [128, NHT, 512], BF16, ctx=ffn_ctx2),
             sb("sil2", [128, 2, 512], BF16, ctx=ffn_ctx2),
             sb("outt2", [128, 4, 1024], BF16, ctx=ffn_ctx2),
             sb("scrow2", [1, CAP], BF16, ctx=ffn_ctx2),
             sb("scband2", [128, CAP], BF16, ctx=ffn_ctx2))
    mgl2 = emit_gl(m_merge2, fset2[7], fset2[8])
    mrg2 = dict(gl=mgl2["gl"], scband=mgl2["scband"])
    m_ag_eo2 = emit_ffn(mrg2, m_ag_g2, fset2, m_ag_eo1)
    ffn_ctx2.close()
    cmb_ctx2 = ExitStack()
    cset2 = (sb("cmb_b2", [128, 3, 1024], BF16, ctx=cmb_ctx2),
             sb("cmb_f2", [128, 3, 1024], ctx=cmb_ctx2))
    m_cmb2 = emit_combine(m_ag_eo2, m_owner2, out_ext.ap(), m_attn_q, cset2)
    cmb_ctx2.close()

    G.wait(m_cmb2)
    G.wait(Q.dmark())
    G.wait(V.mark())
    G.wait(S.mark())
    G.wait(TE.mark())

    globals()["_RECS"] = dict(Q=Q, V=V, S=S, TE=TE, G=G)
    with top, nc.Block() as block:
        @block.sync
        def _(e):
            Q.replay(e)

        @block.vector
        def _(e):
            V.replay(e)

        @block.scalar
        def _(e):
            S.replay(e)

        @block.tensor
        def _(e):
            TE.replay(e)

        @block.gpsimd
        def _(e):
            G.replay(e)

    return nc


_NC_CACHE = {}


def kernel(L0_attn_x, L0_attn_time_decay, L0_attn_time_first,
           L0_attn_time_mix_k, L0_attn_time_mix_v, L0_attn_time_mix_r,
           L0_attn_Wk, L0_attn_Wv, L0_attn_Wr, L0_attn_Wo,
           L0_ffn_wg, L0_ffn_w1, L0_ffn_w3, L0_ffn_w2):
    x = np.ascontiguousarray(np.asarray(L0_attn_x, dtype=np.float32))
    td = np.asarray(L0_attn_time_decay, np.float32).reshape(C)
    tf = np.asarray(L0_attn_time_first, np.float32).reshape(C)
    tmk = np.asarray(L0_attn_time_mix_k, np.float32).reshape(C)
    tmv = np.asarray(L0_attn_time_mix_v, np.float32).reshape(C)
    tmr = np.asarray(L0_attn_time_mix_r, np.float32).reshape(C)

    def chmaj(v):
        return v.reshape(NCT, 128).T

    chvec = np.ascontiguousarray(
        np.stack([chmaj(td), chmaj(tf), chmaj(tmk), chmaj(tmv), chmaj(tmr)],
                 axis=2).astype(np.float32))

    wkT = np.ascontiguousarray(np.asarray(L0_attn_Wk, np.float32).T)
    wvT = np.ascontiguousarray(np.asarray(L0_attn_Wv, np.float32).T)
    wrT = np.ascontiguousarray(np.asarray(L0_attn_Wr, np.float32).T)
    woT = np.ascontiguousarray(np.asarray(L0_attn_Wo, np.float32).T)
    wg = np.ascontiguousarray(np.asarray(L0_ffn_wg, np.float32))
    w1 = np.asarray(L0_ffn_w1, np.float32)
    w3 = np.asarray(L0_ffn_w3, np.float32)
    w2 = np.asarray(L0_ffn_w2, np.float32)

    identbf = np.eye(128, dtype=ml_dtypes.bfloat16)
    p = np.arange(128)
    g16c = (p[:, None] // 16 == np.arange(NB)[None, :]).astype(np.float32)
    h16c = (p[:, None] % 16 == np.arange(16)[None, :]).astype(ml_dtypes.bfloat16)
    tric = (np.arange(NB)[:, None] < np.arange(NB)[None, :]).astype(np.float32)
    onesrow = np.ones((1, 128), np.float32)
    ones128c = np.ones((128, 1), np.float32)
    tidbase = (4096.0 * (p // 16) + (p % 16)).astype(np.float32).reshape(128, 1)
    selu = np.zeros((NB, 128), np.float32)
    for u in range(NB):
        selu[u, 16 * u:16 * u + 16] = 1.0

    in_maps = []
    for r in range(NB):
        rkvc = np.zeros((1, 16), np.float32)
        rkvc[0, r] = 1.0
        rkvc[0, 8] = r
        rkvc[0, 9] = 4096.0 * r
        ohcolc = np.zeros((NB, 1), np.float32)
        ohcolc[r, 0] = 1.0
        in_maps.append({
            "x": np.ascontiguousarray(x[r]),
            "wkT": wkT, "wvT": wvT, "wrT": wrT, "woT": woT,
            "w1": np.ascontiguousarray(w1[r]),
            "w3": np.ascontiguousarray(w3[r]),
            "w2": np.ascontiguousarray(w2[r]),
            "wg": wg,
            "chvec": chvec,
            "rkv": rkvc,
            "ohcol": ohcolc,
            "identbf": identbf,
            "g16": g16c,
            "h16": h16c,
            "triexcl": tric,
            "onesrow": onesrow,
            "ones128": ones128c,
            "tidbase": tidbase,
            "selu16": selu,
            "identf8": np.eye(NB, dtype=np.float32),
            "ones2d": np.ones((128, 128), np.float32),
        })

    if "prog" not in _NC_CACHE:
        nc = build_program()
        nc.finalize()
        _NC_CACHE["prog"] = nc
    nc = _NC_CACHE["prog"]
    res = run_bass_kernel_spmd(nc, in_maps, core_ids=list(range(NB)))
    out = np.stack([np.asarray(res.results[r]["out"], np.float32)
                    for r in range(NB)], axis=0)
    return out


# version 11
# speedup vs baseline: 1.1830x; 1.0227x over previous
import numpy as np
import ml_dtypes
from contextlib import ExitStack

import concourse.bacc as bacc
import concourse.mybir as mybir
from concourse.bass import IndirectOffsetOnAxis
from concourse.bass_utils import run_bass_kernel_spmd
from concourse import library_config as LC

F32 = mybir.dt.float32
BF16 = mybir.dt.bfloat16
I16 = mybir.dt.int16
I32 = mybir.dt.int32
U32 = mybir.dt.uint32
ALU = mybir.AluOpType
ACT = mybir.ActivationFunctionType
AXX = mybir.AxisListType

NB = 8
T = 4096
NT = NB * T
C = 1024
NCT = 8
HID = 2048
NHT = 16
CAP = 4096
EPS = 1e-5
VAL_ITERS = 36
TIE_ITERS = 16
WPAD = 304
CH = 512
NCH = T // CH


class Rec:
    def __init__(self, name, sem, dsem):
        self.name = name
        self.sem = sem
        self.dsem = dsem
        self.q = []
        self.n = 0
        self.nd = 0
        self.last_out = set()
        self.recent = []
        self.waited = {}
        self.ev = []

    def wait(self, mark):
        if mark is None:
            return
        sem, count = mark
        if count <= 0:
            return
        key = id(sem)
        if self.waited.get(key, -1) >= count:
            return
        self.waited[key] = count
        self.ev.append(("w", id(sem), count, len(self.q)))
        self.q.append(lambda e: e.wait_ge(sem, count))

    def mark(self):
        return (self.sem, self.n)

    def dmark(self):
        return (self.dsem, self.nd)

    @staticmethod
    def _names(aps):
        out = set()
        for ap in aps:
            if ap is None or isinstance(ap, (int, float)):
                continue
            t = getattr(ap, "tensor", None)
            if t is not None:
                out.add(t.name)
        return out

    def _hz(self, ins, outs):
        if self.name not in ("vector", "scalar", "gpsimd"):
            return
        rd = self._names(ins)
        wr = self._names(outs)
        rws = set()
        rds = set()
        for (r, w) in self.recent:
            rws |= w
            rds |= r
        if (rd & rws) or (wr & rws) or (wr & rds):
            if self.name == "vector":
                self.q.append(lambda e: e.drain())
            else:
                sem, n = self.sem, self.n
                self.ev.append(("w", id(sem), n, len(self.q)))
                self.q.append(lambda e: e.wait_ge(sem, n))
            self.recent = []
        self.recent = (self.recent + [(rd, wr)])[-3:]

    def drain(self):
        self.q.append(lambda e: e.drain())
        self.recent = []

    def _inc(self, call):
        sem = self.sem
        self.ev.append(("i", id(sem), 1, len(self.q)))
        self.q.append(lambda e: call(e).then_inc(sem, 1))
        self.n += 1

    def ts(self, out, in0, s1, s2=None, op0=ALU.mult, op1=None, accum=None):
        self._hz([in0, s1, s2], [out, accum])
        kw = dict(out=out, in0=in0, scalar1=s1, scalar2=s2, op0=op0)
        if op1 is not None:
            kw["op1"] = op1
        if accum is not None:
            kw["accum_out"] = accum
        self._inc(lambda e: e.tensor_scalar(**kw))

    def tt(self, out, in0, in1, op):
        self._hz([in0, in1], [out])
        self._inc(lambda e: e.tensor_tensor(out=out, in0=in0, in1=in1, op=op))

    def stt(self, out, in0, s, in1, op0, op1, accum=None):
        self._hz([in0, in1, s], [out, accum])
        kw = dict(out=out, in0=in0, scalar=s, in1=in1, op0=op0, op1=op1)
        if accum is not None:
            kw["accum_out"] = accum
        self._inc(lambda e: e.scalar_tensor_tensor(**kw))

    def copy(self, out, in_):
        self._hz([in_], [out])
        self._inc(lambda e: e.tensor_copy(out=out, in_=in_))

    def memset(self, ap, val):
        self._hz([], [ap])
        self._inc(lambda e: e.memset(ap, val))

    def recip(self, out, in_):
        self._hz([in_], [out])
        self._inc(lambda e: e.reciprocal(out=out, in_=in_))

    def reduce(self, out, in_, op, axis=AXX.X):
        self._hz([in_], [out])
        self._inc(lambda e: e.tensor_reduce(out=out, in_=in_, axis=axis, op=op))

    def scan(self, out, d0, d1, initial):
        self._hz([d0, d1, initial], [out])
        self._inc(lambda e: e.tensor_tensor_scan(
            out=out, data0=d0, data1=d1, initial=initial,
            op0=ALU.mult, op1=ALU.add))

    def act(self, out, in_, func, bias=None, scale=None, accum=None):
        self._hz([in_, bias, scale], [out, accum])
        kw = dict(out=out, in_=in_, func=func)
        if bias is not None:
            kw["bias"] = bias
        if scale is not None:
            kw["scale"] = scale
        if accum is not None:
            kw["accum_out"] = accum
        self._inc(lambda e: e.activation(**kw))

    def nop(self):
        self.q.append(lambda e: e.nop())

    def gdiv(self, out, in0, in1):
        self._inc(lambda e: e.tensor_tensor(out=out, in0=in0, in1=in1,
                                            op=ALU.divide))

    def mm(self, out, lhsT, rhs, start, stop):
        ent = {"k": "mm", "out": out, "lhsT": lhsT, "rhs": rhs,
               "start": start, "stop": stop, "inc": False, "sem": self.sem}
        self.q.append(ent)

    def trans(self, out, in_, ident):
        ent = {"k": "tr", "out": out, "in_": in_, "ident": ident,
               "inc": False, "sem": self.sem}
        self.q.append(ent)

    def minc(self):
        assert isinstance(self.q[-1], dict)
        self.q[-1]["inc"] = True
        self.ev.append(("i", id(self.sem), 1, len(self.q) - 1))
        self.n += 1

    def dma(self, out, in_, **kw):
        dsem = self.dsem
        self.wait(self.dmark())
        self.ev.append(("i", id(dsem), 16, len(self.q)))
        self.q.append(lambda e: e.dma_start(out=out, in_=in_, **kw)
                      .then_inc(dsem, 16))
        self.nd += 16
        if self.name in ("vector", "scalar", "gpsimd"):
            self.recent = (self.recent + [(set(), self._names([out]))])[-3:]
        return self.dmark()

    def sg(self, out, in_, nf):
        self._hz([in_], [out, nf])
        self._inc(lambda e: e.sparse_gather(out=out, in_=in_, num_found=nf))

    def lib(self, lib):
        self.q.append(lambda e: e.load_library(lib))

    def pbcast(self, out, in_, channels):
        self._hz([in_], [out])
        self._inc(lambda e: e.partition_broadcast(out_ap=out, in_ap=in_,
                                                  channels=channels))

    def gather(self, out, tab, idxs, n, elem, transpose=False, count_src=None):
        dsem = self.dsem
        self.wait(self.dmark())
        self.ev.append(("i", id(dsem), 16, len(self.q)))

        def emit(e):
            nreg = n if count_src is None else e.value_load(
                count_src, min_val=0, max_val=n)
            e.dma_gather(
                out_ap=out, in_ap=tab, idxs_ap=idxs, num_idxs=n,
                num_idxs_reg=nreg, elem_size=elem, transpose=transpose,
                single_packet=False).then_inc(dsem, 16)
        self.q.append(emit)
        self.nd += 16
        return self.dmark()

    def scatter_add(self, out, in_, idxs, n, elem, count_src=None):
        dsem = self.dsem
        self.wait(self.dmark())
        self.ev.append(("i", id(dsem), 16, len(self.q)))

        def emit(e):
            nreg = n if count_src is None else e.value_load(
                count_src, min_val=0, max_val=n)
            e.dma_scatter_add(
                out_ap=out, in_ap=in_, idxs_ap=idxs, num_idxs=n,
                num_idxs_reg=nreg, elem_size=elem,
                single_packet=False).then_inc(dsem, 16)
        self.q.append(emit)
        self.nd += 16
        return self.dmark()

    def imerge(self, out_ap, idx_ap, in_ap):
        dsem = self.dsem
        self.wait(self.dmark())
        self.ev.append(("i", id(dsem), 16, len(self.q)))
        self.q.append(lambda e: e.indirect_dma_start(
            out=out_ap, out_offset=IndirectOffsetOnAxis(ap=idx_ap, axis=1),
            in_=in_ap, in_offset=None,
            compute_op=ALU.add).then_inc(dsem, 16))
        self.nd += 16
        return self.dmark()

    def iota(self, out, pattern, base, cm):
        self._hz([], [out])
        self._inc(lambda e: e.iota(out=out, pattern=pattern, base=base,
                                   channel_multiplier=cm,
                                   allow_small_or_imprecise_dtypes=True))

    def cc(self, kind, op, ins, outs, ccsem, state):
        state[0] += 1
        count = state[0]
        self.ev.append(("i", id(ccsem), 1, len(self.q)))
        self.q.append(lambda e: e.collective_compute(
            kind, op, replica_groups=[list(range(NB))], ins=ins, outs=outs)
            .then_inc(ccsem, 1))
        return (ccsem, count)

    def replay(self, raw):
        for fn in self.q:
            if isinstance(fn, dict):
                if fn["k"] == "mm":
                    r = raw.matmul(out=fn["out"], lhsT=fn["lhsT"],
                                   rhs=fn["rhs"], start=fn["start"],
                                   stop=fn["stop"])
                else:
                    r = raw.transpose(out=fn["out"], in_=fn["in_"],
                                      identity=fn["ident"])
                if fn["inc"]:
                    r.then_inc(fn["sem"], 1)
            else:
                fn(raw)


def build_program():
    nc = bacc.Bacc("TRN2", target_bir_lowering=False)

    def ein(name, shape, dt=F32):
        return nc.dram_tensor(name, shape, dt, kind="ExternalInput")

    x_in = ein("x", [T, C])
    wkT_in = ein("wkT", [C, C]); wvT_in = ein("wvT", [C, C])
    wrT_in = ein("wrT", [C, C]); woT_in = ein("woT", [C, C])
    w1_in = ein("w1", [C, HID]); w3_in = ein("w3", [C, HID])
    w2_in = ein("w2", [HID, C])
    wg_in = ein("wg", [C, NB])
    chvec_in = ein("chvec", [128, NCT, 5])
    rkv_in = ein("rkv", [1, 16])
    ohcol_in = ein("ohcol", [NB, 1])
    ident_in = ein("identbf", [128, 128], BF16)
    g16_in = ein("g16", [128, NB])
    h16_in = ein("h16", [128, 16], BF16)
    tri_in = ein("triexcl", [NB, NB])
    onesr_in = ein("onesrow", [1, 128])
    ones128_in = ein("ones128", [128, 1])
    tidb_in = ein("tidbase", [128, 1])
    selu_in = ein("selu16", [NB, 128])
    ones2d_in = ein("ones2d", [128, 128])
    identf8_in = ein("identf8", [NB, NB])
    out_ext = nc.dram_tensor("out", [T, C], F32, kind="ExternalOutput")

    def dram(name, shape, dt):
        return nc.dram_tensor(name, shape, dt)

    g_mine = dram("g_mine", [T, C], BF16)
    g_all = dram("g_all", [NT, C], BF16)
    sc_mine = dram("sc_mine", [16, 256 * NB], F32)
    sc_all = dram("sc_all", [128, 256 * NB], F32)
    eo_mine = dram("eo_mine", [CAP, C], BF16)
    eo_all = dram("eo_all", [NB * CAP, C], BF16)
    x2_plane = dram("x2_plane", [T, C], F32)
    w_ids = dram("w_ids", [16, WPAD], I32)
    w_sc = dram("w_sc", [16, WPAD], F32)
    lin_sc = dram("lin_sc", [1, CAP], F32)

    top = ExitStack()

    def sb(name, shape, dt=F32, ctx=None):
        return (ctx or top).enter_context(nc.sbuf_tensor(name, shape, dt))

    def mkpsum(name, shape, dt=F32):
        return top.enter_context(nc.psum_tensor(name, shape, dt))

    def mksem(name):
        return top.enter_context(nc.semaphore(name))

    ident = sb("ident_s", [128, 128], BF16)
    chv = sb("chv", [128, NCT, 8])
    dvec = sb("dvec", [128, NCT, 2])
    wgt = sb("wgt", [128, NCT, NB], BF16)
    rkv = sb("rkv_s", [1, 16])
    ohcol = sb("ohcol_s", [NB, 1])
    g16 = sb("g16_s", [128, NB])
    h16 = sb("h16_s", [128, 16], BF16)
    tri = sb("tri_s", [NB, NB])
    onesr = sb("onesr_s", [1, 128])
    ones128 = sb("ones128_s", [128, 1])
    oh128 = sb("oh128", [128, NB])
    w128 = sb("w128", [128, 1])
    tidb = sb("tidb_s", [128, 1])
    selu16 = sb("selu16_s", [NB, 128])
    identf8 = sb("identf8_s", [NB, NB])
    ones2d = sb("ones2d_s", [128, 128])
    bnds = sb("bnds", [128, 48])
    scr8b = sb("scr8b", [128, 8])
    iota256 = sb("iota256", [128, 256])
    iota48w = sb("iota48w", [16, 48])
    iotap16 = sb("iotap16", [16, 1])
    iotaloc = sb("iotaloc", [16, 256])
    idxcol = sb("idxcol", [128, 1], U32)
    zero16 = sb("zero16", [16, WPAD])
    cid = sb("cid", [128, 256])
    csc = sb("csc", [128, 256])
    tid = sb("tid", [128, 256])
    bnd = sb("bnd", [1, 64])
    cnt = sb("cnt", [128, NB])
    cstat = sb("cstat", [NB, NB])
    starts = sb("starts", [NB, NB])
    sm8 = sb("sm8", [NB, 1])
    scr8 = sb("scr8", [NB, NB])
    myrow = sb("myrow", [1, 2 * NB])
    owcnt_f = sb("owcnt_f", [1, 2 * NB])
    owcnt = sb("owcnt", [1, 2 * NB], I32)
    lm16 = sb("lm16", [16, 256, NB])
    sg_in = sb("sg_in", [16, 256])
    sg_in2 = sb("sg_in2", [16, 256])
    sg_out = sb("sg_out", [16, 48], I32)
    sg_outf = sb("sg_outf", [16, 48])
    sg_nf = sb("sg_nf", [1, 1], U32)
    nfb = sb("nfb", [1, 1])
    m48 = sb("m48", [16, 48])
    m48b = sb("m48b", [16, 48])
    m48c = sb("m48c", [16, 48])
    sg_ids = sb("sg_ids", [16, 48])
    sg_scv = sb("sg_scv", [16, 48])
    mcol = sb("mcol", [16, 4])
    sg_inB = sb("sg_inB", [16, 256])
    sg_in2B = sb("sg_in2B", [16, 256])
    sg_outB = sb("sg_outB", [16, 48], I32)
    sg_outfB = sb("sg_outfB", [16, 48])
    sg_nfB = sb("sg_nfB", [1, 1], U32)
    nfbB = sb("nfbB", [1, 1])
    sgm48B = sb("sgm48B", [16, 48])
    sg_idsB = sb("sg_idsB", [16, 48])
    sg_scvB = sb("sg_scvB", [16, 48])
    mcolB = sb("mcolB", [16, 4])
    idxcolB = sb("idxcolB", [128, 1], U32)
    gl32 = sb("gl32", [16, 256], I32)
    glf = sb("glf", [16, 256])
    gl_idx = sb("gl_idx", [128, 256], I16)
    ow_g = sb("ow_g", [128, NB, 48], I16)
    ow_s = sb("ow_s", [128, NB, 48], I16)
    lsl = sb("lsl", [16, 256])
    stats = sb("stats", [128, 16])
    sq_scr = sb("sq_scr", [128, 1024])
    sq_scr2 = sb("sq_scr2", [128, 1024])
    gb = sb("gb", [128, 2, 1024], BF16)
    gcm = sb("gcm", [128, 2, NCT, 128], BF16)
    scem = sb("scem", [NB, 128])
    sct = sb("sct", [128, NB])
    p_ex = mkpsum("p_ex", [128, 32])
    p_tr = mkpsum("p_tr", [128, 512], BF16)
    p_tr2 = mkpsum("p_tr2", [128, 512], BF16)
    p_mm0 = mkpsum("p_mm0", [128, 512])
    p_mm1 = mkpsum("p_mm1", [128, 512])
    p_mm2 = mkpsum("p_mm2", [128, 512])
    p_mm3 = mkpsum("p_mm3", [128, 512])
    p_sc = mkpsum("p_sc", [128, 128])

    s_q = mksem("s_q"); s_qd = mksem("s_qd")
    s_v = mksem("s_v"); s_vd = mksem("s_vd")
    s_s = mksem("s_s"); s_sd = mksem("s_sd")
    s_t = mksem("s_t"); s_td = mksem("s_td")
    s_g = mksem("s_g"); s_gd = mksem("s_gd")
    s_cc = mksem("s_cc")

    Q = Rec("sync", s_q, s_qd)
    V = Rec("vector", s_v, s_vd)
    S = Rec("scalar", s_s, s_sd)
    TE = Rec("tensor", s_t, s_td)
    G = Rec("gpsimd", s_g, s_gd)
    ccs = [0]

    Q.dma(x2_plane.ap(), x_in.ap())
    for dst, src in [(ident[:, :], ident_in.ap()),
                     (chv[:, :, 0:5], chvec_in.ap()),
                     (rkv[:, :], rkv_in.ap()),
                     (ohcol[:, :], ohcol_in.ap()),
                     (g16[:, :], g16_in.ap()),
                     (h16[:, :], h16_in.ap()),
                     (tri[:, :], tri_in.ap()),
                     (onesr[:, :], onesr_in.ap()),
                     (ones128[:, :], ones128_in.ap()),
                     (tidb[:, :], tidb_in.ap()),
                     (selu16[:, :], selu_in.ap()),
                     (identf8[:, :], identf8_in.ap()),
                     (ones2d[:, :], ones2d_in.ap())]:
        G.dma(dst, src)
    G.dma(wgt[:, :, :], wg_in.ap().rearrange("(ct p) e -> p ct e", p=128))
    m_consts = G.dmark()
    G.iota(iota256[:, :], [[16, 256]], 0, 0)
    G.iota(iota48w[:, :], [[16, 48]], 0, 1)
    G.iota(iotap16[:, :], [[0, 1]], 0, 1)
    G.iota(iotaloc[:, :], [[16, 256]], 0, 1)

    V.wait(m_consts)
    V.memset(zero16[:, :], 0.0)
    V.ts(chv[:, :, 5:8], chv[:, :, 2:5], -1.0, 1.0, ALU.mult, ALU.add)

    S.wait(m_consts)
    S.act(stats[:, 0:8], chv[:, :, 0:1].rearrange("p a b -> p (a b)"), ACT.Exp)
    S.nop()
    S.act(dvec[:, :, 0:1].rearrange("p a b -> p (a b)"), stats[:, 0:8],
          ACT.Exp, scale=-1.0)
    S.act(dvec[:, :, 1:2].rearrange("p a b -> p (a b)"),
          chv[:, :, 1:2].rearrange("p a b -> p (a b)"), ACT.Exp)
    m_dvec = S.mark()

    TE.wait(m_consts)
    TE.mm(p_ex[:, 0:8], onesr[:, :], rkv[0:1, 0:8], True, True)
    TE.minc()
    S.wait(TE.mark())
    S.act(oh128[:, :], p_ex[:, 0:8], ACT.Copy)
    m_oh128 = S.mark()
    V.wait(m_oh128)
    V.tt(cnt[:, :], g16[:, :], oh128[:, :], ALU.mult)
    V.reduce(w128[:, :], cnt[:, :], ALU.add)
    m_w128 = V.mark()

    lg_prev = {"tr": None, "gstore": None, "mm": None, "sct_q": None}

    def lngate(i, xin_ap, xwait=None):
        if xwait is not None:
            V.wait(xwait)
            S.wait(xwait)
        V.ts(sq_scr[:, :], xin_ap, 1.0, None, ALU.mult, ALU.add,
             accum=stats[:, 0:1])
        S.act(sq_scr2[:, :], xin_ap, ACT.Square, accum=stats[:, 1:2])
        m_sq = S.mark()
        V.wait(m_sq)
        V.drain()
        V.ts(stats[:, 2:3], stats[:, 0:1], -1.0 / C)
        V.tt(stats[:, 4:5], stats[:, 2:3], stats[:, 2:3], ALU.mult)
        V.ts(stats[:, 6:7], stats[:, 1:2], 1.0 / C, EPS, ALU.mult, ALU.add)
        V.tt(stats[:, 3:4], stats[:, 6:7], stats[:, 4:5], ALU.subtract)
        V.recip(stats[:, 7:8], stats[:, 3:4])
        m_rc = V.mark()
        S.wait(m_rc)
        S.act(stats[:, 8:9], stats[:, 7:8], ACT.Sqrt)
        S.nop()
        S.act(stats[:, 9:10], stats[:, 2:3], ACT.Identity, scale=stats[:, 8:9])
        m_bias = S.mark()
        V.wait(m_bias)
        V.wait(lg_prev["tr"])
        V.wait(lg_prev["gstore"])
        par = i % 2
        V.ts(gb[:, par, :], xin_ap, stats[:, 8:9], stats[:, 9:10],
             ALU.mult, ALU.add)
        m_g = V.mark()
        Q.wait(m_g)
        gst = Q.dma(g_mine.ap()[i * 128:(i + 1) * 128, :], gb[:, par, :])
        TE.wait(m_g)
        TE.wait(lg_prev["mm"])
        for h in range(2):
            pt = p_tr if h == 0 else p_tr2
            for j in range(4):
                ct = h * 4 + j
                TE.trans(pt[:, j * 128:(j + 1) * 128],
                         gb[:, par, ct * 128:(ct + 1) * 128], ident[:, :])
        TE.minc()
        m_tr = TE.mark()
        S.wait(m_tr)
        S.act(gcm[:, par, 0:4, :].rearrange("p a b -> p (a b)"), p_tr[:, :],
              ACT.Copy)
        S.act(gcm[:, par, 4:8, :].rearrange("p a b -> p (a b)"), p_tr2[:, :],
              ACT.Copy)
        m_gcm = S.mark()
        TE.wait(m_gcm)
        TE.wait(lg_prev["sct_q"])
        for ct in range(NCT):
            TE.mm(p_sc[0:NB, :], wgt[:, ct, :], gcm[:, par, ct, :],
                  ct == 0, ct == NCT - 1)
        TE.minc()
        m_mm = TE.mark()
        S.wait(m_mm)
        S.act(scem[:, :], p_sc[0:NB, :], ACT.Copy)
        m_scem = S.mark()
        TE.wait(m_scem)
        TE.trans(p_ex[:, 8:16], scem[:, :], identf8[:, :])
        TE.minc()
        m_sct = TE.mark()
        S.wait(m_sct)
        S.act(sct[:, :], p_ex[:, 8:16], ACT.Exp)
        m_exp = S.mark()
        V.wait(m_exp)
        V.reduce(stats[:, 10:11], sct[:, :], ALU.add)
        V.recip(stats[:, 11:12], stats[:, 10:11])
        V.ts(sct[:, :], sct[:, :], stats[:, 11:12])
        m_sm = V.mark()
        Q.wait(m_sm)
        scq = Q.dma(sc_mine.ap().rearrange("q (f e) -> q f e", e=NB)
                    [:, 8 * i:8 * i + 8, :].rearrange("q w e -> w q e"),
                    sct[:, :])
        lg_prev["tr"] = m_tr
        lg_prev["gstore"] = gst
        lg_prev["mm"] = m_mm
        lg_prev["sct_q"] = scq

    p1ctx = ExitStack()
    xt2 = sb("xt2", [128, 2, 1024], ctx=p1ctx)
    ldm = [None] * 34
    ldm[0] = Q.dma(xt2[:, 0, :], x_in.ap()[0:128, :])
    ldm[1] = Q.dma(xt2[:, 1, :], x_in.ap()[128:256, :])
    for i in range(32):
        lngate(i, xt2[:, i % 2, :], xwait=ldm[i])
        if i + 2 < 32:
            Q.wait(V.mark())
            ldm[i + 2] = Q.dma(xt2[:, i % 2, :],
                               x_in.ap()[(i + 2) * 128:(i + 3) * 128, :])
    p1ctx.close()
    m_lg1_q = Q.dmark()

    G.wait(m_lg1_q)
    m_ag_sc1 = G.cc("AllGather", ALU.bypass, [sc_mine.ap().opt()],
                    [sc_all.ap().opt()], s_cc, ccs)
    m_ag_g1 = G.cc("AllGather", ALU.bypass, [g_mine.ap().opt()],
                   [g_all.ap().opt()], s_cc, ccs)

    def emit_selection(m_ag_sc, selset, m_wfree=()):
        selsc, selmsk, seltie, selmb, selscr, selscr8 = selset
        Q.wait(m_ag_sc)
        Q.dma(selsc[:, :, :],
              sc_all.ap().rearrange("p (f e) -> p f e", e=NB))
        m_sel_ld = Q.dmark()
        for mk in m_wfree:
            G.wait(mk)
        G.dma(w_ids.ap(), zero16[:, :])
        G.dma(w_sc.ap(), zero16[:, :])

        V.wait(m_sel_ld)
        V.ts(tid[:, :], iota256[:, :], tidb[:, 0:1], None, ALU.add)
        V.memset(bnds[:, 0:8], 0.0)
        V.memset(bnds[:, 8:16], 1.0)

        for it in range(VAL_ITERS):
            V.tt(bnds[:, 16:24], bnds[:, 0:8], bnds[:, 8:16], ALU.add)
            V.ts(bnds[:, 16:24], bnds[:, 16:24], 0.5)
            for e in range(NB):
                V.ts(selscr[:, :], selsc[:, :, e], bnds[:, 16 + e:17 + e],
                     None, ALU.is_gt, ALU.add, accum=cnt[:, e:e + 1])
            TE.wait(V.mark())
            TE.mm(p_ex[:, 16:24], ones2d[:, :], cnt[:, :], True, True)
            TE.minc()
            V.wait(TE.mark())
            V.ts(bnds[:, 24:32], p_ex[:, 16:24], float(CAP), None, ALU.is_ge)
            V.tt(bnds[:, 32:40], bnds[:, 16:24], bnds[:, 0:8], ALU.subtract)
            V.tt(bnds[:, 40:48], bnds[:, 24:32], bnds[:, 32:40], ALU.mult)
            V.tt(bnds[:, 0:8], bnds[:, 0:8], bnds[:, 40:48], ALU.add)
            V.tt(bnds[:, 32:40], bnds[:, 8:16], bnds[:, 16:24], ALU.subtract)
            V.tt(bnds[:, 40:48], bnds[:, 24:32], bnds[:, 32:40], ALU.mult)
            V.tt(bnds[:, 8:16], bnds[:, 16:24], bnds[:, 40:48], ALU.add)

        for e in range(NB):
            V.ts(selmsk[:, :, e], selsc[:, :, e], bnds[:, 8 + e:9 + e],
                 None, ALU.is_gt, ALU.add, accum=cnt[:, e:e + 1])
        for e in range(NB):
            V.ts(seltie[:, :, e], selsc[:, :, e], bnds[:, e:e + 1],
                 None, ALU.is_gt)
        for e in range(NB):
            V.tt(seltie[:, :, e], seltie[:, :, e], selmsk[:, :, e],
                 ALU.subtract)
        TE.wait(V.mark())
        TE.mm(p_ex[:, 16:24], ones2d[:, :], cnt[:, :], True, True)
        TE.minc()
        V.wait(TE.mark())
        V.ts(bnds[:, 40:48], p_ex[:, 16:24], -1.0, float(CAP),
             ALU.mult, ALU.add)
        V.copy(selscr8[:, 0:1, 0], bnds[:, 40:48].rearrange(
            "p (a e) -> p a e", a=1)[:, 0, 0:1])
        V.copy(bnd[:, 48:56], bnds[0:1, 40:48])
        V.memset(bnds[:, 0:8], 0.0)
        V.memset(bnds[:, 8:16], float(NT))

        for it in range(TIE_ITERS):
            V.tt(bnds[:, 16:24], bnds[:, 0:8], bnds[:, 8:16], ALU.add)
            V.ts(bnds[:, 16:24], bnds[:, 16:24], 0.5)
            for e in range(NB):
                V.stt(selscr[:, :], tid[:, :], bnds[:, 16 + e:17 + e],
                      seltie[:, :, e], ALU.is_lt, ALU.mult,
                      accum=cnt[:, e:e + 1])
            TE.wait(V.mark())
            TE.mm(p_ex[:, 16:24], ones2d[:, :], cnt[:, :], True, True)
            TE.minc()
            V.wait(TE.mark())
            V.tt(bnds[:, 24:32], p_ex[:, 16:24], bnds[:, 40:48], ALU.is_ge)
            V.tt(bnds[:, 32:40], bnds[:, 16:24], bnds[:, 8:16], ALU.subtract)
            V.tt(scr8b[:, 0:8], bnds[:, 24:32], bnds[:, 32:40], ALU.mult)
            V.tt(bnds[:, 8:16], bnds[:, 8:16], scr8b[:, 0:8], ALU.add)
            V.tt(bnds[:, 32:40], bnds[:, 0:8], bnds[:, 16:24], ALU.subtract)
            V.tt(scr8b[:, 0:8], bnds[:, 24:32], bnds[:, 32:40], ALU.mult)
            V.tt(bnds[:, 0:8], bnds[:, 16:24], scr8b[:, 0:8], ALU.add)

        for e in range(NB):
            V.stt(selscr[:, :], tid[:, :], bnds[:, 8 + e:9 + e],
                  seltie[:, :, e], ALU.is_lt, ALU.mult)
            V.tt(selmsk[:, :, e], selmsk[:, :, e], selscr[:, :], ALU.add)
        for e in range(NB):
            V.ts(selscr[:, :], selmsk[:, :, e], 1.0, None, ALU.mult, ALU.add,
                 accum=cnt[:, e:e + 1])
        V.wait(m_w128)
        V.tt(selmb[:, :, :], selmsk[:, :, :],
             w128[:, 0:1].rearrange("p (a b) -> p a b", a=1)
             .to_broadcast([128, 256, NB]), ALU.mult)
        m_fmask = V.mark()
        TE.wait(m_fmask)
        TE.mm(p_ex[0:8, 24:32], g16[:, :], cnt[:, :], True, True)
        TE.minc()
        S.wait(TE.mark())
        S.act(cstat[:, :], p_ex[0:8, 24:32], ACT.Copy)
        TE.wait(S.mark())
        TE.mm(p_ex[0:8, 16:24], tri[:, :], cstat[:, :], True, True)
        TE.minc()
        S.wait(TE.mark())
        S.act(starts[:, :], p_ex[0:8, 16:24], ACT.Copy)
        m_starts = S.mark()
        TE.wait(m_starts)
        TE.mm(p_ex[0:1, 16:24], ohcol[:, :], starts[:, :], True, True)
        TE.mm(p_ex[0:1, 24:32], ohcol[:, :], cstat[:, :], True, True)
        TE.minc()
        S.wait(TE.mark())
        S.act(myrow[:, 0:8], p_ex[0:1, 16:24], ACT.Copy)
        S.act(myrow[:, 8:16], p_ex[0:1, 24:32], ACT.Copy)
        m_myrow = S.mark()

        V.wait(m_starts)
        V.tt(scr8[:, :], starts[:, :], oh128[0:8, :], ALU.mult)
        V.reduce(sm8[:, :], scr8[:, :], ALU.add)
        ohb = oh128[:, 0:NB].rearrange("p (a e) -> p a e", a=1) \
            .to_broadcast([128, 256, NB])
        V.tt(selscr8[:, :, :], selmsk[:, :, :], ohb, ALU.mult)
        V.reduce(selscr[:, :], selscr8[:, :, :], ALU.add)
        V.tt(selscr8[:, :, :], selsc[:, :, :], ohb, ALU.mult)
        V.reduce(csc[:, :], selscr8[:, :, :], ALU.add)
        V.ts(cid[:, :], tid[:, :], 1.0, 2.0, ALU.mult, ALU.add)
        V.tt(cid[:, :], cid[:, :], selscr[:, :], ALU.mult)
        V.ts(cid[:, :], cid[:, :], 1.0, -1.0, ALU.mult, ALU.add)
        V.ts(csc[:, :], csc[:, :], 1.0, 1.0, ALU.mult, ALU.add)
        V.tt(csc[:, :], csc[:, :], selscr[:, :], ALU.mult)
        V.ts(csc[:, :], csc[:, :], 1.0, -1.0, ALU.mult, ALU.add)
        m_cands = V.mark()

        TE.wait(m_cands)
        for ncol in range(4):
            if ncol > 0:
                TE.wait(S.mark())
            TE.mm(p_mm0[0:16, :], h16[:, :],
                  selmb[:, :, :].rearrange("p f e -> p (f e)")
                  [:, ncol * 512:(ncol + 1) * 512], True, True)
            TE.minc()
            S.wait(TE.mark())
            S.act(lm16[:, :, :].rearrange("p f e -> p (f e)")
                  [:, ncol * 512:(ncol + 1) * 512], p_mm0[0:16, :], ACT.Copy)
        m_lm16 = S.mark()
        return dict(cands=m_cands, myrow=m_myrow, lm16=m_lm16)

    def emit_merge(msel):
        G.lib(LC.sparse_gather)
        G.wait(msel["cands"])
        INX = [sg_in, sg_inB]
        IN2X = [sg_in2, sg_in2B]
        OUTX = [sg_out, sg_outB]
        OUTFX = [sg_outf, sg_outfB]
        NFX = [sg_nf, sg_nfB]
        NFBX = [nfb, nfbB]
        M48X = [m48, sgm48B]
        IDSX = [sg_ids, sg_idsB]
        SCVX = [sg_scv, sg_scvB]
        MCX = [mcol, mcolB]
        IDXX = [idxcol, idxcolB]
        vm = {}
        img = {}

        def gstage(u):
            q = u & 1
            if u >= 2:
                G.wait(vm[u - 2])
            G.dma(INX[q][:, :], cid[16 * u:16 * u + 16, :])
            G.dma(IN2X[q][:, :], csc[16 * u:16 * u + 16, :])
            G.wait(G.dmark())
            G.sg(OUTX[q][:, :], INX[q][:, :], NFX[q][:, :])
            G.sg(OUTFX[q][:, :], IN2X[q][:, :], NFX[q][:, :])
            return G.mark()

        def vstage(u, m_sg):
            q = u & 1
            c0, c1 = 16 + 2 * q, 17 + 2 * q
            V.wait(m_sg)
            if u >= 2:
                V.wait(img[u - 2])
            V.copy(NFBX[q][:, :], NFX[q][:, :])
            TE.wait(V.mark())
            TE.mm(p_ex[0:16, c0:c0 + 1], onesr[0:1, 0:16],
                  NFBX[q][0:1, 0:1], True, True)
            TE.mm(p_ex[0:16, c1:c1 + 1], selu16[:, 16 * u:16 * u + 16],
                  sm8[:, :], True, True)
            TE.minc()
            V.wait(TE.mark())
            V.ts(M48X[q][:, :], iota48w[:, :], p_ex[0:16, c0:c0 + 1],
                 None, ALU.is_lt)
            V.tt(IDSX[q][:, :], OUTX[q][:, :], M48X[q][:, :], ALU.mult)
            V.ts(MCX[q][:, 0:1], p_ex[0:16, c1:c1 + 1], 1.0)
            for kpow in (2048.0, 1024.0, 512.0, 256.0, 128.0, 64.0, 32.0,
                         16.0):
                V.ts(MCX[q][:, 1:2], MCX[q][:, 0:1], kpow, None, ALU.is_ge)
                V.stt(MCX[q][:, 0:1], MCX[q][:, 1:2], -kpow, MCX[q][:, 0:1],
                      ALU.mult, ALU.add)
            V.tt(MCX[q][:, 1:2], p_ex[0:16, c1:c1 + 1], MCX[q][:, 0:1],
                 ALU.subtract)
            V.ts(MCX[q][:, 1:2], MCX[q][:, 1:2], 1.0 / 16)
            V.tt(MCX[q][:, 2:3], MCX[q][:, 0:1], iotap16[:, :], ALU.add)
            V.ts(MCX[q][:, 3:4], MCX[q][:, 2:3], 16.0, None, ALU.is_ge)
            V.stt(MCX[q][:, 2:3], MCX[q][:, 3:4], -16.0, MCX[q][:, 2:3],
                  ALU.mult, ALU.add)
            V.tt(MCX[q][:, 3:4], MCX[q][:, 1:2], MCX[q][:, 3:4], ALU.add)
            V.stt(MCX[q][:, 1:2], MCX[q][:, 2:3], float(WPAD),
                  MCX[q][:, 3:4], ALU.mult, ALU.add)
            V.copy(IDXX[q][0:16, 0:1], MCX[q][:, 1:2])
            V.tt(SCVX[q][:, :], OUTFX[q][:, :], M48X[q][:, :], ALU.mult)
            return V.mark()

        def istage(u):
            q = u & 1
            G.wait(vm[u])
            G.imerge(w_ids.ap().rearrange("(a r) c -> a (r c)", a=1),
                     IDXX[q][0:16, 0:1], IDSX[q][:, :])
            G.imerge(w_sc.ap().rearrange("(a r) c -> a (r c)", a=1),
                     IDXX[q][0:16, 0:1], SCVX[q][:, :])
            img[u] = G.dmark()

        vm[0] = vstage(0, gstage(0))
        for u in range(1, NB):
            m_sg = gstage(u)
            istage(u - 1)
            vm[u] = vstage(u, m_sg)
        istage(NB - 1)
        return img[NB - 1]

    def emit_owner_lists(msel, m_merge):
        V.wait(msel["lm16"])
        V.wait(msel["myrow"])
        V.wait(m_merge)
        for e in range(NB):
            V.ts(lsl[:, :], iotaloc[:, :], 1.0, 2.0, ALU.mult, ALU.add)
            V.tt(lsl[:, :], lsl[:, :], lm16[:, :, e], ALU.mult)
            V.ts(lsl[:, :], lsl[:, :], 1.0, -1.0, ALU.mult, ALU.add)
            m_c = V.mark()
            G.wait(m_c)
            G.sg(sg_out[:, :], lsl[:, :], sg_nf[:, :])
            m_sg = G.mark()
            V.wait(m_sg)
            V.copy(nfb[:, :], sg_nf[:, :])
            TE.wait(V.mark())
            TE.mm(p_ex[0:16, 16:17], onesr[0:1, 0:16], nfb[0:1, 0:1],
                  True, True)
            TE.mm(p_ex[0:16, 17:18], onesr[0:1, 0:16], myrow[0:1, e:e + 1],
                  True, True)
            TE.minc()
            V.wait(TE.mark())
            V.ts(m48[:, :], iota48w[:, :], p_ex[0:16, 16:17], None, ALU.is_lt)
            V.tt(sg_scv[:, :], sg_out[:, :], m48[:, :], ALU.mult)
            V.ts(sg_scv[:, :], sg_scv[:, :], 1.0, -1.0, ALU.mult, ALU.add)
            V.copy(ow_s[0:16, e, :], sg_scv[:, :])
            V.ts(sg_scv[:, :], iota48w[:, :], p_ex[0:16, 17:18], None, ALU.add)
            V.ts(m48b[:, :], sg_scv[:, :], 2048.0, None, ALU.is_ge)
            V.stt(sg_scv[:, :], m48b[:, :], -2048.0, sg_scv[:, :],
                  ALU.mult, ALU.add)
            V.ts(m48c[:, :], sg_scv[:, :], 1024.0, None, ALU.is_ge)
            V.stt(sg_scv[:, :], m48c[:, :], -1024.0, sg_scv[:, :],
                  ALU.mult, ALU.add)
            V.stt(m48c[:, :], m48b[:, :], 2.0, m48c[:, :], ALU.mult, ALU.add)
            V.stt(sg_scv[:, :], m48c[:, :], 8192.0, sg_scv[:, :],
                  ALU.mult, ALU.add)
            V.ts(sg_scv[:, :], sg_scv[:, :], 1.0, float(e * 1024) + 1.0,
                 ALU.mult, ALU.add)
            V.tt(sg_scv[:, :], sg_scv[:, :], m48[:, :], ALU.mult)
            V.ts(sg_scv[:, :], sg_scv[:, :], 1.0, -1.0, ALU.mult, ALU.add)
            V.copy(ow_g[0:16, e, :], sg_scv[:, :])
        ohf = owcnt_f.rearrange("a (e h) -> a e h", h=2)
        V.ts(ohf[:, :, 0], myrow[0:1, 8:16], 384.0, None, ALU.min)
        V.tt(ohf[:, :, 1], myrow[0:1, 8:16], ohf[:, :, 0], ALU.subtract)
        V.copy(owcnt[:, :], owcnt_f[:, :])
        m_ow0 = V.mark()
        Q.wait(m_ow0)
        Q.dma(ow_s[16:32, :, :], ow_s[0:16, :, :])
        Q.dma(ow_g[16:32, :, :], ow_g[0:16, :, :])
        Q.wait(Q.dmark())
        Q.dma(ow_s[32:64, :, :], ow_s[0:32, :, :])
        Q.dma(ow_g[32:64, :, :], ow_g[0:32, :, :])
        Q.wait(Q.dmark())
        Q.dma(ow_s[64:128, :, :], ow_s[0:64, :, :])
        Q.dma(ow_g[64:128, :, :], ow_g[0:64, :, :])
        return Q.dmark()

    def emit_gl(m_merge, scrow, scband):
        Q.wait(m_merge)
        Q.dma(gl32[:, :], w_ids.ap()[:, 0:256])
        V.wait(Q.dmark())
        V.ts(glf[:, :], gl32[:, :], 1.0, -1.0, ALU.mult, ALU.add)
        V.copy(gl_idx[0:16, :], glf[:, :])
        m_gl0 = V.mark()
        Q.wait(m_gl0)
        Q.dma(gl_idx[16:32, :], gl_idx[0:16, :])
        Q.wait(Q.dmark())
        Q.dma(gl_idx[32:64, :], gl_idx[0:32, :])
        Q.wait(Q.dmark())
        Q.dma(gl_idx[64:128, :], gl_idx[0:64, :])
        m_gl = Q.dmark()
        for rr in range(16):
            Q.dma(lin_sc.ap().rearrange("a (c r) -> (a c) r", r=16)[:, rr:rr + 1],
                  w_sc.ap()[rr:rr + 1, 0:256].rearrange("r c -> c r"),
                  allow_slow_non_contiguous=True)
        m_lin = Q.dmark()
        ret_lin = m_lin
        G.lib(LC.mlp)
        G.wait(m_lin)
        m_scrow = G.dma(scrow[:, :], lin_sc.ap())
        G.wait(m_scrow)
        G.pbcast(scband[:, :], scrow[:, :], 128)
        return dict(gl=m_gl, scband=G.mark(), lin=ret_lin)

    def emit_ffn(mrg, m_ag_g, fset, m_eo_free):
        fw1, fw3, fw2db, rt, hbuf, sil, outt, scrow, scband = fset
        mw = G.dmark()
        G.dma(fw1[:, :, :, :],
              w1_in.ap().rearrange("(ct p) (ht j) -> p ct ht j", p=128, j=128))
        G.dma(fw3[:, :, :, :],
              w3_in.ap().rearrange("(ct p) (ht j) -> p ct ht j", p=128, j=128))
        m_w13 = G.dmark()
        G.wait(mrg["gl"])
        G.wait(m_ag_g)
        G.dma(fw2db[:, :, :], w2_in.ap().rearrange("(ht p) d -> p ht d", p=128))
        m_w2 = G.dmark()
        gmk = [None] * 4
        rt_free = None
        m_eo_prev = None
        ag_marks = []
        for k in range(4):
            G.wait(rt_free)
            gmk[k] = G.gather(rt[:, :, :],
                              g_all.ap(), gl_idx[:, k * 64:(k + 1) * 64],
                              1024, C, transpose=True)
            V.wait(gmk[k])
            V.wait(mrg["scband"])
            V.tt(rt[:, :, :], rt[:, :, :],
                 scband[:, k * 1024:(k + 1) * 1024]
                 .rearrange("p (a b) -> p a b", a=1)
                 .to_broadcast([128, NCT, 1024]), ALU.mult)
            m_rt = V.mark()

            for n in range(2):
                kk = 2 * k + n
                TE.wait(m_rt)
                TE.wait(S.mark())
                TE.wait(m_w13)
                vm = [None] * NHT
                for ht in range(NHT):
                    pmu = p_mm0 if ht % 2 == 0 else p_mm2
                    pmv = p_mm1 if ht % 2 == 0 else p_mm3
                    if ht >= 2:
                        TE.wait(vm[ht - 2])
                    for ct in range(NCT):
                        TE.mm(pmu[:, :], fw1[:, ct, ht, :],
                              rt[:, ct, n * 512:(n + 1) * 512],
                              ct == 0, ct == NCT - 1)
                    for ct in range(NCT):
                        TE.mm(pmv[:, :], fw3[:, ct, ht, :],
                              rt[:, ct, n * 512:(n + 1) * 512],
                              ct == 0, ct == NCT - 1)
                    TE.minc()
                    m_u = TE.mark()
                    S.wait(m_u)
                    S.act(sil[:, ht % 2, :], pmu[:, :], ACT.Silu)
                    m_sil = S.mark()
                    V.wait(m_sil)
                    V.stt(hbuf[:, ht, :], pmv[:, :], 0.5, sil[:, ht % 2, :],
                          ALU.mult, ALU.mult)
                    vm[ht] = V.mark()
                m_h = V.mark()
                TE.wait(m_h)
                TE.wait(m_w2)
                sm4 = [None] * 4
                for st4 in range(4):
                    pma = p_mm2 if st4 % 2 == 0 else p_mm0
                    pmb = p_mm3 if st4 % 2 == 0 else p_mm1
                    if st4 >= 2:
                        TE.wait(sm4[st4 - 2])
                    if st4 == 0:
                        S.wait(m_eo_prev)
                    for dn in range(2):
                        pm = pma if dn == 0 else pmb
                        for ht in range(NHT):
                            TE.mm(pm[:, :],
                                  hbuf[:, ht, st4 * 128:(st4 + 1) * 128],
                                  fw2db[:, ht, dn * 512:(dn + 1) * 512],
                                  ht == 0, ht == NHT - 1)
                    TE.minc()
                    m_o = TE.mark()
                    S.wait(m_o)
                    S.act(outt[:, st4, 0:512], pma[:, :], ACT.Copy)
                    S.act(outt[:, st4, 512:1024], pmb[:, :], ACT.Copy)
                    sm4[st4] = S.mark()
                m_ev = S.mark()
                Q.wait(m_ev)
                if kk == 0:
                    Q.wait(m_eo_free)
                Q.dma(eo_mine.ap().rearrange("(a p) d -> p a d", p=128)
                      [:, kk * 4:(kk + 1) * 4, :], outt[:, :, :])
                m_eo_prev = Q.dmark()
                if kk % 2 == 1:
                    kc = kk // 2
                    G.wait(Q.dmark())
                    ag_marks.append(G.cc(
                        "AllGather", ALU.bypass,
                        [eo_mine.ap()[1024 * kc:1024 * (kc + 1), :].opt()],
                        [eo_all.ap()[8192 * kc:8192 * (kc + 1), :].opt()],
                        s_cc, ccs))
            rt_free = TE.mark()
        return ag_marks[-1]

    def emit_combine(m_ag_eo, m_owner, target_ap, m_target_ready, cset):
        cmb_b, cmb_f = cset
        G.wait(m_ag_eo)
        G.wait(m_owner)
        G.wait(m_target_ready)
        prev_cast = None
        for e in range(NB):
            for hhalf in range(2):
                G.wait(prev_cast)
                mg = G.gather(cmb_b[:, :, :], eo_all.ap(),
                              ow_g[:, e, hhalf * 24:(hhalf + 1) * 24],
                              384, C, transpose=False,
                              count_src=owcnt[0:1, 2 * e + hhalf:
                                              2 * e + hhalf + 1])
                V.wait(mg)
                V.copy(cmb_f[:, :, :].rearrange("p a b -> p (a b)"),
                       cmb_b[:, :, :].rearrange("p a b -> p (a b)"))
                m_cast = V.mark()
                prev_cast = m_cast
                G.wait(m_cast)
                G.scatter_add(target_ap, cmb_f[:, :, :],
                              ow_s[:, e, hhalf * 24:(hhalf + 1) * 24],
                              384, C,
                              count_src=owcnt[0:1, 2 * e + hhalf:
                                              2 * e + hhalf + 1])
        return G.dmark()

    sel_ctx = ExitStack()
    selset = (sb("selsc", [128, 256, NB], ctx=sel_ctx),
              sb("selmsk", [128, 256, NB], ctx=sel_ctx),
              sb("seltie", [128, 256, NB], ctx=sel_ctx),
              sb("selmb", [128, 256, NB], BF16, ctx=sel_ctx),
              sb("selscr", [128, 256], ctx=sel_ctx),
              sb("selscr8", [128, 256, NB], ctx=sel_ctx))
    msel1 = emit_selection(m_ag_sc1, selset)
    m_merge1 = emit_merge(msel1)
    m_owner1 = emit_owner_lists(msel1, m_merge1)
    sel_ctx.close()

    ffn_ctx = ExitStack()
    fset1 = (sb("fw1", [128, NCT, NHT, 128], BF16, ctx=ffn_ctx),
             sb("fw3", [128, NCT, NHT, 128], BF16, ctx=ffn_ctx),
             sb("fw2db", [128, NHT, 1024], BF16, ctx=ffn_ctx),
             sb("rt", [128, NCT, 1024], BF16, ctx=ffn_ctx),
             sb("hbuf", [128, NHT, 512], BF16, ctx=ffn_ctx),
             sb("sil", [128, 2, 512], BF16, ctx=ffn_ctx),
             sb("outt", [128, 4, 1024], BF16, ctx=ffn_ctx),
             sb("scrow", [1, CAP], BF16, ctx=ffn_ctx),
             sb("scband", [128, CAP], BF16, ctx=ffn_ctx))
    mgl1 = emit_gl(m_merge1, fset1[7], fset1[8])
    mrg1 = dict(gl=mgl1["gl"], scband=mgl1["scband"])
    m_ag_eo1 = emit_ffn(mrg1, m_ag_g1, fset1, None)
    ffn_ctx.close()
    cmb_ctx = ExitStack()
    cset1 = (sb("cmb_b", [128, 3, 1024], BF16, ctx=cmb_ctx),
             sb("cmb_f", [128, 3, 1024], ctx=cmb_ctx))
    m_cmb1 = emit_combine(m_ag_eo1, m_owner1, x2_plane.ap(), Q.dmark(), cset1)
    cmb_ctx.close()

    attn_ctx = ExitStack()
    wk4 = sb("wk4", [128, NCT, NCT, 128], BF16, ctx=attn_ctx)
    wv4 = sb("wv4", [128, NCT, NCT, 128], BF16, ctx=attn_ctx)
    wr4 = sb("wr4", [128, NCT, NCT, 128], BF16, ctx=attn_ctx)
    wo4 = sb("wo4", [128, NCT, 1024], BF16, ctx=attn_ctx)
    xt34 = sb("xt34", [128, 4, 1024], ctx=attn_ctx)
    g2cm = sb("g2cm", [128, NCT, CH + 1], BF16, ctx=attn_ctx)
    mixb = sb("mixb", [128, 3, CH], BF16, ctx=attn_ctx)
    xk = sb("xk", [128, NCT, CH], BF16, ctx=attn_ctx)
    xv = sb("xv", [128, NCT, CH], BF16, ctx=attn_ctx)
    xr = sb("xr", [128, NCT, CH], BF16, ctx=attn_ctx)
    kd = sb("kd", [128, CH], ctx=attn_ctx)
    vdd = sb("vdd", [128, 2, CH], ctx=attn_ctx)
    ad = sb("ad", [128, 2, CH], ctx=attn_ctx)
    avd = sb("avd", [128, CH], ctx=attn_ctx)
    srd = sb("srd", [128, 2, CH], BF16, ctx=attn_ctx)
    numb = sb("numb", [128, CH], ctx=attn_ctx)
    denb = sb("denb", [128, CH], ctx=attn_ctx)
    yb = sb("yb", [128, CH], ctx=attn_ctx)
    zb = sb("zb", [128, NCT, CH], BF16, ctx=attn_ctx)
    pbuf = sb("pbuf", [128, CH + 1], ctx=attn_ctx)
    qbuf = sb("qbuf", [128, CH + 1], ctx=attn_ctx)
    pcar = sb("pcar", [128, NCT], ctx=attn_ctx)
    qcar = sb("qcar", [128, NCT], ctx=attn_ctx)
    ddcur = sb("ddcur", [128, CH], ctx=attn_ctx)

    G.lib(LC.standard)
    G.wait(m_cmb1)
    G.wait(TE.mark())
    G.wait(S.mark())
    G.wait(V.mark())
    G.dma(wk4[:, :, :, :],
          wkT_in.ap().rearrange("(ct p) (dt j) -> p ct dt j", p=128, j=128))
    G.dma(wv4[:, :, :, :],
          wvT_in.ap().rearrange("(ct p) (dt j) -> p ct dt j", p=128, j=128))
    G.dma(wr4[:, :, :, :],
          wrT_in.ap().rearrange("(ct p) (dt j) -> p ct dt j", p=128, j=128))
    G.dma(wo4[:, :, :],
          woT_in.ap().rearrange("(ct p) d -> p ct d", p=128))
    m_aw = G.dmark()

    V.wait(m_dvec)
    V.wait(m_aw)
    V.memset(pcar[:, :], 0.0)
    V.memset(qcar[:, :], 0.0)
    V.memset(g2cm[:, :, 0:1].rearrange("p a b -> p (a b)"), 0.0)
    m_attn_init = V.mark()

    m_x2_ready = m_cmb1
    prev_xk_cons = None
    prev_xt_cons = None
    for c in range(NCH):
        Q.wait(m_x2_ready)
        Q.wait(prev_xt_cons)
        mx = Q.dma(xt34[:, :, :],
                   x2_plane.ap()[c * CH:(c + 1) * CH, :]
                   .rearrange("(a p) d -> p a d", p=128))
        if c > 0:
            V.copy(g2cm[:, :, 0:1].rearrange("p a b -> p (a b)"),
                   g2cm[:, :, CH:CH + 1].rearrange("p a b -> p (a b)"))
        for tt in range(4):
            V.wait(mx)
            S.wait(mx)
            V.wait(m_attn_init)
            V.ts(sq_scr[:, :], xt34[:, tt, :], 1.0, None, ALU.mult, ALU.add,
                 accum=stats[:, 0:1])
            S.act(sq_scr2[:, :], xt34[:, tt, :], ACT.Square,
                  accum=stats[:, 1:2])
            m_sq = S.mark()
            V.wait(m_sq)
            V.drain()
            V.ts(stats[:, 2:3], stats[:, 0:1], -1.0 / C)
            V.tt(stats[:, 4:5], stats[:, 2:3], stats[:, 2:3], ALU.mult)
            V.ts(stats[:, 6:7], stats[:, 1:2], 1.0 / C, EPS, ALU.mult,
                 ALU.add)
            V.tt(stats[:, 3:4], stats[:, 6:7], stats[:, 4:5], ALU.subtract)
            V.recip(stats[:, 7:8], stats[:, 3:4])
            m_rc = V.mark()
            S.wait(m_rc)
            S.act(stats[:, 8:9], stats[:, 7:8], ACT.Sqrt)
            S.nop()
            S.act(stats[:, 9:10], stats[:, 2:3], ACT.Identity,
                  scale=stats[:, 8:9])
            m_bias = S.mark()
            V.wait(m_bias)
            V.wait(lg_prev["tr"])
            V.ts(gb[:, tt % 2, :], xt34[:, tt, :], stats[:, 8:9],
                 stats[:, 9:10], ALU.mult, ALU.add)
            m_g2 = V.mark()
            TE.wait(m_g2)
            TE.wait(prev_xk_cons)
            for h in range(2):
                pt = p_tr if h == 0 else p_tr2
                for j in range(4):
                    ct = h * 4 + j
                    TE.trans(pt[:, j * 128:(j + 1) * 128],
                             gb[:, tt % 2, ct * 128:(ct + 1) * 128],
                             ident[:, :])
            TE.minc()
            m_tr = TE.mark()
            S.wait(m_tr)
            for h in range(2):
                pt = p_tr if h == 0 else p_tr2
                for j in range(4):
                    ct = h * 4 + j
                    S.act(g2cm[:, ct, 1 + tt * 128:1 + (tt + 1) * 128],
                          pt[:, j * 128:(j + 1) * 128], ACT.Copy)
            lg_prev["tr"] = m_tr
        m_g2cm = S.mark()

        V.wait(m_g2cm)
        S.wait(m_g2cm)
        for ct in range(NCT):
            for mi, t1col in enumerate([5, 6, 7]):
                S.act(mixb[:, mi, :], g2cm[:, ct, 0:CH], ACT.Identity,
                      scale=chv[:, ct, t1col:t1col + 1])
            m_mx = S.mark()
            V.wait(m_mx)
            V.stt(xk[:, ct, :], g2cm[:, ct, 1:CH + 1], chv[:, ct, 2:3],
                  mixb[:, 0, :], ALU.mult, ALU.add)
            V.stt(xv[:, ct, :], g2cm[:, ct, 1:CH + 1], chv[:, ct, 3:4],
                  mixb[:, 1, :], ALU.mult, ALU.add)
            V.stt(xr[:, ct, :], g2cm[:, ct, 1:CH + 1], chv[:, ct, 4:5],
                  mixb[:, 2, :], ALU.mult, ALU.add)
            S.wait(V.mark())
        m_mix = V.mark()

        m_z = None
        m_prev_evac = None
        m_z_hist = [None] * NCT
        for dt in range(NCT):
            par = dt % 2
            TE.wait(m_mix)
            TE.wait(m_aw)
            TE.wait(m_prev_evac)
            for ct in range(NCT):
                TE.mm(p_mm0[:, :], wk4[:, ct, dt, :], xk[:, ct, :],
                      ct == 0, ct == NCT - 1)
            for ct in range(NCT):
                TE.mm(p_mm1[:, :], wv4[:, ct, dt, :], xv[:, ct, :],
                      ct == 0, ct == NCT - 1)
            for ct in range(NCT):
                TE.mm(p_mm2[:, :], wr4[:, ct, dt, :], xr[:, ct, :],
                      ct == 0, ct == NCT - 1)
            TE.minc()
            m_kvr = TE.mark()
            S.wait(m_kvr)
            if dt >= 2:
                S.wait(m_z_hist[dt - 2])
            S.act(kd[:, :], p_mm0[:, :], ACT.Copy)
            S.act(vdd[:, par, :], p_mm1[:, :], ACT.Copy)
            S.act(srd[:, par, :], p_mm2[:, :], ACT.Sigmoid)
            S.nop()
            S.act(ad[:, par, :], kd[:, :], ACT.Exp)
            m_sc_dt = S.mark()
            m_prev_evac = m_sc_dt
            V.wait(m_sc_dt)
            V.ts(ddcur[:, :], dvec[:, dt, 0:1].to_broadcast([128, CH]), 1.0)
            V.tt(avd[:, :], ad[:, par, :], vdd[:, par, :], ALU.mult)
            V.copy(pbuf[:, 0:1], pcar[:, dt:dt + 1])
            V.copy(qbuf[:, 0:1], qcar[:, dt:dt + 1])
            V.scan(pbuf[:, 1:CH + 1], ddcur[:, :], avd[:, :],
                   pbuf[:, 0:1])
            V.scan(qbuf[:, 1:CH + 1], ddcur[:, :], ad[:, par, :],
                   qbuf[:, 0:1])
            V.copy(pcar[:, dt:dt + 1], pbuf[:, CH:CH + 1])
            V.copy(qcar[:, dt:dt + 1], qbuf[:, CH:CH + 1])
            V.stt(numb[:, :], avd[:, :], dvec[:, dt, 1:2],
                  pbuf[:, 0:CH], ALU.mult, ALU.add)
            V.stt(denb[:, :], ad[:, par, :], dvec[:, dt, 1:2],
                  qbuf[:, 0:CH], ALU.mult, ALU.add)
            V.recip(yb[:, :], denb[:, :])
            V.tt(yb[:, :], yb[:, :], numb[:, :], ALU.mult)
            V.tt(zb[:, dt, :], yb[:, :], srd[:, par, :], ALU.mult)
            m_z = V.mark()
            m_z_hist[dt] = m_z
        prev_xk_cons = m_z

        for tt in range(4):
            TE.wait(m_z)
            for dn in range(2):
                pm = p_mm2 if dn == 0 else p_mm3
                for ct in range(NCT):
                    TE.mm(pm[:, :], zb[:, ct, tt * 128:(tt + 1) * 128],
                          wo4[:, ct, dn * 512:(dn + 1) * 512],
                          ct == 0, ct == NCT - 1)
            TE.minc()
            m_wo = TE.mark()
            V.wait(m_wo)
            V.tt(xt34[:, tt, 0:512], xt34[:, tt, 0:512], p_mm2[:, :], ALU.add)
            V.tt(xt34[:, tt, 512:1024], xt34[:, tt, 512:1024], p_mm3[:, :],
                 ALU.add)
            m_x3 = V.mark()
            TE.wait(m_x3)
            Q.wait(m_x3)
            Q.dma(out_ext.ap()[(c * 4 + tt) * 128:(c * 4 + tt + 1) * 128, :],
                  xt34[:, tt, :])
            lngate(c * 4 + tt, xt34[:, tt, :], xwait=m_x3)
        prev_xt_cons = V.mark()
    m_attn_q = Q.dmark()
    m_attn_done = (V.mark(), TE.mark(), S.mark(), G.mark())
    attn_ctx.close()

    G.wait(m_attn_q)
    m_ag_sc2 = G.cc("AllGather", ALU.bypass, [sc_mine.ap().opt()],
                    [sc_all.ap().opt()], s_cc, ccs)
    m_ag_g2 = G.cc("AllGather", ALU.bypass, [g_mine.ap().opt()],
                   [g_all.ap().opt()], s_cc, ccs)

    sel_ctx2 = ExitStack()
    selset2 = (sb("selsc2", [128, 256, NB], ctx=sel_ctx2),
               sb("selmsk2", [128, 256, NB], ctx=sel_ctx2),
               sb("seltie2", [128, 256, NB], ctx=sel_ctx2),
               sb("selmb2", [128, 256, NB], BF16, ctx=sel_ctx2),
               sb("selscr2", [128, 256], ctx=sel_ctx2),
               sb("selscr82", [128, 256, NB], ctx=sel_ctx2))
    msel2 = emit_selection(m_ag_sc2, selset2,
                           m_wfree=(mgl1['gl'], mgl1['lin']))
    m_merge2 = emit_merge(msel2)
    m_owner2 = emit_owner_lists(msel2, m_merge2)
    sel_ctx2.close()

    ffn_ctx2 = ExitStack()
    fset2 = (sb("fw1b", [128, NCT, NHT, 128], BF16, ctx=ffn_ctx2),
             sb("fw3b", [128, NCT, NHT, 128], BF16, ctx=ffn_ctx2),
             sb("fw2db2", [128, NHT, 1024], BF16, ctx=ffn_ctx2),
             sb("rt2", [128, NCT, 1024], BF16, ctx=ffn_ctx2),
             sb("hbuf2", [128, NHT, 512], BF16, ctx=ffn_ctx2),
             sb("sil2", [128, 2, 512], BF16, ctx=ffn_ctx2),
             sb("outt2", [128, 4, 1024], BF16, ctx=ffn_ctx2),
             sb("scrow2", [1, CAP], BF16, ctx=ffn_ctx2),
             sb("scband2", [128, CAP], BF16, ctx=ffn_ctx2))
    mgl2 = emit_gl(m_merge2, fset2[7], fset2[8])
    mrg2 = dict(gl=mgl2["gl"], scband=mgl2["scband"])
    m_ag_eo2 = emit_ffn(mrg2, m_ag_g2, fset2, m_ag_eo1)
    ffn_ctx2.close()
    cmb_ctx2 = ExitStack()
    cset2 = (sb("cmb_b2", [128, 3, 1024], BF16, ctx=cmb_ctx2),
             sb("cmb_f2", [128, 3, 1024], ctx=cmb_ctx2))
    m_cmb2 = emit_combine(m_ag_eo2, m_owner2, out_ext.ap(), m_attn_q, cset2)
    cmb_ctx2.close()

    G.wait(m_cmb2)
    G.wait(Q.dmark())
    G.wait(V.mark())
    G.wait(S.mark())
    G.wait(TE.mark())

    globals()["_RECS"] = dict(Q=Q, V=V, S=S, TE=TE, G=G)
    with top, nc.Block() as block:
        @block.sync
        def _(e):
            Q.replay(e)

        @block.vector
        def _(e):
            V.replay(e)

        @block.scalar
        def _(e):
            S.replay(e)

        @block.tensor
        def _(e):
            TE.replay(e)

        @block.gpsimd
        def _(e):
            G.replay(e)

    return nc


_NC_CACHE = {}


def kernel(L0_attn_x, L0_attn_time_decay, L0_attn_time_first,
           L0_attn_time_mix_k, L0_attn_time_mix_v, L0_attn_time_mix_r,
           L0_attn_Wk, L0_attn_Wv, L0_attn_Wr, L0_attn_Wo,
           L0_ffn_wg, L0_ffn_w1, L0_ffn_w3, L0_ffn_w2):
    x = np.ascontiguousarray(np.asarray(L0_attn_x, dtype=np.float32))
    td = np.asarray(L0_attn_time_decay, np.float32).reshape(C)
    tf = np.asarray(L0_attn_time_first, np.float32).reshape(C)
    tmk = np.asarray(L0_attn_time_mix_k, np.float32).reshape(C)
    tmv = np.asarray(L0_attn_time_mix_v, np.float32).reshape(C)
    tmr = np.asarray(L0_attn_time_mix_r, np.float32).reshape(C)

    def chmaj(v):
        return v.reshape(NCT, 128).T

    chvec = np.ascontiguousarray(
        np.stack([chmaj(td), chmaj(tf), chmaj(tmk), chmaj(tmv), chmaj(tmr)],
                 axis=2).astype(np.float32))

    wkT = np.ascontiguousarray(np.asarray(L0_attn_Wk, np.float32).T)
    wvT = np.ascontiguousarray(np.asarray(L0_attn_Wv, np.float32).T)
    wrT = np.ascontiguousarray(np.asarray(L0_attn_Wr, np.float32).T)
    woT = np.ascontiguousarray(np.asarray(L0_attn_Wo, np.float32).T)
    wg = np.ascontiguousarray(np.asarray(L0_ffn_wg, np.float32))
    w1 = np.asarray(L0_ffn_w1, np.float32)
    w3 = np.asarray(L0_ffn_w3, np.float32)
    w2 = np.asarray(L0_ffn_w2, np.float32)

    identbf = np.eye(128, dtype=ml_dtypes.bfloat16)
    p = np.arange(128)
    g16c = (p[:, None] // 16 == np.arange(NB)[None, :]).astype(np.float32)
    h16c = (p[:, None] % 16 == np.arange(16)[None, :]).astype(ml_dtypes.bfloat16)
    tric = (np.arange(NB)[:, None] < np.arange(NB)[None, :]).astype(np.float32)
    onesrow = np.ones((1, 128), np.float32)
    ones128c = np.ones((128, 1), np.float32)
    tidbase = (4096.0 * (p // 16) + (p % 16)).astype(np.float32).reshape(128, 1)
    selu = np.zeros((NB, 128), np.float32)
    for u in range(NB):
        selu[u, 16 * u:16 * u + 16] = 1.0

    in_maps = []
    for r in range(NB):
        rkvc = np.zeros((1, 16), np.float32)
        rkvc[0, r] = 1.0
        rkvc[0, 8] = r
        rkvc[0, 9] = 4096.0 * r
        ohcolc = np.zeros((NB, 1), np.float32)
        ohcolc[r, 0] = 1.0
        in_maps.append({
            "x": np.ascontiguousarray(x[r]),
            "wkT": wkT, "wvT": wvT, "wrT": wrT, "woT": woT,
            "w1": np.ascontiguousarray(w1[r]),
            "w3": np.ascontiguousarray(w3[r]),
            "w2": np.ascontiguousarray(w2[r]),
            "wg": wg,
            "chvec": chvec,
            "rkv": rkvc,
            "ohcol": ohcolc,
            "identbf": identbf,
            "g16": g16c,
            "h16": h16c,
            "triexcl": tric,
            "onesrow": onesrow,
            "ones128": ones128c,
            "tidbase": tidbase,
            "selu16": selu,
            "identf8": np.eye(NB, dtype=np.float32),
            "ones2d": np.ones((128, 128), np.float32),
        })

    if "prog" not in _NC_CACHE:
        nc = build_program()
        nc.finalize()
        _NC_CACHE["prog"] = nc
    nc = _NC_CACHE["prog"]
    res = run_bass_kernel_spmd(nc, in_maps, core_ids=list(range(NB)))
    out = np.stack([np.asarray(res.results[r]["out"], np.float32)
                    for r in range(NB)], axis=0)
    return out
